# version 05
# baseline (speedup 1.0000x reference)
import math
import numpy as np
import ml_dtypes
import concourse.bass as bass
import concourse.mybir as mybir
from concourse.bass_utils import run_bass_kernel_spmd

F32 = mybir.dt.float32
BF16 = mybir.dt.bfloat16
AF = mybir.ActivationFunctionType
OP = mybir.AluOpType
AX = mybir.AxisListType

B_FULL, T, D = 32, 2048, 1024
NCORES = 8
BPC = B_FULL // NCORES
DILS = [1, 2, 4, 8]
LS = [16, 8, 4, 2]
HOFF = [0, 2048, 3072, 3584]
NBLK = 30
NORM = 1.0 / math.sqrt(128.0)
NEG = -1.0e9
EPS = 1e-5

JT_FFN = [
    [(0, 1, 2, 512)],
    [(0, 1025, 2, 512)],
    [(0, 2, 4, 512)],
    [(0, 4, 8, 256), (256, 0, 8, 256)],
]
JT_OP = [
    [(0, 0, 1, 2, 512)],
    [(0, 0, 1025, 2, 512)],
    [(0, 0, 2, 4, 512), (1, 0, 1, 2, 512)],
    [(0, 0, 4, 8, 256), (0, 256, 0, 8, 256),
     (1, 0, 2, 4, 256), (1, 256, 0, 4, 256),
     (2, 0, 1, 2, 256), (2, 256, 0, 2, 256),
     (3, 256, 0, 1, 256)],
]

UNITS = [
    [(0, 0, 4), (0, 4, 4), (0, 8, 4), (0, 12, 4)],
    [(1, 0, 4), (1, 4, 4), (2, 0, 4), (3, 0, 2)],
]
UOFF = [0, 2048]


DEBUG_TAPS = []
DEBUG_BI0_ONLY = False


def _gperm():
    t = np.arange(T)
    return np.concatenate([t[t % 2 == 1], t[t % 4 == 2],
                           t[t % 8 == 4], t[t % 8 == 0]])


class _Op:
    __slots__ = ("eng", "waits", "fn", "needed", "val", "is_dma", "sem",
                 "lane")

    def __init__(self, eng, waits, fn, is_dma, lane):
        self.eng, self.waits, self.fn = eng, waits, fn
        self.needed, self.val, self.is_dma = False, None, is_dma
        self.sem = None
        self.lane = lane


class Builder:
    def __init__(self, nc):
        self.nc = nc
        self.ops = []

    def emit(self, eng, fn, waits=(), is_dma=False, lane=None):
        ws = tuple(w for w in waits if w is not None)
        for w in ws:
            assert isinstance(w, _Op)
        op = _Op(eng, ws, fn, is_dma, lane)
        self.ops.append(op)
        return op

    def dma(self, eng, out, in_, waits=(), lane="g"):
        return self.emit(eng, lambda e: e.dma_start(out=out, in_=in_),
                         waits, is_dma=True, lane=lane)


def _sl(ap, st0, stp, cnt):
    return ap[:, st0:st0 + stp * (cnt - 1) + 1:stp]


def build(dump=()):
    nc = bass.Bass("TRN2", target_bir_lowering=False, debug=False)
    b = Builder(nc)

    def din(name, shape, dt):
        return nc.dram_tensor(name, list(shape), dt, kind="ExternalInput").ap()

    x_d = din("x", (BPC, D, T), F32)
    xg_d = din("xg", (BPC, D, T), F32)
    wq_d = din("wq_big", (128, 4096), F32)
    wk_d = din("wk_big", (128, 4096), F32)
    wv_d = din("wv_big", (128, 4096), F32)
    wo_d = din("wo_big", (128, 4096), F32)
    bo_d = din("bo_t", (128, 8), F32)
    fb_d = din("fb_t", (128, 8), F32)
    wrt_d = din("wrt", (D, D), F32)
    wit_d = din("wit", (D, D), F32)
    c_d = din("cdft", (D, D), BF16)
    s_d = din("sdft", (D, D), BF16)
    sn_d = din("sndft", (D, D), BF16)
    mbt_d = din("mbt", (4, 128, 128), BF16)
    id_d = din("identb", (128, 128), BF16)
    out_d = nc.dram_tensor("out", [BPC, D, T], F32, kind="ExternalOutput").ap()

    def dbg_out(name, shape, dt=F32):
        return nc.dram_tensor("dbg_" + name, list(shape), dt,
                              kind="ExternalOutput").ap()

    import contextlib
    stack = contextlib.ExitStack()
    sb = lambda name, shape, dt: stack.enter_context(
        nc.sbuf_tensor(name, list(shape), dt))
    psum = lambda name: stack.enter_context(
        nc.psum_tensor(name, [128, 512], F32))

    with stack:
        xb = [sb(f"xb{c}", [128, T], BF16) for c in range(8)]
        xn = [sb(f"xn{c}", [128, T], BF16) for c in range(8)]
        lnstat = sb("lnstat", [128, 4096], BF16)
        mu_bf = lnstat[:, 0:2048]
        invs_bf = lnstat[:, 2048:4096]
        e_all = lnstat[:, 0:3840]
        xfp = [sb(f"xfp{i}", [128, 1024], F32) for i in range(2)]
        xsq_t = [sb(f"xsq{i}", [128, 512], BF16) for i in range(2)]
        st_a = sb("st_a", [128, 512], F32)
        st_b = sb("st_b", [128, 512], F32)
        st_c = sb("st_c", [128, 512], F32)
        ones = sb("ones", [128, 128], BF16)
        czc = sb("czc", [128, 2], F32)
        wq_all = sb("wq_all", [128, 4096], BF16)
        wk_all = sb("wk_all", [128, 4096], BF16)
        wv_all = sb("wv_all", [128, 4096], BF16)
        wo_all = sb("wo_all", [128, 4096], BF16)
        fb_sb = sb("fb_sb", [128, 8], F32)
        bo_sb = sb("bo_sb", [128, 8], F32)
        mbt_sb = [sb(f"mbt{h}", [128, 128], BF16) for h in range(4)]
        identb = sb("identb_sb", [128, 128], BF16)
        m_sb = [sb(f"m{c}", [128, 1024], BF16) for c in range(8)]
        q_all = sb("q_all", [128, 3840], BF16)
        k_all = sb("k_all", [128, 3840], BF16)
        v_pm = sb("v_pm", [128, 2048], BF16)
        v_tm = sb("v_tm", [128, 3840], BF16)
        at_all = sb("at_all", [128, 3840], BF16)
        rs_sb = sb("rs_sb", [128, 32], F32)
        rsi_sb = sb("rsi_sb", [128, 32], F32)
        y_sb = sb("y_sb", [128, 4096], BF16)
        out_sb = [sb(f"os{i}", [128, 512], F32) for i in range(4)]
        xg_sb = [sb(f"xgs{i}", [128, 512], F32) for i in range(4)]
        ps = [psum(f"ps{i}") for i in range(8)]

        chunk = lambda t, j: t[:, j * 1024:(j + 1) * 1024]
        c_sb = [chunk(xn[e // 2], e % 2) for e in range(8)]
        sneg_sb = [chunk(xb[e // 2], e % 2) for e in range(8)]
        s_sb = [chunk(xn[4 + e // 2], e % 2) for e in range(8)]
        wrT_b = [chunk(xb[4 + e // 2], e % 2) for e in range(8)]
        wiT_b = [chunk(q_all, 0), chunk(q_all, 1),
                 chunk(k_all, 0), chunk(k_all, 1),
                 chunk(v_tm, 0), chunk(v_tm, 1),
                 chunk(at_all, 0), chunk(at_all, 1)]
        br_ch = [chunk(wq_all, j) for j in range(4)] + \
                [chunk(wk_all, j) for j in range(4)]
        bi_ch = [chunk(wv_all, j) for j in range(4)] + \
                [chunk(wo_all, j) for j in range(4)]

        bank_w = [None] * 8
        bank_r = [None] * 8

        def mm_group(bank, mms, waits=()):
            ws0 = [w for w in waits if w is not None]
            if bank_r[bank] is not None:
                ws0.append(bank_r[bank])
            ev = None
            n = len(mms)
            for i, mm in enumerate(mms):
                w = list(ws0) if i == 0 else []
                if mm[0] == "T":
                    _, src, idt = mm
                    ev = b.emit("tensor",
                                lambda e, src=src, idt=idt, bank=bank:
                                e.transpose(
                                    ps[bank][:, :].bitcast(BF16)[:, 0:128],
                                    src, idt),
                                waits=tuple(w))
                    continue
                lh, rh, o = mm[0], mm[1], mm[2]
                if len(mm) > 3 and mm[3] is not None:
                    w.append(mm[3])
                ev = b.emit("tensor",
                            lambda e, lh=lh, rh=rh, o=o, s0=(i == 0),
                            s1=(i == n - 1):
                            e.matmul(o, lh, rh, start=s0, stop=s1),
                            waits=tuple(w))
            bank_w[bank] = ev
            return ev

        def reader(eng, fn, bank, waits=()):
            ev = b.emit(eng, fn, waits=tuple(waits) + (bank_w[bank],))
            bank_r[bank] = ev
            return ev

        ones_ev = b.emit("vector", lambda e: e.memset(ones[:, :], 1.0))
        b.emit("vector", lambda e: e.memset(czc[:, 0:1], 0.0))
        cz_ev = b.emit("vector", lambda e: e.memset(czc[:, 1:2], EPS))
        nc.const_aps.register(F32, 0.0, czc[:, 0:1])
        nc.const_aps.register(F32, EPS, czc[:, 1:2])

        gl_last = [None] * 4
        c_ld, s_ld, sn_ld = [None] * 8, [None] * 8, [None] * 8
        for e in range(8):
            ln = e % 4
            c_ld[e] = b.dma("gpsimd", c_sb[e], c_d[e * 128:(e + 1) * 128, :],
                            waits=(gl_last[ln],), lane=f"gc{ln}")
            gl_last[ln] = c_ld[e]
        for e in range(8):
            ln = e % 4
            sn_ld[e] = b.dma("gpsimd", sneg_sb[e],
                             sn_d[e * 128:(e + 1) * 128, :],
                             waits=(gl_last[ln],), lane=f"gc{ln}")
            gl_last[ln] = sn_ld[e]
        for e in range(8):
            ln = e % 4
            s_ld[e] = b.dma("gpsimd", s_sb[e], s_d[e * 128:(e + 1) * 128, :],
                            waits=(gl_last[ln],), lane=f"gc{ln}")
            gl_last[ln] = s_ld[e]
        mbt_ld = [None] * 4
        for h in range(4):
            mbt_ld[h] = b.dma("gpsimd", mbt_sb[h][:, :], mbt_d[h],
                              waits=(gl_last[h % 4],), lane=f"gc{h % 4}")
            gl_last[h % 4] = mbt_ld[h]
        small_ready = b.dma("gpsimd", identb[:, :], id_d[:, :],
                            waits=(gl_last[0],), lane="gc0")
        gl_last[0] = small_ready
        b.dma("gpsimd", fb_sb[:, :], fb_d[:, :], waits=(gl_last[1],),
              lane="gc1")
        gl_last[1] = b.ops[-1]
        b.dma("gpsimd", bo_sb[:, :], bo_d[:, :], waits=(gl_last[2],),
              lane="gc2")
        gl_last[2] = b.ops[-1]
        sm_fb_bo = (small_ready, gl_last[1], gl_last[2])

        xfp_last = [None, None]
        wrT_cast, wiT_cast = [None] * 8, [None] * 8
        for e in range(8):
            i = e % 2
            ld = b.dma("sync", xfp[i][:, :], wrt_d[e * 128:(e + 1) * 128, :],
                       waits=(xfp_last[i],), lane=f"xf{i}")
            wrT_cast[e] = b.emit("vector",
                                 lambda en, d=wrT_b[e], s_=xfp[i]:
                                 en.tensor_copy(d, s_[:, :]), waits=(ld,))
            xfp_last[i] = wrT_cast[e]
        for e in range(8):
            i = e % 2
            ld = b.dma("sync", xfp[i][:, :], wit_d[e * 128:(e + 1) * 128, :],
                       waits=(xfp_last[i],), lane=f"xf{i}")
            wiT_cast[e] = b.emit("scalar",
                                 lambda en, d=wiT_b[e], s_=xfp[i]:
                                 en.activation(d, s_[:, :], AF.Copy,
                                               bias=0.0, scale=1.0),
                                 waits=(ld, cz_ev))
            xfp_last[i] = wiT_cast[e]

        br_ev = [None] * 8
        bi_ev = [None] * 8
        ev_par = 0

        def m_wave(kind, half):
            nonlocal ev_par
            groups = [(rc, hf) for rc in range(half * 4, half * 4 + 4)
                      for hf in range(2)]
            for ec in range(8):
                for gi, (rc, hf) in enumerate(groups):
                    if kind == "A":
                        lh, rh = wrT_b[ec][:, rc * 128:(rc + 1) * 128], \
                            c_sb[ec][:, hf * 512:(hf + 1) * 512]
                        w = (wrT_cast[ec], c_ld[ec])
                    elif kind == "A2":
                        lh, rh = wiT_b[ec][:, rc * 128:(rc + 1) * 128], \
                            sneg_sb[ec][:, hf * 512:(hf + 1) * 512]
                        w = (wiT_cast[ec], sn_ld[ec])
                    elif kind == "B":
                        lh, rh = wiT_b[ec][:, rc * 128:(rc + 1) * 128], \
                            c_sb[ec][:, hf * 512:(hf + 1) * 512]
                        w = (wiT_cast[ec], c_ld[ec])
                    elif kind == "B2":
                        lh, rh = wrT_b[ec][:, rc * 128:(rc + 1) * 128], \
                            s_sb[ec][:, hf * 512:(hf + 1) * 512]
                        w = (wrT_cast[ec], s_ld[ec])
                    elif kind == "C":
                        lh, rh = c_sb[ec][:, rc * 128:(rc + 1) * 128], \
                            br_ch[ec][:, hf * 512:(hf + 1) * 512]
                        w = (c_ld[ec], br_ev[ec])
                    else:
                        lh, rh = s_sb[ec][:, rc * 128:(rc + 1) * 128], \
                            bi_ch[ec][:, hf * 512:(hf + 1) * 512]
                        w = (s_ld[ec], bi_ev[ec])
                    bank = gi
                    first = (ec == 0 and kind in ("A", "B", "C"))
                    ws = list(w) if gi == 0 else []
                    if first and bank_r[bank] is not None:
                        ws.append(bank_r[bank])
                    ev = b.emit("tensor",
                                lambda e, lh=lh, rh=rh, bank=bank,
                                s0=first,
                                s1=(ec == 7 and kind in ("A2", "B2", "C2")):
                                e.matmul(ps[bank][:, :], lh, rh,
                                         start=s0, stop=s1),
                                waits=tuple(ws))
                    bank_w[bank] = ev
            if kind in ("A", "B", "C"):
                return
            for gi, (rc, hf) in enumerate(groups):
                if kind == "A2":
                    dst = br_ch[rc][:, hf * 512:(hf + 1) * 512]
                elif kind == "B2":
                    dst = bi_ch[rc][:, hf * 512:(hf + 1) * 512]
                else:
                    dst = m_sb[rc][:, hf * 512:(hf + 1) * 512]
                eng = "vector" if ev_par % 2 == 0 else "scalar"
                ev_par += 1
                if eng == "vector":
                    ev = reader(eng, lambda e, d=dst, bank=gi:
                                e.tensor_copy(d, ps[bank][:, :]), gi)
                else:
                    ev = reader(eng, lambda e, d=dst, bank=gi:
                                e.activation(d, ps[bank][:, :], AF.Copy,
                                             bias=0.0, scale=1.0), gi)
                if kind == "A2":
                    br_ev[rc] = ev
                elif kind == "B2":
                    bi_ev[rc] = ev

        for half in range(2):
            m_wave("A", half)
            m_wave("A2", half)
        for half in range(2):
            m_wave("B", half)
            m_wave("B2", half)
        b2_done = next(op for op in reversed(b.ops) if op.eng == "tensor")
        for half in range(2):
            m_wave("C", half)
            m_wave("C2", half)
        m_mm_done = next(op for op in reversed(b.ops) if op.eng == "tensor")
        m_ready_v = next(op for op in reversed(b.ops) if op.eng == "vector")
        m_ready_s = next(op for op in reversed(b.ops) if op.eng == "scalar")

        xb_guard = [[b2_done] for _ in range(8)]
        pend_cast = {}

        def emit_prefetch(nbi, dc):
            evs = []
            for hf in range(2):
                i = (dc * 2 + hf) % 2
                ld = b.dma("sync", xfp[i][:, :],
                           x_d[nbi, dc * 128:(dc + 1) * 128,
                               hf * 1024:(hf + 1) * 1024],
                           waits=(xfp_last[i],), lane=f"xf{i}")
                eng = "vector" if dc % 2 else "scalar"
                w = (ld,) + tuple(xb_guard[dc])
                if eng == "vector":
                    ev = b.emit(eng, lambda e, dc=dc, hf=hf, s_=xfp[i]:
                                e.tensor_copy(
                                    xb[dc][:, hf * 1024:(hf + 1) * 1024],
                                    s_[:, :]), waits=w)
                else:
                    ev = b.emit(eng, lambda e, dc=dc, hf=hf, s_=xfp[i]:
                                e.activation(
                                    xb[dc][:, hf * 1024:(hf + 1) * 1024],
                                    s_[:, :], AF.Copy, bias=0.0,
                                    scale=1.0), waits=w + (cz_ev,))
                xfp_last[i] = ev
                evs.append(ev)
            pend_cast[dc] = evs[-1]

        for dc in range(8):
            emit_prefetch(0, dc)

        w_cast = {}
        for nmi, (src, dst, scale) in enumerate(
                ((wq_d, wq_all, NORM), (wk_d, wk_all, 1.0),
                 (wv_d, wv_all, 1.0), (wo_d, wo_all, 1.0))):
            for j in range(4):
                i = (nmi * 4 + j) % 2
                ld = b.dma("sync", xfp[i][:, :], src[:, j * 1024:(j + 1) * 1024],
                           waits=(xfp_last[i],), lane=f"xf{i}")
                if scale == 1.0:
                    cast = b.emit("vector",
                                  lambda e, d=chunk(dst, j), s_=xfp[i]:
                                  e.tensor_copy(d, s_[:, :]),
                                  waits=(ld, m_mm_done))
                else:
                    cast = b.emit("scalar",
                                  lambda e, d=chunk(dst, j), s_=xfp[i],
                                  sc=scale:
                                  e.activation(d, s_[:, :], AF.Copy,
                                               bias=0.0, scale=sc),
                                  waits=(ld, m_mm_done))
                xfp_last[i] = cast
                w_cast[(nmi, j)] = cast
        wq_ready = lambda h: w_cast[(0, h)]
        wk_ready = lambda h: w_cast[(1, h)]
        wv_ready = lambda h: w_cast[(2, h)]
        wo_ready = w_cast[(3, 3)]

        xn_guard = [[m_mm_done] for _ in range(8)]
        lnstat_guard = [m_mm_done]
        xg_guard = [[None] for _ in range(4)]
        out_guard = [None] * 4
        ysl_guard = [None] * 8
        sq_mm = [None, None]
        last_stores = [None]
        apply_evs = [None] * 8

        def emit_stats(bi):
            cast_ev = [None] * 8
            for dc in range(8):
                cast_ev[dc] = pend_cast.pop(dc)
            sx_last = None
            invs_done = None
            prev_rsq = None
            for tt in range(4):
                sl = slice(tt * 512, (tt + 1) * 512)
                mms = [(ones[:, :], xb[dc][:, sl], ps[6][:, :],
                        cast_ev[dc] if tt == 0 else None) for dc in range(8)]
                sx_last = mm_group(6, mms, waits=(ones_ev,))
                ev = None
                for dc in range(8):
                    j = dc % 2
                    sq = b.emit("vector",
                                lambda e, j=j, dc=dc, sl=sl:
                                e.tensor_mul(xsq_t[j][:, :], xb[dc][:, sl],
                                             xb[dc][:, sl]),
                                waits=(cast_ev[dc], sq_mm[j]))
                    w = [bank_r[7], ones_ev] if dc == 0 else []
                    w.append(sq)
                    ev = b.emit("tensor",
                                lambda e, j=j, s0=(dc == 0), s1=(dc == 7):
                                e.matmul(ps[7][:, :], ones[:, :],
                                         xsq_t[j][:, :], start=s0, stop=s1),
                                waits=tuple(x for x in w if x is not None))
                    sq_mm[j] = ev
                bank_w[7] = ev
                mu_ev = reader("vector", lambda e: e.tensor_scalar_mul(
                    st_a[:, :], ps[6][:, :], 1.0 / D), 6,
                    waits=(lnstat_guard[0] if tt == 0 else None,))
                b.emit("vector", lambda e, sl=sl: e.tensor_copy(
                    mu_bf[:, sl], st_a[:, :]))
                musq = b.emit("vector", lambda e: e.tensor_mul(
                    st_b[:, :], st_a[:, :], st_a[:, :]))
                var = reader("vector", lambda e: e.scalar_tensor_tensor(
                    st_c[:, :], ps[7][:, :], 1.0 / D, st_b[:, :],
                    OP.mult, OP.subtract), 7, waits=(musq,))
                sstd = b.emit("scalar", lambda e: e.activation(
                    st_b[:, :], st_c[:, :], AF.Sqrt, bias=czc[:, 1:2],
                    scale=1.0), waits=(var, prev_rsq))
                prev_rsq = sstd
                b.emit("vector", lambda e: e.reciprocal(
                    st_a[:, :], st_b[:, :]), waits=(sstd,))
                invs_done = b.emit("vector", lambda e, sl=sl: e.tensor_copy(
                    invs_bf[:, sl], st_a[:, :]))
            return invs_done, sx_last

        def emit_apply(stats_done, sx_last):
            for dc in range(8):
                b.emit("vector", lambda e, dc=dc: e.tensor_sub(
                    xn[dc][:, :], xb[dc][:, :], mu_bf[:, :]),
                    waits=(stats_done,) + tuple(xn_guard[dc]))
                apply_evs[dc] = b.emit("vector", lambda e, dc=dc:
                                       e.tensor_mul(xn[dc][:, :], xn[dc][:, :],
                                                    invs_bf[:, :]))
            for dc in range(8):
                xb_guard[dc] = [apply_evs[dc], sx_last]
            lnstat_guard[0] = apply_evs[7]
            return apply_evs[7]

        pcount = [0]

        def emit_proj(h, xn_ready, vpm_guard):
            L, dil = LS[h], DILS[h]
            qk_all = None
            v_all = None
            ngrp = (L + 3) // 4
            for (wt, wrdy, dst, eng) in (
                    (wq_all, wq_ready(h), q_all, "scalar"),
                    (wk_all, wk_ready(h), k_all, "scalar"),
                    (wv_all, wv_ready(h), v_pm, "vector")):
                for g in range(ngrp):
                    nb = min(4, L - g * 4)
                    bank = 4 + pcount[0] % 2
                    pcount[0] += 1
                    mms = []
                    for dc in range(8):
                        rhs = (xn[dc][:, :]
                               .rearrange("p (n bb) -> p n bb", bb=dil * 128)
                               [:, g * 4:g * 4 + nb, 0:128])
                        lh = wt[:, (h * 8 + dc) * 128:(h * 8 + dc + 1) * 128]
                        mms.append((lh, rhs, ps[bank][:, 0:nb * 128]))
                    mm_group(bank, mms, waits=(xn_ready, wrdy))
                    src = (ps[bank][:, 0:nb * 128]
                           .rearrange("p (n s) -> p n s", s=128))
                    base = dst[:, 0:128 * L] if dst is v_pm else \
                        dst[:, HOFF[h]:HOFF[h] + 128 * L]
                    dstv = (base.rearrange("p (s n) -> p s n", n=L)
                            [:, :, g * 4:g * 4 + nb].transpose([0, 2, 1]))
                    evw = (vpm_guard,) if dst is v_pm else ()
                    ev = reader(eng,
                                (lambda e, dstv=dstv, src=src:
                                 e.copy(dstv, src)) if eng == "scalar"
                                else (lambda e, dstv=dstv, src=src:
                                      e.tensor_copy(dstv, src)), bank,
                                waits=evw)
                    if dst is v_pm:
                        v_all = ev
                    else:
                        qk_all = ev
            vt_last = None
            tr_last = None
            for sbk in range(L):
                cols = slice(sbk * 128, (sbk + 1) * 128)
                bank = 4 + pcount[0] % 2
                pcount[0] += 1
                tr_last = mm_group(bank, [("T", v_pm[:, cols], identb[:, :])],
                                   waits=(v_all, small_ready))
                vt_last = reader(
                    "vector",
                    lambda e, sbk=sbk, h=h, bank=bank:
                    e.tensor_copy(v_tm[:, HOFF[h] + sbk * 128:
                                       HOFF[h] + (sbk + 1) * 128],
                                  ps[bank][:, :].bitcast(BF16)[:, 0:128]),
                    bank)
            return qk_all, vt_last, tr_last

        def emit_chain_scores(u, qk_evs):
            sc_done = [None] * 4
            for bk, pieces in enumerate(_unit_banks(u)):
                bank = 4 + bk
                for (h, sbk, col) in pieces:
                    qc = slice(HOFF[h] + sbk * 128, HOFF[h] + (sbk + 1) * 128)
                    mms = [(mbt_sb[h][:, :], identb[:, :],
                            ps[bank][:, col:col + 128]),
                           (k_all[:, qc], q_all[:, qc],
                            ps[bank][:, col:col + 128])]
                    sc_done[bk] = mm_group(
                        bank, mms,
                        waits=(qk_evs[h], small_ready, mbt_ld[h]))
            return sc_done

        def _unit_banks(u):
            out = [[] for _ in range(4)]
            bb = 0
            for (h, s0, nb) in UNITS[u]:
                for j in range(nb):
                    out[bb // 4].append((h, s0 + j, (bb % 4) * 128))
                    bb += 1
            return out

        def emit_chain_mid(u, sc_done, apply_last):
            banks = _unit_banks(u)
            ncols = [len(p) * 128 for p in banks]
            exp_evs = []
            for bk in range(4):
                bank = 4 + bk
                off = UOFF[u] + bk * 512
                ev = reader("scalar",
                            lambda e, bank=bank, off=off, nc_=ncols[bk]:
                            e.activation(e_all[:, off:off + nc_],
                                         ps[bank][:, 0:nc_], AF.Exp,
                                         bias=0.0, scale=1.0), bank,
                            waits=(apply_last,))
                exp_evs.append(ev)
            nblk = sum(len(p) for p in banks)
            r0 = 16 * u
            red = b.emit("vector",
                         lambda e, u=u, nblk=nblk, r0=r0:
                         e.reduce_sum(
                             rs_sb[:, r0:r0 + nblk]
                             .rearrange("p (n o) -> p n o", o=1),
                             e_all[:, UOFF[u]:UOFF[u] + nblk * 128]
                             .rearrange("p (n s) -> p n s", s=128),
                             AX.X), waits=tuple(exp_evs))
            rec = None
            for c0 in range(0, nblk, 8):
                cw = min(8, nblk - c0)
                rec = b.emit("vector",
                             lambda e, a=r0 + c0, w=cw:
                             e.reciprocal(rsi_sb[:, a:a + w],
                                          rs_sb[:, a:a + w]),
                             waits=(red,) if c0 == 0 else ())
            scale_evs = [[None] * len(p) for p in banks]
            bb = 0
            for bk in range(4):
                for j in range(len(banks[bk])):
                    off = UOFF[u] + bb * 128
                    scale_evs[bk][j] = b.emit(
                        "scalar",
                        lambda e, off=off, r=r0 + bb:
                        e.activation(e_all[:, off:off + 128],
                                     e_all[:, off:off + 128], AF.Copy,
                                     bias=0.0, scale=rsi_sb[:, r:r + 1]),
                        waits=(rec,))
                    bb += 1
            return scale_evs

        def emit_chain_av(u, scale_evs, vt_done):
            att_done = None
            banks = _unit_banks(u)
            for bk in range(4):
                bank = 4 + bk
                mms = []
                for j, (h, sbk, col) in enumerate(banks[bk]):
                    vc = slice(HOFF[h] + sbk * 128, HOFF[h] + (sbk + 1) * 128)
                    eoff = UOFF[u] + bk * 512 + col
                    mms.append((v_tm[:, vc], e_all[:, eoff:eoff + 128],
                                ps[bank][:, col:col + 128],
                                scale_evs[bk][j]))
                mm_group(bank, mms, waits=(vt_done,))
                nc_ = len(banks[bk]) * 128
                off = UOFF[u] + bk * 512
                att_done = reader("scalar",
                                  lambda e, off=off, nc_=nc_, bank=bank:
                                  e.copy(at_all[:, off:off + nc_],
                                         ps[bank][:, 0:nc_]), bank)
            return att_done

        def emit_y(bi, ui, xn_ready):
            jt, oc = ui % 4, ui // 4
            bank = ui % 2
            slot = ui % 8
            mms = []
            for dc in range(8):
                for (pc, st0, stp, cnt) in JT_FFN[jt]:
                    mms.append((m_sb[dc][:, oc * 128:(oc + 1) * 128],
                                _sl(xn[dc][:, :], st0, stp, cnt),
                                ps[bank][:, pc:pc + cnt]))
            gw = ()
            if ui == 0:
                gw = (xn_ready,) + ((m_ready_v, m_ready_s) if bi == 0 else ())
            ylast = mm_group(bank, mms, waits=gw)
            yev = reader("vector",
                         lambda e, slot=slot, oc=oc, bank=bank:
                         e.tensor_scalar_add(
                             y_sb[:, slot * 512:(slot + 1) * 512],
                             ps[bank][:, :], fb_sb[:, oc:oc + 1]), bank,
                         waits=(sm_fb_bo[1] if (bi == 0 and ui == 0) else None,
                                ysl_guard[slot]))
            return ylast, yev

        def emit_silu(gi, yev_last):
            half = gi % 2
            ev = b.emit("scalar",
                        lambda e, half=half:
                        e.activation(y_sb[:, half * 2048:(half + 1) * 2048],
                                     y_sb[:, half * 2048:(half + 1) * 2048],
                                     AF.Silu, bias=0.0, scale=1.0),
                        waits=(yev_last,))
            return ev

        def emit_s(bi, si, silu_ev, att_evs):
            jt, oc = si % 4, si // 4
            bank = 2 + si % 2
            slot = si % 8
            ob = si % 4
            xgb = si % 4
            xg_ld = b.dma("gpsimd", xg_sb[xgb][:, :],
                          xg_d[bi, oc * 128:(oc + 1) * 128,
                               jt * 512:(jt + 1) * 512],
                          waits=tuple(xg_guard[xgb]), lane=f"xg{xgb}")
            mms = []
            opw = [att_evs[0] if jt < 2 else att_evs[3]]
            if bi == 0 and si == 0:
                opw += [wo_ready, sm_fb_bo[2]]
            for (h, pc, st0, stp, cnt) in JT_OP[jt]:
                mms.append((wo_all[:, h * 1024 + oc * 128:
                                   h * 1024 + (oc + 1) * 128],
                            _sl(at_all[:, HOFF[h]:HOFF[h] + 128 * LS[h]],
                                st0, stp, cnt),
                            ps[bank][:, pc:pc + cnt]))
            mm_group(bank, mms, waits=tuple(opw))
            comb = reader("vector",
                          lambda e, slot=slot, oc=oc, bank=bank, ob=ob:
                          e.scalar_tensor_tensor(
                              out_sb[ob][:, :], ps[bank][:, :],
                              bo_sb[:, oc:oc + 1],
                              y_sb[:, slot * 512:(slot + 1) * 512],
                              OP.add, OP.mult), bank,
                          waits=(silu_ev, out_guard[ob]))
            ysl_guard[slot] = comb
            add = b.emit("vector",
                         lambda e, ob=ob, xgb=xgb:
                         e.tensor_add(out_sb[ob][:, :], out_sb[ob][:, :],
                                      xg_sb[xgb][:, :]),
                         waits=(xg_ld,))
            xg_guard[xgb] = [add]
            stv = b.dma("sync", out_d[bi, oc * 128:(oc + 1) * 128,
                                      jt * 512:(jt + 1) * 512],
                        out_sb[ob][:, :], waits=(add,),
                        lane=f"st{ob}")
            out_guard[ob] = stv
            last_stores[0] = stv
            return stv

        if "msb" in dump:
            mo = dbg_out("msb", (8, 128, 1024), BF16)
            for dc in range(8):
                b.dma("gpsimd", mo[dc], m_sb[dc][:, :],
                      waits=(m_ready_v, m_ready_s), lane="gt")
        for bi in range(1 if DEBUG_BI0_ONLY else BPC):
            stats_done, sx_last = emit_stats(bi)
            xn_ready = emit_apply(stats_done, sx_last)
            if "xn" in dump and bi == 0:
                xo = dbg_out("xn", (8, 128, T), BF16)
                for dc in range(8):
                    b.dma("gpsimd", xo[dc], xn[dc][:, :],
                          waits=(xn_ready,), lane="gt")
            qk_evs = [None] * 4
            vt_evs = [None] * 4
            vpm_guard = small_ready
            for h in range(4):
                qk_evs[h], vt_evs[h], vpm_guard = \
                    emit_proj(h, xn_ready, vpm_guard)
            if "qkv" in dump and bi == 0:
                qo = dbg_out("qk", (2, 128, 3840), BF16)
                b.dma("gpsimd", qo[0], q_all[:, :], waits=(qk_evs[3],),
                      lane="gt")
                b.dma("gpsimd", qo[1], k_all[:, :], waits=(qk_evs[3],),
                      lane="gt")
                vo = dbg_out("vtm", (128, 3840), BF16)
                b.dma("gpsimd", vo, v_tm[:, :], waits=(vt_evs[3],), lane="gt")
            att_evs = [None] * 4
            yi = si = 0
            yevs = {}
            silu_evs = {}
            have_att1 = [False]

            def run_y(n):
                nonlocal yi
                for _ in range(n):
                    if yi >= 32 or yi - si >= 8:
                        return
                    _, yev = emit_y(bi, yi, xn_ready)
                    yevs[yi] = yev
                    yi += 1
                    if yi % 4 == 0:
                        gi = yi // 4 - 1
                        silu_evs[gi] = emit_silu(gi, yevs[4 * gi + 3])

            def run_s(n):
                nonlocal si
                for _ in range(n):
                    if si >= 32 or (si // 4 + 1) * 4 > yi:
                        return
                    if si % 4 >= 2 and not have_att1[0]:
                        return
                    emit_s(bi, si, silu_evs[si // 4], att_evs)
                    si += 1

            sc = emit_chain_scores(0, qk_evs)
            run_y(4)
            scale0 = emit_chain_mid(0, sc, xn_ready)
            run_y(2)
            att_evs[0] = emit_chain_av(0, scale0, vt_evs[0])
            run_s(2)
            run_y(3)
            sc = emit_chain_scores(1, qk_evs)
            run_y(1)
            scale1 = emit_chain_mid(1, sc, xn_ready)
            run_y(1)
            att1 = emit_chain_av(1, scale1, vt_evs[3])
            att_evs[1] = att_evs[2] = att_evs[3] = att1
            have_att1[0] = True
            e_free = att1
            if bi + 1 < BPC:
                for dc in range(8):
                    emit_prefetch(bi + 1, dc)
            if "chain" in dump and bi == 0:
                eo = dbg_out("eall", (128, 3840), BF16)
                tp = b.dma("gpsimd", eo, e_all, waits=(att1,), lane="gt")
                ro = dbg_out("rs", (2, 128, 32))
                b.dma("gpsimd", ro[0], rs_sb[:, :], waits=(att1,), lane="gt")
                b.dma("gpsimd", ro[1], rsi_sb[:, :], waits=(att1,), lane="gt")
                ao = dbg_out("atall", (128, 3840), BF16)
                b.dma("gpsimd", ao, at_all[:, :], waits=(att1,), lane="gt")
                e_free = tp
            while yi < 32 or si < 32:
                prog = yi + si
                run_y(2)
                run_s(2)
                assert yi + si > prog, (yi, si)
            for dc in range(8):
                xn_guard[dc] = [yevs[31]]
            lnstat_guard = [e_free]

        b.emit("gpsimd", lambda e: e.engine_nop(),
               waits=tuple(s for s in last_stores if s is not None))

        for op in b.ops:
            if op.is_dma:
                op.needed = True
            for w in op.waits:
                w.needed = True
        ctr = {}
        for op in b.ops:
            if not op.needed:
                continue
            key = ("dma_" + op.lane) if op.is_dma else op.eng
            op.sem = key
            ctr[key] = ctr.get(key, 0) + (16 if op.is_dma else 1)
            op.val = ctr[key]
        sems = {key: stack.enter_context(nc.semaphore("sem_" + key))
                for key in sorted(ctr)}
        with nc.Block() as block:
            def mk(name):
                def run(eng_obj):
                    for op in b.ops:
                        if op.eng != name:
                            continue
                        for w in op.waits:
                            if (not w.is_dma) and w.eng == name:
                                continue
                            eng_obj.wait_ge(sems[w.sem], w.val)
                        inst = op.fn(eng_obj)
                        if op.needed:
                            inst.then_inc(sems[op.sem],
                                          16 if op.is_dma else 1)
                return run
            block.sync(mk("sync"))
            block.tensor(mk("tensor"))
            block.vector(mk("vector"))
            block.scalar(mk("scalar"))
            block.gpsimd(mk("gpsimd"))
    return nc


def _stage_inputs(L0_attn_x, L0_attn_Wk, L0_attn_Wq, L0_attn_Wv,
                  L0_attn_W_out, L0_attn_b_out, L0_ffn_w_freq_real,
                  L0_ffn_w_freq_imag, L0_ffn_bias):
    perm = _gperm()
    f32 = np.float32
    idx = np.arange(D)
    ang = (2.0 * np.pi / D) * ((idx[:, None] * idx[None, :]) % D)
    Cm = (np.cos(ang) / 32.0).astype(ml_dtypes.bfloat16)
    Sm = (np.sin(ang) / 32.0).astype(ml_dtypes.bfloat16)
    Snm = (-np.sin(ang) / 32.0).astype(ml_dtypes.bfloat16)
    mbt = np.zeros((4, 128, 128), np.float32)
    for h in range(4):
        L = LS[h]
        p = np.arange(128)
        sl_p, n = p // L, p % L
        m = p % L
        allowed = (sl_p[:, None] == sl_p[None, :]) & \
                  (m[None, :] <= n[:, None])
        mb = np.where(allowed, 0.0, NEG).astype(f32)
        mbt[h] = mb.T
    mbt = mbt.astype(ml_dtypes.bfloat16)

    def pack_qkv(W, scalefold=False):
        big = np.empty((128, 4096), f32)
        for h in range(4):
            wt = np.asarray(W[h], f32).T
            for dc in range(8):
                big[:, (h * 8 + dc) * 128:(h * 8 + dc + 1) * 128] = \
                    wt[dc * 128:(dc + 1) * 128, :]
        return np.ascontiguousarray(big)

    wq = pack_qkv(L0_attn_Wq)
    wk = pack_qkv(L0_attn_Wk)
    wv = pack_qkv(L0_attn_Wv)
    wo_t = np.asarray(L0_attn_W_out, f32).T
    wo = np.empty((128, 4096), f32)
    for h in range(4):
        wo[:, h * 1024:(h + 1) * 1024] = wo_t[h * 128:(h + 1) * 128, :]
    wrt = np.ascontiguousarray(np.asarray(L0_ffn_w_freq_real, f32).T)
    wit = np.ascontiguousarray(np.asarray(L0_ffn_w_freq_imag, f32).T)
    fbT = np.ascontiguousarray(np.asarray(L0_ffn_bias, f32).reshape(8, 128).T)
    boT = np.ascontiguousarray(
        np.asarray(L0_attn_b_out, f32).reshape(8, 128).T)
    shared = dict(wq_big=wq, wk_big=wk, wv_big=wv, wo_big=np.ascontiguousarray(wo),
                  bo_t=boT, fb_t=fbT, wrt=wrt, wit=wit,
                  cdft=Cm, sdft=Sm, sndft=Snm, mbt=mbt,
                  identb=np.eye(128, dtype=ml_dtypes.bfloat16))
    xs = np.asarray(L0_attn_x, f32)
    in_maps = []
    for c in range(NCORES):
        xc = xs[BPC * c:BPC * (c + 1)]
        m = dict(shared)
        m["x"] = np.ascontiguousarray(xc.transpose(0, 2, 1))
        m["xg"] = np.ascontiguousarray(xc[:, perm, :].transpose(0, 2, 1))
        in_maps.append(m)
    return in_maps, perm


def kernel(L0_attn_x, L0_attn_Wk, L0_attn_Wq, L0_attn_Wv, L0_attn_W_out,
           L0_attn_b_out, L0_ffn_w_freq_real, L0_ffn_w_freq_imag,
           L0_ffn_bias):
    in_maps, perm = _stage_inputs(
        L0_attn_x, L0_attn_Wk, L0_attn_Wq, L0_attn_Wv, L0_attn_W_out,
        L0_attn_b_out, L0_ffn_w_freq_real, L0_ffn_w_freq_imag, L0_ffn_bias)
    nc = build(dump=tuple(DEBUG_TAPS))
    res = run_bass_kernel_spmd(nc, in_maps, core_ids=list(range(NCORES)))
    out = np.empty((B_FULL, T, D), np.float32)
    for c in range(NCORES):
        o = np.asarray(res.results[c]["out"])
        out[BPC * c:BPC * (c + 1)][:, perm, :] = o.transpose(0, 2, 1)
    if DEBUG_TAPS:
        kernel.debug = [{k: np.asarray(v) for k, v in r.items()
                         if k.startswith("dbg_")} for r in res.results]
    return out


# version 06
# speedup vs baseline: 1.0743x; 1.0743x over previous
import math
import numpy as np
import ml_dtypes
import concourse.bass as bass
import concourse.mybir as mybir
from concourse.bass_utils import run_bass_kernel_spmd

F32 = mybir.dt.float32
BF16 = mybir.dt.bfloat16
AF = mybir.ActivationFunctionType
OP = mybir.AluOpType
AX = mybir.AxisListType

B_FULL, T, D = 32, 2048, 1024
NCORES = 8
BPC = B_FULL // NCORES
DILS = [1, 2, 4, 8]
LS = [16, 8, 4, 2]
HOFF = [0, 2048, 3072, 3584]
NBLK = 30
NORM = 1.0 / math.sqrt(128.0)
NEG = -1.0e9
EPS = 1e-5

JT_FFN = [
    [(0, 1, 2, 512)],
    [(0, 1025, 2, 512)],
    [(0, 2, 4, 512)],
    [(0, 4, 8, 256), (256, 0, 8, 256)],
]
JT_OP = [
    [(0, 0, 1, 2, 512)],
    [(0, 0, 1025, 2, 512)],
    [(0, 0, 2, 4, 512), (1, 0, 1, 2, 512)],
    [(0, 0, 4, 8, 256), (0, 256, 0, 8, 256),
     (1, 0, 2, 4, 256), (1, 256, 0, 4, 256),
     (2, 0, 1, 2, 256), (2, 256, 0, 2, 256),
     (3, 256, 0, 1, 256)],
]

UNITS = [
    [(0, 0, 4), (0, 4, 4), (0, 8, 4), (0, 12, 4)],
    [(1, 0, 4), (1, 4, 4), (2, 0, 4), (3, 0, 2)],
]
UOFF = [0, 2048]


DEBUG_TAPS = []
DEBUG_BI0_ONLY = False


def _gperm():
    t = np.arange(T)
    return np.concatenate([t[t % 2 == 1], t[t % 4 == 2],
                           t[t % 8 == 4], t[t % 8 == 0]])


class _Op:
    __slots__ = ("eng", "waits", "fn", "needed", "val", "is_dma", "sem",
                 "lane")

    def __init__(self, eng, waits, fn, is_dma, lane):
        self.eng, self.waits, self.fn = eng, waits, fn
        self.needed, self.val, self.is_dma = False, None, is_dma
        self.sem = None
        self.lane = lane


class Builder:
    def __init__(self, nc):
        self.nc = nc
        self.ops = []

    def emit(self, eng, fn, waits=(), is_dma=False, lane=None):
        ws = tuple(w for w in waits if w is not None)
        for w in ws:
            assert isinstance(w, _Op)
        op = _Op(eng, ws, fn, is_dma, lane)
        self.ops.append(op)
        return op

    def dma(self, eng, out, in_, waits=(), lane="g"):
        return self.emit(eng, lambda e: e.dma_start(out=out, in_=in_),
                         waits, is_dma=True, lane=lane)


def _sl(ap, st0, stp, cnt):
    return ap[:, st0:st0 + stp * (cnt - 1) + 1:stp]


def build(dump=()):
    nc = bass.Bass("TRN2", target_bir_lowering=False, debug=False)
    b = Builder(nc)

    def din(name, shape, dt):
        return nc.dram_tensor(name, list(shape), dt, kind="ExternalInput").ap()

    x_d = din("x", (BPC, D, T), F32)
    xg_d = din("xg", (BPC, D, T), F32)
    wq_d = din("wq_big", (128, 4096), F32)
    wk_d = din("wk_big", (128, 4096), F32)
    wv_d = din("wv_big", (128, 4096), F32)
    wo_d = din("wo_big", (128, 4096), F32)
    bo_d = din("bo_t", (128, 8), F32)
    fb_d = din("fb_t", (128, 8), F32)
    wrt_d = din("wrt", (D, D), F32)
    wit_d = din("wit", (D, D), F32)
    c_d = din("cdft", (D, D), BF16)
    s_d = din("sdft", (D, D), BF16)
    sn_d = din("sndft", (D, D), BF16)
    mbt_d = din("mbt", (4, 128, 128), BF16)
    id_d = din("identb", (128, 128), BF16)
    out_d = nc.dram_tensor("out", [BPC, D, T], F32, kind="ExternalOutput").ap()

    def dbg_out(name, shape, dt=F32):
        return nc.dram_tensor("dbg_" + name, list(shape), dt,
                              kind="ExternalOutput").ap()

    import contextlib
    stack = contextlib.ExitStack()
    sb = lambda name, shape, dt: stack.enter_context(
        nc.sbuf_tensor(name, list(shape), dt))
    psum = lambda name: stack.enter_context(
        nc.psum_tensor(name, [128, 512], F32))

    with stack:
        xb = [sb(f"xb{c}", [128, T], BF16) for c in range(8)]
        xn = [sb(f"xn{c}", [128, T], BF16) for c in range(8)]
        lnstat = sb("lnstat", [128, 4096], BF16)
        mu_bf = lnstat[:, 0:2048]
        invs_bf = lnstat[:, 2048:4096]
        e_all = lnstat[:, 0:3840]
        xfp = [sb(f"xfp{i}", [128, 1024], F32) for i in range(4)]
        xsq_t = [sb(f"xsq{i}", [128, 512], BF16) for i in range(2)]
        st_a = sb("st_a", [128, 512], F32)
        st_b = sb("st_b", [128, 512], F32)
        st_c = sb("st_c", [128, 512], F32)
        ones = sb("ones", [128, 128], BF16)
        czc = sb("czc", [128, 2], F32)
        wq_all = sb("wq_all", [128, 4096], BF16)
        wk_all = sb("wk_all", [128, 4096], BF16)
        wv_all = sb("wv_all", [128, 4096], BF16)
        wo_all = sb("wo_all", [128, 4096], BF16)
        fb_sb = sb("fb_sb", [128, 8], F32)
        bo_sb = sb("bo_sb", [128, 8], F32)
        mbt_sb = [sb(f"mbt{h}", [128, 128], BF16) for h in range(4)]
        identb = sb("identb_sb", [128, 128], BF16)
        m_sb = [sb(f"m{c}", [128, 1024], BF16) for c in range(8)]
        q_all = sb("q_all", [128, 3840], BF16)
        k_all = sb("k_all", [128, 3840], BF16)
        v_tm = sb("v_tm", [128, 3840], BF16)
        at_all = sb("at_all", [128, 3840], BF16)
        v_pm = at_all[:, 0:2048]
        rs_sb = sb("rs_sb", [128, 32], F32)
        rsi_sb = sb("rsi_sb", [128, 32], F32)
        y_sb = sb("y_sb", [128, 4096], BF16)
        out_sb = [sb(f"os{i}", [128, 512], F32) for i in range(4)]
        xg_sb = [sb(f"xgs{i}", [128, 512], F32) for i in range(4)]
        ps = [psum(f"ps{i}") for i in range(8)]

        chunk = lambda t, j: t[:, j * 1024:(j + 1) * 1024]
        c_sb = [chunk(xn[e // 2], e % 2) for e in range(8)]
        sneg_sb = [chunk(xb[e // 2], e % 2) for e in range(8)]
        s_sb = [chunk(xn[4 + e // 2], e % 2) for e in range(8)]
        wrT_b = [chunk(xb[4 + e // 2], e % 2) for e in range(8)]
        wiT_b = [chunk(q_all, 0), chunk(q_all, 1),
                 chunk(k_all, 0), chunk(k_all, 1),
                 chunk(v_tm, 0), chunk(v_tm, 1),
                 chunk(at_all, 0), chunk(at_all, 1)]
        br_ch = [chunk(wq_all, j) for j in range(4)] + \
                [chunk(wk_all, j) for j in range(4)]
        bi_ch = [chunk(wv_all, j) for j in range(4)] + \
                [chunk(wo_all, j) for j in range(4)]

        bank_w = [None] * 8
        bank_r = [None] * 8

        def mm_group(bank, mms, waits=()):
            ws0 = [w for w in waits if w is not None]
            if bank_r[bank] is not None:
                ws0.append(bank_r[bank])
            ev = None
            n = len(mms)
            for i, mm in enumerate(mms):
                w = list(ws0) if i == 0 else []
                if mm[0] == "T":
                    _, src, idt = mm
                    ev = b.emit("tensor",
                                lambda e, src=src, idt=idt, bank=bank:
                                e.transpose(
                                    ps[bank][:, :].bitcast(BF16)[:, 0:128],
                                    src, idt),
                                waits=tuple(w))
                    continue
                lh, rh, o = mm[0], mm[1], mm[2]
                if len(mm) > 3 and mm[3] is not None:
                    w.append(mm[3])
                ev = b.emit("tensor",
                            lambda e, lh=lh, rh=rh, o=o, s0=(i == 0),
                            s1=(i == n - 1):
                            e.matmul(o, lh, rh, start=s0, stop=s1),
                            waits=tuple(w))
            bank_w[bank] = ev
            return ev

        def reader(eng, fn, bank, waits=()):
            ev = b.emit(eng, fn, waits=tuple(waits) + (bank_w[bank],))
            bank_r[bank] = ev
            return ev

        ones_ev = b.emit("vector", lambda e: e.memset(ones[:, :], 1.0))
        b.emit("vector", lambda e: e.memset(czc[:, 0:1], 0.0))
        cz_ev = b.emit("vector", lambda e: e.memset(czc[:, 1:2], EPS))
        nc.const_aps.register(F32, 0.0, czc[:, 0:1])
        nc.const_aps.register(F32, EPS, czc[:, 1:2])

        gl_last = [None] * 4
        c_ld, s_ld, sn_ld = [None] * 8, [None] * 8, [None] * 8
        for e in range(8):
            ln = e % 4
            c_ld[e] = b.dma("gpsimd", c_sb[e], c_d[e * 128:(e + 1) * 128, :],
                            waits=(gl_last[ln],), lane=f"gc{ln}")
            gl_last[ln] = c_ld[e]
        for e in range(8):
            ln = e % 4
            sn_ld[e] = b.dma("gpsimd", sneg_sb[e],
                             sn_d[e * 128:(e + 1) * 128, :],
                             waits=(gl_last[ln],), lane=f"gc{ln}")
            gl_last[ln] = sn_ld[e]
        for e in range(8):
            ln = e % 4
            s_ld[e] = b.dma("gpsimd", s_sb[e], s_d[e * 128:(e + 1) * 128, :],
                            waits=(gl_last[ln],), lane=f"gc{ln}")
            gl_last[ln] = s_ld[e]
        mbt_ld = [None] * 4
        for h in range(4):
            mbt_ld[h] = b.dma("gpsimd", mbt_sb[h][:, :], mbt_d[h],
                              waits=(gl_last[h % 4],), lane=f"gc{h % 4}")
            gl_last[h % 4] = mbt_ld[h]
        small_ready = b.dma("gpsimd", identb[:, :], id_d[:, :],
                            waits=(gl_last[0],), lane="gc0")
        gl_last[0] = small_ready
        b.dma("gpsimd", fb_sb[:, :], fb_d[:, :], waits=(gl_last[1],),
              lane="gc1")
        gl_last[1] = b.ops[-1]
        b.dma("gpsimd", bo_sb[:, :], bo_d[:, :], waits=(gl_last[2],),
              lane="gc2")
        gl_last[2] = b.ops[-1]
        sm_fb_bo = (small_ready, gl_last[1], gl_last[2])

        xfp_last = [None] * 4
        wrT_cast, wiT_cast = [None] * 8, [None] * 8
        for e in range(8):
            i = e % 4
            ld = b.dma("sync", xfp[i][:, :], wrt_d[e * 128:(e + 1) * 128, :],
                       waits=(xfp_last[i],), lane=f"xf{i}")
            wrT_cast[e] = b.emit("vector",
                                 lambda en, d=wrT_b[e], s_=xfp[i]:
                                 en.tensor_copy(d, s_[:, :]), waits=(ld,))
            xfp_last[i] = wrT_cast[e]
        for e in range(8):
            i = e % 4
            ld = b.dma("sync", xfp[i][:, :], wit_d[e * 128:(e + 1) * 128, :],
                       waits=(xfp_last[i],), lane=f"xf{i}")
            wiT_cast[e] = b.emit("scalar",
                                 lambda en, d=wiT_b[e], s_=xfp[i]:
                                 en.activation(d, s_[:, :], AF.Copy,
                                               bias=0.0, scale=1.0),
                                 waits=(ld, cz_ev))
            xfp_last[i] = wiT_cast[e]

        br_ev = [None] * 8
        bi_ev = [None] * 8
        ev_par = 0

        def m_wave(kind, half):
            nonlocal ev_par
            groups = [(rc, hf) for rc in range(half * 4, half * 4 + 4)
                      for hf in range(2)]
            for ec in range(8):
                for gi, (rc, hf) in enumerate(groups):
                    if kind == "A":
                        lh, rh = wrT_b[ec][:, rc * 128:(rc + 1) * 128], \
                            c_sb[ec][:, hf * 512:(hf + 1) * 512]
                        w = (wrT_cast[ec], c_ld[ec])
                    elif kind == "A2":
                        lh, rh = wiT_b[ec][:, rc * 128:(rc + 1) * 128], \
                            sneg_sb[ec][:, hf * 512:(hf + 1) * 512]
                        w = (wiT_cast[ec], sn_ld[ec])
                    elif kind == "B":
                        lh, rh = wiT_b[ec][:, rc * 128:(rc + 1) * 128], \
                            c_sb[ec][:, hf * 512:(hf + 1) * 512]
                        w = (wiT_cast[ec], c_ld[ec])
                    elif kind == "B2":
                        lh, rh = wrT_b[ec][:, rc * 128:(rc + 1) * 128], \
                            s_sb[ec][:, hf * 512:(hf + 1) * 512]
                        w = (wrT_cast[ec], s_ld[ec])
                    elif kind == "C":
                        lh, rh = c_sb[ec][:, rc * 128:(rc + 1) * 128], \
                            br_ch[ec][:, hf * 512:(hf + 1) * 512]
                        w = (c_ld[ec], br_ev[ec])
                    else:
                        lh, rh = s_sb[ec][:, rc * 128:(rc + 1) * 128], \
                            bi_ch[ec][:, hf * 512:(hf + 1) * 512]
                        w = (s_ld[ec], bi_ev[ec])
                    bank = gi
                    first = (ec == 0 and kind in ("A", "B", "C"))
                    ws = list(w) if gi == 0 else []
                    if first and bank_r[bank] is not None:
                        ws.append(bank_r[bank])
                    ev = b.emit("tensor",
                                lambda e, lh=lh, rh=rh, bank=bank,
                                s0=first,
                                s1=(ec == 7 and kind in ("A2", "B2", "C2")):
                                e.matmul(ps[bank][:, :], lh, rh,
                                         start=s0, stop=s1),
                                waits=tuple(ws))
                    bank_w[bank] = ev
            if kind in ("A", "B", "C"):
                return
            for gi, (rc, hf) in enumerate(groups):
                if kind == "A2":
                    dst = br_ch[rc][:, hf * 512:(hf + 1) * 512]
                elif kind == "B2":
                    dst = bi_ch[rc][:, hf * 512:(hf + 1) * 512]
                else:
                    dst = m_sb[rc][:, hf * 512:(hf + 1) * 512]
                eng = "vector" if ev_par % 2 == 0 else "scalar"
                ev_par += 1
                if eng == "vector":
                    ev = reader(eng, lambda e, d=dst, bank=gi:
                                e.tensor_copy(d, ps[bank][:, :]), gi)
                else:
                    ev = reader(eng, lambda e, d=dst, bank=gi:
                                e.activation(d, ps[bank][:, :], AF.Copy,
                                             bias=0.0, scale=1.0), gi)
                if kind == "A2":
                    br_ev[rc] = ev
                elif kind == "B2":
                    bi_ev[rc] = ev

        for half in range(2):
            m_wave("A", half)
            m_wave("A2", half)
        for half in range(2):
            m_wave("B", half)
            m_wave("B2", half)
        b2_done = next(op for op in reversed(b.ops) if op.eng == "tensor")
        for half in range(2):
            m_wave("C", half)
            m_wave("C2", half)
        m_mm_done = next(op for op in reversed(b.ops) if op.eng == "tensor")
        m_ready_v = next(op for op in reversed(b.ops) if op.eng == "vector")
        m_ready_s = next(op for op in reversed(b.ops) if op.eng == "scalar")

        xb_guard = [[b2_done] for _ in range(8)]
        pend_cast = {}

        def emit_prefetch(nbi, dc):
            evs = []
            for hf in range(2):
                i = (dc * 2 + hf) % 4
                ld = b.dma("sync", xfp[i][:, :],
                           x_d[nbi, dc * 128:(dc + 1) * 128,
                               hf * 1024:(hf + 1) * 1024],
                           waits=(xfp_last[i],), lane=f"xf{i}")
                eng = "vector" if dc % 2 else "scalar"
                w = (ld,) + tuple(xb_guard[dc])
                if eng == "vector":
                    ev = b.emit(eng, lambda e, dc=dc, hf=hf, s_=xfp[i]:
                                e.tensor_copy(
                                    xb[dc][:, hf * 1024:(hf + 1) * 1024],
                                    s_[:, :]), waits=w)
                else:
                    ev = b.emit(eng, lambda e, dc=dc, hf=hf, s_=xfp[i]:
                                e.activation(
                                    xb[dc][:, hf * 1024:(hf + 1) * 1024],
                                    s_[:, :], AF.Copy, bias=0.0,
                                    scale=1.0), waits=w + (cz_ev,))
                xfp_last[i] = ev
                evs.append(ev)
            pend_cast[dc] = evs[-1]

        for dc in range(8):
            emit_prefetch(0, dc)

        w_cast = {}
        for nmi, (src, dst, scale) in enumerate(
                ((wq_d, wq_all, NORM), (wk_d, wk_all, 1.0),
                 (wv_d, wv_all, 1.0), (wo_d, wo_all, 1.0))):
            for j in range(4):
                i = (nmi * 4 + j) % 4
                ld = b.dma("sync", xfp[i][:, :], src[:, j * 1024:(j + 1) * 1024],
                           waits=(xfp_last[i],), lane=f"xf{i}")
                if scale == 1.0:
                    cast = b.emit("vector",
                                  lambda e, d=chunk(dst, j), s_=xfp[i]:
                                  e.tensor_copy(d, s_[:, :]),
                                  waits=(ld, m_mm_done))
                else:
                    cast = b.emit("scalar",
                                  lambda e, d=chunk(dst, j), s_=xfp[i],
                                  sc=scale:
                                  e.activation(d, s_[:, :], AF.Copy,
                                               bias=0.0, scale=sc),
                                  waits=(ld, m_mm_done))
                xfp_last[i] = cast
                w_cast[(nmi, j)] = cast
        wq_ready = lambda h: w_cast[(0, h)]
        wk_ready = lambda h: w_cast[(1, h)]
        wv_ready = lambda h: w_cast[(2, h)]
        wo_ready = w_cast[(3, 3)]

        xn_guard = [[m_mm_done] for _ in range(8)]
        lnstat_guard = [m_mm_done]
        xg_guard = [[None] for _ in range(4)]
        out_guard = [None] * 4
        ysl_guard = [None] * 8
        sq_mm = [None, None]
        last_stores = [None]
        apply_evs = [None] * 8

        def emit_stats(bi):
            cast_ev = [None] * 8
            for dc in range(8):
                cast_ev[dc] = pend_cast.pop(dc)
            sx_last = None
            invs_done = None
            prev_rsq = None
            for tt in range(4):
                sl = slice(tt * 512, (tt + 1) * 512)
                mms = [(ones[:, :], xb[dc][:, sl], ps[6][:, :],
                        cast_ev[dc] if tt == 0 else None) for dc in range(8)]
                sx_last = mm_group(6, mms, waits=(ones_ev,))
                ev = None
                for dc in range(8):
                    j = dc % 2
                    sq = b.emit("vector",
                                lambda e, j=j, dc=dc, sl=sl:
                                e.tensor_mul(xsq_t[j][:, :], xb[dc][:, sl],
                                             xb[dc][:, sl]),
                                waits=(cast_ev[dc], sq_mm[j]))
                    w = [bank_r[7], ones_ev] if dc == 0 else []
                    w.append(sq)
                    ev = b.emit("tensor",
                                lambda e, j=j, s0=(dc == 0), s1=(dc == 7):
                                e.matmul(ps[7][:, :], ones[:, :],
                                         xsq_t[j][:, :], start=s0, stop=s1),
                                waits=tuple(x for x in w if x is not None))
                    sq_mm[j] = ev
                bank_w[7] = ev
                mu_ev = reader("vector", lambda e: e.tensor_scalar_mul(
                    st_a[:, :], ps[6][:, :], 1.0 / D), 6,
                    waits=(lnstat_guard[0] if tt == 0 else None,))
                b.emit("vector", lambda e, sl=sl: e.tensor_copy(
                    mu_bf[:, sl], st_a[:, :]))
                musq = b.emit("vector", lambda e: e.tensor_mul(
                    st_b[:, :], st_a[:, :], st_a[:, :]))
                var = reader("vector", lambda e: e.scalar_tensor_tensor(
                    st_c[:, :], ps[7][:, :], 1.0 / D, st_b[:, :],
                    OP.mult, OP.subtract), 7, waits=(musq,))
                sstd = b.emit("scalar", lambda e: e.activation(
                    st_b[:, :], st_c[:, :], AF.Sqrt, bias=czc[:, 1:2],
                    scale=1.0), waits=(var, prev_rsq))
                prev_rsq = sstd
                b.emit("vector", lambda e: e.reciprocal(
                    st_a[:, :], st_b[:, :]), waits=(sstd,))
                invs_done = b.emit("vector", lambda e, sl=sl: e.tensor_copy(
                    invs_bf[:, sl], st_a[:, :]))
            return invs_done, sx_last

        def emit_apply(stats_done, sx_last):
            for dc in range(8):
                b.emit("vector", lambda e, dc=dc: e.tensor_sub(
                    xn[dc][:, :], xb[dc][:, :], mu_bf[:, :]),
                    waits=(stats_done,) + tuple(xn_guard[dc]))
                apply_evs[dc] = b.emit("vector", lambda e, dc=dc:
                                       e.tensor_mul(xn[dc][:, :], xn[dc][:, :],
                                                    invs_bf[:, :]))
            for dc in range(8):
                xb_guard[dc] = [apply_evs[dc], sx_last]
            lnstat_guard[0] = apply_evs[7]
            return apply_evs[7]

        pcount = [0]

        def emit_proj(h, xn_ready, vpm_guard):
            L, dil = LS[h], DILS[h]
            qk_all = None
            v_all = None
            ngrp = (L + 3) // 4
            for (wt, wrdy, dst, eng) in (
                    (wq_all, wq_ready(h), q_all, "scalar"),
                    (wk_all, wk_ready(h), k_all, "scalar"),
                    (wv_all, wv_ready(h), v_pm, "vector")):
                for g in range(ngrp):
                    nb = min(4, L - g * 4)
                    bank = 4 + pcount[0] % 2
                    pcount[0] += 1
                    mms = []
                    for dc in range(8):
                        rhs = (xn[dc][:, :]
                               .rearrange("p (n bb) -> p n bb", bb=dil * 128)
                               [:, g * 4:g * 4 + nb, 0:128])
                        lh = wt[:, (h * 8 + dc) * 128:(h * 8 + dc + 1) * 128]
                        mms.append((lh, rhs, ps[bank][:, 0:nb * 128]))
                    mm_group(bank, mms, waits=(xn_ready, wrdy))
                    src = (ps[bank][:, 0:nb * 128]
                           .rearrange("p (n s) -> p n s", s=128))
                    base = dst[:, 0:128 * L] if dst is v_pm else \
                        dst[:, HOFF[h]:HOFF[h] + 128 * L]
                    dstv = (base.rearrange("p (s n) -> p s n", n=L)
                            [:, :, g * 4:g * 4 + nb].transpose([0, 2, 1]))
                    evw = (vpm_guard,) if dst is v_pm else ()
                    ev = reader(eng,
                                (lambda e, dstv=dstv, src=src:
                                 e.copy(dstv, src)) if eng == "scalar"
                                else (lambda e, dstv=dstv, src=src:
                                      e.tensor_copy(dstv, src)), bank,
                                waits=evw)
                    if dst is v_pm:
                        v_all = ev
                    else:
                        qk_all = ev
            vt_last = None
            tr_last = None
            for sbk in range(L):
                cols = slice(sbk * 128, (sbk + 1) * 128)
                bank = 4 + pcount[0] % 2
                pcount[0] += 1
                tr_last = mm_group(bank, [("T", v_pm[:, cols], identb[:, :])],
                                   waits=(v_all, small_ready))
                vt_last = reader(
                    "vector",
                    lambda e, sbk=sbk, h=h, bank=bank:
                    e.tensor_copy(v_tm[:, HOFF[h] + sbk * 128:
                                       HOFF[h] + (sbk + 1) * 128],
                                  ps[bank][:, :].bitcast(BF16)[:, 0:128]),
                    bank)
            return qk_all, vt_last, tr_last

        def emit_chain_scores(u, qk_evs):
            sc_done = [None] * 4
            for bk, pieces in enumerate(_unit_banks(u)):
                bank = 4 + bk
                for (h, sbk, col) in pieces:
                    qc = slice(HOFF[h] + sbk * 128, HOFF[h] + (sbk + 1) * 128)
                    mms = [(mbt_sb[h][:, :], identb[:, :],
                            ps[bank][:, col:col + 128]),
                           (k_all[:, qc], q_all[:, qc],
                            ps[bank][:, col:col + 128])]
                    sc_done[bk] = mm_group(
                        bank, mms,
                        waits=(qk_evs[h], small_ready, mbt_ld[h]))
            return sc_done

        def _unit_banks(u):
            out = [[] for _ in range(4)]
            bb = 0
            for (h, s0, nb) in UNITS[u]:
                for j in range(nb):
                    out[bb // 4].append((h, s0 + j, (bb % 4) * 128))
                    bb += 1
            return out

        def emit_chain_mid(u, sc_done, apply_last):
            banks = _unit_banks(u)
            ncols = [len(p) * 128 for p in banks]
            exp_evs = []
            for bk in range(4):
                bank = 4 + bk
                off = UOFF[u] + bk * 512
                ev = reader("scalar",
                            lambda e, bank=bank, off=off, nc_=ncols[bk]:
                            e.activation(e_all[:, off:off + nc_],
                                         ps[bank][:, 0:nc_], AF.Exp,
                                         bias=0.0, scale=1.0), bank,
                            waits=(apply_last,))
                exp_evs.append(ev)
            nblk = sum(len(p) for p in banks)
            r0 = 16 * u
            red = b.emit("vector",
                         lambda e, u=u, nblk=nblk, r0=r0:
                         e.reduce_sum(
                             rs_sb[:, r0:r0 + nblk]
                             .rearrange("p (n o) -> p n o", o=1),
                             e_all[:, UOFF[u]:UOFF[u] + nblk * 128]
                             .rearrange("p (n s) -> p n s", s=128),
                             AX.X), waits=tuple(exp_evs))
            rec = None
            for c0 in range(0, nblk, 8):
                cw = min(8, nblk - c0)
                rec = b.emit("vector",
                             lambda e, a=r0 + c0, w=cw:
                             e.reciprocal(rsi_sb[:, a:a + w],
                                          rs_sb[:, a:a + w]),
                             waits=(red,) if c0 == 0 else ())
            scale_evs = [[None] * len(p) for p in banks]
            bb = 0
            for bk in range(4):
                for j in range(len(banks[bk])):
                    off = UOFF[u] + bb * 128
                    scale_evs[bk][j] = b.emit(
                        "scalar",
                        lambda e, off=off, r=r0 + bb:
                        e.activation(e_all[:, off:off + 128],
                                     e_all[:, off:off + 128], AF.Copy,
                                     bias=0.0, scale=rsi_sb[:, r:r + 1]),
                        waits=(rec,))
                    bb += 1
            return scale_evs

        def emit_chain_av(u, scale_evs, vt_done):
            att_done = None
            banks = _unit_banks(u)
            for bk in range(4):
                bank = 4 + bk
                mms = []
                for j, (h, sbk, col) in enumerate(banks[bk]):
                    vc = slice(HOFF[h] + sbk * 128, HOFF[h] + (sbk + 1) * 128)
                    eoff = UOFF[u] + bk * 512 + col
                    mms.append((v_tm[:, vc], e_all[:, eoff:eoff + 128],
                                ps[bank][:, col:col + 128],
                                scale_evs[bk][j]))
                mm_group(bank, mms, waits=(vt_done,))
                nc_ = len(banks[bk]) * 128
                off = UOFF[u] + bk * 512
                att_done = reader("scalar",
                                  lambda e, off=off, nc_=nc_, bank=bank:
                                  e.copy(at_all[:, off:off + nc_],
                                         ps[bank][:, 0:nc_]), bank)
            return att_done

        def emit_y(bi, ui, xn_ready):
            jt, oc = ui % 4, ui // 4
            bank = ui % 2
            slot = ui % 8
            mms = []
            for dc in range(8):
                for (pc, st0, stp, cnt) in JT_FFN[jt]:
                    mms.append((m_sb[dc][:, oc * 128:(oc + 1) * 128],
                                _sl(xn[dc][:, :], st0, stp, cnt),
                                ps[bank][:, pc:pc + cnt]))
            gw = ()
            if ui == 0:
                gw = (xn_ready,) + ((m_ready_v, m_ready_s) if bi == 0 else ())
            ylast = mm_group(bank, mms, waits=gw)
            yev = reader("vector",
                         lambda e, slot=slot, oc=oc, bank=bank:
                         e.tensor_scalar_add(
                             y_sb[:, slot * 512:(slot + 1) * 512],
                             ps[bank][:, :], fb_sb[:, oc:oc + 1]), bank,
                         waits=(sm_fb_bo[1] if (bi == 0 and ui == 0) else None,
                                ysl_guard[slot]))
            return ylast, yev

        def emit_silu(gi, yev_last):
            half = gi % 2
            ev = b.emit("scalar",
                        lambda e, half=half:
                        e.activation(y_sb[:, half * 2048:(half + 1) * 2048],
                                     y_sb[:, half * 2048:(half + 1) * 2048],
                                     AF.Silu, bias=0.0, scale=1.0),
                        waits=(yev_last,))
            return ev

        def emit_s(bi, si, silu_ev, att_evs):
            jt, oc = si % 4, si // 4
            bank = 2 + si % 2
            slot = si % 8
            ob = si % 4
            xgb = si % 4
            xg_ld = b.dma("gpsimd", xg_sb[xgb][:, :],
                          xg_d[bi, oc * 128:(oc + 1) * 128,
                               jt * 512:(jt + 1) * 512],
                          waits=tuple(xg_guard[xgb]), lane=f"xg{xgb}")
            mms = []
            opw = [att_evs[0] if jt < 2 else att_evs[3]]
            if bi == 0 and si == 0:
                opw += [wo_ready, sm_fb_bo[2]]
            for (h, pc, st0, stp, cnt) in JT_OP[jt]:
                mms.append((wo_all[:, h * 1024 + oc * 128:
                                   h * 1024 + (oc + 1) * 128],
                            _sl(at_all[:, HOFF[h]:HOFF[h] + 128 * LS[h]],
                                st0, stp, cnt),
                            ps[bank][:, pc:pc + cnt]))
            mm_group(bank, mms, waits=tuple(opw))
            comb = reader("vector",
                          lambda e, slot=slot, oc=oc, bank=bank, ob=ob:
                          e.scalar_tensor_tensor(
                              out_sb[ob][:, :], ps[bank][:, :],
                              bo_sb[:, oc:oc + 1],
                              y_sb[:, slot * 512:(slot + 1) * 512],
                              OP.add, OP.mult), bank,
                          waits=(silu_ev, out_guard[ob]))
            ysl_guard[slot] = comb
            add = b.emit("vector",
                         lambda e, ob=ob, xgb=xgb:
                         e.tensor_add(out_sb[ob][:, :], out_sb[ob][:, :],
                                      xg_sb[xgb][:, :]),
                         waits=(xg_ld,))
            xg_guard[xgb] = [add]
            stv = b.dma("sync", out_d[bi, oc * 128:(oc + 1) * 128,
                                      jt * 512:(jt + 1) * 512],
                        out_sb[ob][:, :], waits=(add,),
                        lane=f"st{ob}")
            out_guard[ob] = stv
            last_stores[0] = stv
            return stv

        if "msb" in dump:
            mo = dbg_out("msb", (8, 128, 1024), BF16)
            for dc in range(8):
                b.dma("gpsimd", mo[dc], m_sb[dc][:, :],
                      waits=(m_ready_v, m_ready_s), lane="gt")
        nbatch = 1 if DEBUG_BI0_ONLY else BPC
        stats_state = None
        for bi in range(nbatch):
            if stats_state is None:
                stats_state = emit_stats(bi)
            stats_done, sx_last = stats_state
            stats_state = None
            xn_ready = emit_apply(stats_done, sx_last)
            if "xn" in dump and bi == 0:
                xo = dbg_out("xn", (8, 128, T), BF16)
                for dc in range(8):
                    b.dma("gpsimd", xo[dc], xn[dc][:, :],
                          waits=(xn_ready,), lane="gt")
            qk_evs = [None] * 4
            vt_evs = [None] * 4
            vpm_guard = small_ready
            for h in range(4):
                qk_evs[h], vt_evs[h], vpm_guard = \
                    emit_proj(h, xn_ready, vpm_guard)
            if "qkv" in dump and bi == 0:
                qo = dbg_out("qk", (2, 128, 3840), BF16)
                b.dma("gpsimd", qo[0], q_all[:, :], waits=(qk_evs[3],),
                      lane="gt")
                b.dma("gpsimd", qo[1], k_all[:, :], waits=(qk_evs[3],),
                      lane="gt")
                vo = dbg_out("vtm", (128, 3840), BF16)
                b.dma("gpsimd", vo, v_tm[:, :], waits=(vt_evs[3],), lane="gt")
            att_evs = [None] * 4
            yi = si = 0
            yevs = {}
            silu_evs = {}
            have_att1 = [False]

            def run_y(n):
                nonlocal yi
                for _ in range(n):
                    if yi >= 32 or yi - si >= 8:
                        return
                    _, yev = emit_y(bi, yi, xn_ready)
                    yevs[yi] = yev
                    yi += 1
                    if yi % 4 == 0:
                        gi = yi // 4 - 1
                        silu_evs[gi] = emit_silu(gi, yevs[4 * gi + 3])

            def run_s(n):
                nonlocal si
                for _ in range(n):
                    if si >= 32 or (si // 4 + 1) * 4 > yi:
                        return
                    if si % 4 >= 2 and not have_att1[0]:
                        return
                    emit_s(bi, si, silu_evs[si // 4], att_evs)
                    si += 1

            sc = emit_chain_scores(0, qk_evs)
            run_y(4)
            scale0 = emit_chain_mid(0, sc, xn_ready)
            run_y(2)
            att_evs[0] = emit_chain_av(0, scale0, vt_evs[0])
            run_s(2)
            run_y(3)
            sc = emit_chain_scores(1, qk_evs)
            run_y(1)
            scale1 = emit_chain_mid(1, sc, xn_ready)
            run_y(1)
            att1 = emit_chain_av(1, scale1, vt_evs[3])
            att_evs[1] = att_evs[2] = att_evs[3] = att1
            have_att1[0] = True
            e_free = att1
            if bi + 1 < BPC:
                for dc in range(8):
                    emit_prefetch(bi + 1, dc)
            if "chain" in dump and bi == 0:
                eo = dbg_out("eall", (128, 3840), BF16)
                tp = b.dma("gpsimd", eo, e_all, waits=(att1,), lane="gt")
                ro = dbg_out("rs", (2, 128, 32))
                b.dma("gpsimd", ro[0], rs_sb[:, :], waits=(att1,), lane="gt")
                b.dma("gpsimd", ro[1], rsi_sb[:, :], waits=(att1,), lane="gt")
                ao = dbg_out("atall", (128, 3840), BF16)
                b.dma("gpsimd", ao, at_all[:, :], waits=(att1,), lane="gt")
                e_free = tp
            lnstat_guard[0] = e_free
            while yi < 32 or si < 32:
                prog = yi + si
                run_y(2)
                run_s(2)
                if si >= 16 and stats_state is None and bi + 1 < nbatch:
                    stats_state = emit_stats(bi + 1)
                assert yi + si > prog, (yi, si)
            if stats_state is None and bi + 1 < nbatch:
                stats_state = emit_stats(bi + 1)
            for dc in range(8):
                xn_guard[dc] = [yevs[31]]

        b.emit("gpsimd", lambda e: e.engine_nop(),
               waits=tuple(s for s in last_stores if s is not None))

        for op in b.ops:
            if op.is_dma:
                op.needed = True
            for w in op.waits:
                w.needed = True
        ctr = {}
        for op in b.ops:
            if not op.needed:
                continue
            key = ("dma_" + op.lane) if op.is_dma else op.eng
            op.sem = key
            ctr[key] = ctr.get(key, 0) + (16 if op.is_dma else 1)
            op.val = ctr[key]
        sems = {key: stack.enter_context(nc.semaphore("sem_" + key))
                for key in sorted(ctr)}
        with nc.Block() as block:
            def mk(name):
                def run(eng_obj):
                    for op in b.ops:
                        if op.eng != name:
                            continue
                        for w in op.waits:
                            if (not w.is_dma) and w.eng == name:
                                continue
                            eng_obj.wait_ge(sems[w.sem], w.val)
                        inst = op.fn(eng_obj)
                        if op.needed:
                            inst.then_inc(sems[op.sem],
                                          16 if op.is_dma else 1)
                return run
            block.sync(mk("sync"))
            block.tensor(mk("tensor"))
            block.vector(mk("vector"))
            block.scalar(mk("scalar"))
            block.gpsimd(mk("gpsimd"))
    return nc


def _stage_inputs(L0_attn_x, L0_attn_Wk, L0_attn_Wq, L0_attn_Wv,
                  L0_attn_W_out, L0_attn_b_out, L0_ffn_w_freq_real,
                  L0_ffn_w_freq_imag, L0_ffn_bias):
    perm = _gperm()
    f32 = np.float32
    idx = np.arange(D)
    ang = (2.0 * np.pi / D) * ((idx[:, None] * idx[None, :]) % D)
    Cm = (np.cos(ang) / 32.0).astype(ml_dtypes.bfloat16)
    Sm = (np.sin(ang) / 32.0).astype(ml_dtypes.bfloat16)
    Snm = (-np.sin(ang) / 32.0).astype(ml_dtypes.bfloat16)
    mbt = np.zeros((4, 128, 128), np.float32)
    for h in range(4):
        L = LS[h]
        p = np.arange(128)
        sl_p, n = p // L, p % L
        m = p % L
        allowed = (sl_p[:, None] == sl_p[None, :]) & \
                  (m[None, :] <= n[:, None])
        mb = np.where(allowed, 0.0, NEG).astype(f32)
        mbt[h] = mb.T
    mbt = mbt.astype(ml_dtypes.bfloat16)

    def pack_qkv(W, scalefold=False):
        big = np.empty((128, 4096), f32)
        for h in range(4):
            wt = np.asarray(W[h], f32).T
            for dc in range(8):
                big[:, (h * 8 + dc) * 128:(h * 8 + dc + 1) * 128] = \
                    wt[dc * 128:(dc + 1) * 128, :]
        return np.ascontiguousarray(big)

    wq = pack_qkv(L0_attn_Wq)
    wk = pack_qkv(L0_attn_Wk)
    wv = pack_qkv(L0_attn_Wv)
    wo_t = np.asarray(L0_attn_W_out, f32).T
    wo = np.empty((128, 4096), f32)
    for h in range(4):
        wo[:, h * 1024:(h + 1) * 1024] = wo_t[h * 128:(h + 1) * 128, :]
    wrt = np.ascontiguousarray(np.asarray(L0_ffn_w_freq_real, f32).T)
    wit = np.ascontiguousarray(np.asarray(L0_ffn_w_freq_imag, f32).T)
    fbT = np.ascontiguousarray(np.asarray(L0_ffn_bias, f32).reshape(8, 128).T)
    boT = np.ascontiguousarray(
        np.asarray(L0_attn_b_out, f32).reshape(8, 128).T)
    shared = dict(wq_big=wq, wk_big=wk, wv_big=wv, wo_big=np.ascontiguousarray(wo),
                  bo_t=boT, fb_t=fbT, wrt=wrt, wit=wit,
                  cdft=Cm, sdft=Sm, sndft=Snm, mbt=mbt,
                  identb=np.eye(128, dtype=ml_dtypes.bfloat16))
    xs = np.asarray(L0_attn_x, f32)
    in_maps = []
    for c in range(NCORES):
        xc = xs[BPC * c:BPC * (c + 1)]
        m = dict(shared)
        m["x"] = np.ascontiguousarray(xc.transpose(0, 2, 1))
        m["xg"] = np.ascontiguousarray(xc[:, perm, :].transpose(0, 2, 1))
        in_maps.append(m)
    return in_maps, perm


def kernel(L0_attn_x, L0_attn_Wk, L0_attn_Wq, L0_attn_Wv, L0_attn_W_out,
           L0_attn_b_out, L0_ffn_w_freq_real, L0_ffn_w_freq_imag,
           L0_ffn_bias):
    in_maps, perm = _stage_inputs(
        L0_attn_x, L0_attn_Wk, L0_attn_Wq, L0_attn_Wv, L0_attn_W_out,
        L0_attn_b_out, L0_ffn_w_freq_real, L0_ffn_w_freq_imag, L0_ffn_bias)
    nc = build(dump=tuple(DEBUG_TAPS))
    res = run_bass_kernel_spmd(nc, in_maps, core_ids=list(range(NCORES)))
    out = np.empty((B_FULL, T, D), np.float32)
    for c in range(NCORES):
        o = np.asarray(res.results[c]["out"])
        out[BPC * c:BPC * (c + 1)][:, perm, :] = o.transpose(0, 2, 1)
    if DEBUG_TAPS:
        kernel.debug = [{k: np.asarray(v) for k, v in r.items()
                         if k.startswith("dbg_")} for r in res.results]
    return out


# version 07
# speedup vs baseline: 1.0922x; 1.0167x over previous
import math
import numpy as np
import ml_dtypes
import concourse.bass as bass
import concourse.mybir as mybir
from concourse.bass_utils import run_bass_kernel_spmd

F32 = mybir.dt.float32
BF16 = mybir.dt.bfloat16
AF = mybir.ActivationFunctionType
OP = mybir.AluOpType
AX = mybir.AxisListType

B_FULL, T, D = 32, 2048, 1024
NCORES = 8
BPC = B_FULL // NCORES
DILS = [1, 2, 4, 8]
LS = [16, 8, 4, 2]
HOFF = [0, 2048, 3072, 3584]
NBLK = 30
NORM = 1.0 / math.sqrt(128.0)
NEG = -1.0e9
EPS = 1e-5

JT_FFN = [
    [(0, 1, 2, 512)],
    [(0, 1025, 2, 512)],
    [(0, 2, 4, 512)],
    [(0, 4, 8, 256), (256, 0, 8, 256)],
]
JT_OP = [
    [(0, 0, 1, 2, 512)],
    [(0, 0, 1025, 2, 512)],
    [(0, 0, 2, 4, 512), (1, 0, 1, 2, 512)],
    [(0, 0, 4, 8, 256), (0, 256, 0, 8, 256),
     (1, 0, 2, 4, 256), (1, 256, 0, 4, 256),
     (2, 0, 1, 2, 256), (2, 256, 0, 2, 256),
     (3, 256, 0, 1, 256)],
]

UNITS = [
    [(0, 0, 4), (0, 4, 4), (0, 8, 4), (0, 12, 4)],
    [(1, 0, 4), (1, 4, 4), (2, 0, 4), (3, 0, 2)],
]
UOFF = [0, 2048]


DEBUG_TAPS = []
DEBUG_BI0_ONLY = False


def _gperm():
    t = np.arange(T)
    return np.concatenate([t[t % 2 == 1], t[t % 4 == 2],
                           t[t % 8 == 4], t[t % 8 == 0]])


class _Op:
    __slots__ = ("eng", "waits", "fn", "needed", "val", "is_dma", "sem",
                 "lane")

    def __init__(self, eng, waits, fn, is_dma, lane):
        self.eng, self.waits, self.fn = eng, waits, fn
        self.needed, self.val, self.is_dma = False, None, is_dma
        self.sem = None
        self.lane = lane


class Builder:
    def __init__(self, nc):
        self.nc = nc
        self.ops = []

    def emit(self, eng, fn, waits=(), is_dma=False, lane=None):
        ws = tuple(w for w in waits if w is not None)
        for w in ws:
            assert isinstance(w, _Op)
        op = _Op(eng, ws, fn, is_dma, lane)
        self.ops.append(op)
        return op

    def dma(self, eng, out, in_, waits=(), lane="g"):
        return self.emit(eng, lambda e: e.dma_start(out=out, in_=in_),
                         waits, is_dma=True, lane=lane)


def _sl(ap, st0, stp, cnt):
    return ap[:, st0:st0 + stp * (cnt - 1) + 1:stp]


def build(dump=()):
    nc = bass.Bass("TRN2", target_bir_lowering=False, debug=False)
    b = Builder(nc)

    def din(name, shape, dt):
        return nc.dram_tensor(name, list(shape), dt, kind="ExternalInput").ap()

    x_d = din("x", (BPC, D, T), F32)
    xg_d = din("xg", (BPC, D, T), F32)
    wq_d = din("wq_big", (128, 4096), F32)
    wk_d = din("wk_big", (128, 4096), F32)
    wv_d = din("wv_big", (128, 4096), F32)
    wo_d = din("wo_big", (128, 4096), F32)
    bo_d = din("bo_t", (128, 8), F32)
    fb_d = din("fb_t", (128, 8), F32)
    wrt_d = din("wrt", (D, D), F32)
    wit_d = din("wit", (D, D), F32)
    c_d = din("cdft", (D, D), BF16)
    s_d = din("sdft", (D, D), BF16)
    sn_d = din("sndft", (D, D), BF16)
    mbt_d = din("mbt", (4, 128, 128), BF16)
    id_d = din("identb", (128, 128), BF16)
    out_d = nc.dram_tensor("out", [BPC, D, T], F32, kind="ExternalOutput").ap()

    def dbg_out(name, shape, dt=F32):
        return nc.dram_tensor("dbg_" + name, list(shape), dt,
                              kind="ExternalOutput").ap()

    import contextlib
    stack = contextlib.ExitStack()
    sb = lambda name, shape, dt: stack.enter_context(
        nc.sbuf_tensor(name, list(shape), dt))
    psum = lambda name: stack.enter_context(
        nc.psum_tensor(name, [128, 512], F32))

    with stack:
        xb = [sb(f"xb{c}", [128, T], BF16) for c in range(8)]
        xn = [sb(f"xn{c}", [128, T], BF16) for c in range(8)]
        lnstat = sb("lnstat", [128, 4096], BF16)
        mu_bf = lnstat[:, 0:2048]
        invs_bf = lnstat[:, 2048:4096]
        e_all = lnstat[:, 0:3840]
        xfp = [sb(f"xfp{i}", [128, 1024], F32) for i in range(4)]
        xsq_t = [sb(f"xsq{i}", [128, 512], BF16) for i in range(2)]
        st_a = sb("st_a", [128, 512], F32)
        st_b = sb("st_b", [128, 512], F32)
        st_c = sb("st_c", [128, 512], F32)
        ones = sb("ones", [128, 128], BF16)
        czc = sb("czc", [128, 2], F32)
        wq_all = sb("wq_all", [128, 4096], BF16)
        wk_all = sb("wk_all", [128, 4096], BF16)
        wv_all = sb("wv_all", [128, 4096], BF16)
        wo_all = sb("wo_all", [128, 4096], BF16)
        fb_sb = sb("fb_sb", [128, 8], F32)
        bo_sb = sb("bo_sb", [128, 8], F32)
        mbt_sb = [sb(f"mbt{h}", [128, 128], BF16) for h in range(4)]
        identb = sb("identb_sb", [128, 128], BF16)
        m_sb = [sb(f"m{c}", [128, 1024], BF16) for c in range(8)]
        q_all = sb("q_all", [128, 3840], BF16)
        k_all = sb("k_all", [128, 3840], BF16)
        v_tm = sb("v_tm", [128, 3840], BF16)
        at_all = sb("at_all", [128, 3840], BF16)
        v_pm = at_all[:, 0:2048]
        rs_sb = sb("rs_sb", [128, 32], F32)
        rsi_sb = sb("rsi_sb", [128, 32], F32)
        y_sb = sb("y_sb", [128, 6144], BF16)
        out_sb = [sb(f"os{i}", [128, 512], F32) for i in range(4)]
        xg_sb = [sb(f"xgs{i}", [128, 512], F32) for i in range(4)]
        ps = [psum(f"ps{i}") for i in range(8)]

        chunk = lambda t, j: t[:, j * 1024:(j + 1) * 1024]
        c_sb = [chunk(xn[e // 2], e % 2) for e in range(8)]
        sneg_sb = [chunk(xb[e // 2], e % 2) for e in range(8)]
        s_sb = [chunk(xn[4 + e // 2], e % 2) for e in range(8)]
        wrT_b = [chunk(xb[4 + e // 2], e % 2) for e in range(8)]
        wiT_b = [chunk(q_all, 0), chunk(q_all, 1),
                 chunk(k_all, 0), chunk(k_all, 1),
                 chunk(v_tm, 0), chunk(v_tm, 1),
                 chunk(at_all, 0), chunk(at_all, 1)]
        br_ch = [chunk(wq_all, j) for j in range(4)] + \
                [chunk(wk_all, j) for j in range(4)]
        bi_ch = [chunk(wv_all, j) for j in range(4)] + \
                [chunk(wo_all, j) for j in range(4)]

        bank_w = [None] * 8
        bank_r = [None] * 8

        def mm_group(bank, mms, waits=()):
            ws0 = [w for w in waits if w is not None]
            if bank_r[bank] is not None:
                ws0.append(bank_r[bank])
            ev = None
            n = len(mms)
            for i, mm in enumerate(mms):
                w = list(ws0) if i == 0 else []
                if mm[0] == "T":
                    _, src, idt = mm
                    ev = b.emit("tensor",
                                lambda e, src=src, idt=idt, bank=bank:
                                e.transpose(
                                    ps[bank][:, :].bitcast(BF16)[:, 0:128],
                                    src, idt),
                                waits=tuple(w))
                    continue
                lh, rh, o = mm[0], mm[1], mm[2]
                if len(mm) > 3 and mm[3] is not None:
                    w.append(mm[3])
                ev = b.emit("tensor",
                            lambda e, lh=lh, rh=rh, o=o, s0=(i == 0),
                            s1=(i == n - 1):
                            e.matmul(o, lh, rh, start=s0, stop=s1),
                            waits=tuple(w))
            bank_w[bank] = ev
            return ev

        def reader(eng, fn, bank, waits=()):
            ev = b.emit(eng, fn, waits=tuple(waits) + (bank_w[bank],))
            bank_r[bank] = ev
            return ev

        ones_ev = b.emit("vector", lambda e: e.memset(ones[:, :], 1.0))
        b.emit("vector", lambda e: e.memset(czc[:, 0:1], 0.0))
        cz_ev = b.emit("vector", lambda e: e.memset(czc[:, 1:2], EPS))
        nc.const_aps.register(F32, 0.0, czc[:, 0:1])
        nc.const_aps.register(F32, EPS, czc[:, 1:2])

        gl_last = [None] * 4
        c_ld, s_ld, sn_ld = [None] * 8, [None] * 8, [None] * 8
        for e in range(8):
            ln = e % 4
            c_ld[e] = b.dma("gpsimd", c_sb[e], c_d[e * 128:(e + 1) * 128, :],
                            waits=(gl_last[ln],), lane=f"gc{ln}")
            gl_last[ln] = c_ld[e]
        for e in range(8):
            ln = e % 4
            sn_ld[e] = b.dma("gpsimd", sneg_sb[e],
                             sn_d[e * 128:(e + 1) * 128, :],
                             waits=(gl_last[ln],), lane=f"gc{ln}")
            gl_last[ln] = sn_ld[e]
        for e in range(8):
            ln = e % 4
            s_ld[e] = b.dma("gpsimd", s_sb[e], s_d[e * 128:(e + 1) * 128, :],
                            waits=(gl_last[ln],), lane=f"gc{ln}")
            gl_last[ln] = s_ld[e]
        mbt_ld = [None] * 4
        for h in range(4):
            mbt_ld[h] = b.dma("gpsimd", mbt_sb[h][:, :], mbt_d[h],
                              waits=(gl_last[h % 4],), lane=f"gc{h % 4}")
            gl_last[h % 4] = mbt_ld[h]
        small_ready = b.dma("gpsimd", identb[:, :], id_d[:, :],
                            waits=(gl_last[0],), lane="gc0")
        gl_last[0] = small_ready
        b.dma("gpsimd", fb_sb[:, :], fb_d[:, :], waits=(gl_last[1],),
              lane="gc1")
        gl_last[1] = b.ops[-1]
        b.dma("gpsimd", bo_sb[:, :], bo_d[:, :], waits=(gl_last[2],),
              lane="gc2")
        gl_last[2] = b.ops[-1]
        sm_fb_bo = (small_ready, gl_last[1], gl_last[2])

        xfp_last = [None] * 4
        wrT_cast, wiT_cast = [None] * 8, [None] * 8
        for e in range(8):
            i = e % 4
            ld = b.dma("sync", xfp[i][:, :], wrt_d[e * 128:(e + 1) * 128, :],
                       waits=(xfp_last[i],), lane=f"xf{i}")
            wrT_cast[e] = b.emit("vector",
                                 lambda en, d=wrT_b[e], s_=xfp[i]:
                                 en.tensor_copy(d, s_[:, :]), waits=(ld,))
            xfp_last[i] = wrT_cast[e]
        for e in range(8):
            i = e % 4
            ld = b.dma("sync", xfp[i][:, :], wit_d[e * 128:(e + 1) * 128, :],
                       waits=(xfp_last[i],), lane=f"xf{i}")
            wiT_cast[e] = b.emit("scalar",
                                 lambda en, d=wiT_b[e], s_=xfp[i]:
                                 en.activation(d, s_[:, :], AF.Copy,
                                               bias=0.0, scale=1.0),
                                 waits=(ld, cz_ev))
            xfp_last[i] = wiT_cast[e]

        br_ev = [None] * 8
        bi_ev = [None] * 8
        ev_par = 0

        def m_wave(kind, half):
            nonlocal ev_par
            groups = [(rc, hf) for rc in range(half * 4, half * 4 + 4)
                      for hf in range(2)]
            for ec in range(8):
                for gi, (rc, hf) in enumerate(groups):
                    if kind == "A":
                        lh, rh = wrT_b[ec][:, rc * 128:(rc + 1) * 128], \
                            c_sb[ec][:, hf * 512:(hf + 1) * 512]
                        w = (wrT_cast[ec], c_ld[ec])
                    elif kind == "A2":
                        lh, rh = wiT_b[ec][:, rc * 128:(rc + 1) * 128], \
                            sneg_sb[ec][:, hf * 512:(hf + 1) * 512]
                        w = (wiT_cast[ec], sn_ld[ec])
                    elif kind == "B":
                        lh, rh = wiT_b[ec][:, rc * 128:(rc + 1) * 128], \
                            c_sb[ec][:, hf * 512:(hf + 1) * 512]
                        w = (wiT_cast[ec], c_ld[ec])
                    elif kind == "B2":
                        lh, rh = wrT_b[ec][:, rc * 128:(rc + 1) * 128], \
                            s_sb[ec][:, hf * 512:(hf + 1) * 512]
                        w = (wrT_cast[ec], s_ld[ec])
                    elif kind == "C":
                        lh, rh = c_sb[ec][:, rc * 128:(rc + 1) * 128], \
                            br_ch[ec][:, hf * 512:(hf + 1) * 512]
                        w = (c_ld[ec], br_ev[ec])
                    else:
                        lh, rh = s_sb[ec][:, rc * 128:(rc + 1) * 128], \
                            bi_ch[ec][:, hf * 512:(hf + 1) * 512]
                        w = (s_ld[ec], bi_ev[ec])
                    bank = gi
                    first = (ec == 0 and kind in ("A", "B", "C"))
                    ws = list(w) if gi == 0 else []
                    if first and bank_r[bank] is not None:
                        ws.append(bank_r[bank])
                    ev = b.emit("tensor",
                                lambda e, lh=lh, rh=rh, bank=bank,
                                s0=first,
                                s1=(ec == 7 and kind in ("A2", "B2", "C2")):
                                e.matmul(ps[bank][:, :], lh, rh,
                                         start=s0, stop=s1),
                                waits=tuple(ws))
                    bank_w[bank] = ev
            if kind in ("A", "B", "C"):
                return
            for gi, (rc, hf) in enumerate(groups):
                if kind == "A2":
                    dst = br_ch[rc][:, hf * 512:(hf + 1) * 512]
                elif kind == "B2":
                    dst = bi_ch[rc][:, hf * 512:(hf + 1) * 512]
                else:
                    dst = m_sb[rc][:, hf * 512:(hf + 1) * 512]
                eng = "vector" if ev_par % 2 == 0 else "scalar"
                ev_par += 1
                if eng == "vector":
                    ev = reader(eng, lambda e, d=dst, bank=gi:
                                e.tensor_copy(d, ps[bank][:, :]), gi)
                else:
                    ev = reader(eng, lambda e, d=dst, bank=gi:
                                e.activation(d, ps[bank][:, :], AF.Copy,
                                             bias=0.0, scale=1.0), gi)
                if kind == "A2":
                    br_ev[rc] = ev
                elif kind == "B2":
                    bi_ev[rc] = ev

        for half in range(2):
            m_wave("A", half)
            m_wave("A2", half)
        for half in range(2):
            m_wave("B", half)
            m_wave("B2", half)
        b2_done = next(op for op in reversed(b.ops) if op.eng == "tensor")
        for half in range(2):
            m_wave("C", half)
            m_wave("C2", half)
        m_mm_done = next(op for op in reversed(b.ops) if op.eng == "tensor")
        m_ready_v = next(op for op in reversed(b.ops) if op.eng == "vector")
        m_ready_s = next(op for op in reversed(b.ops) if op.eng == "scalar")

        xb_guard = [[b2_done] for _ in range(8)]
        pend_cast = {}

        def emit_prefetch(nbi, dc):
            evs = []
            for hf in range(2):
                i = (dc * 2 + hf) % 4
                ld = b.dma("sync", xfp[i][:, :],
                           x_d[nbi, dc * 128:(dc + 1) * 128,
                               hf * 1024:(hf + 1) * 1024],
                           waits=(xfp_last[i],), lane=f"xf{i}")
                eng = "vector" if dc % 2 else "scalar"
                w = (ld,) + tuple(xb_guard[dc])
                if eng == "vector":
                    ev = b.emit(eng, lambda e, dc=dc, hf=hf, s_=xfp[i]:
                                e.tensor_copy(
                                    xb[dc][:, hf * 1024:(hf + 1) * 1024],
                                    s_[:, :]), waits=w)
                else:
                    ev = b.emit(eng, lambda e, dc=dc, hf=hf, s_=xfp[i]:
                                e.activation(
                                    xb[dc][:, hf * 1024:(hf + 1) * 1024],
                                    s_[:, :], AF.Copy, bias=0.0,
                                    scale=1.0), waits=w + (cz_ev,))
                xfp_last[i] = ev
                evs.append(ev)
            pend_cast[dc] = evs[-1]

        for dc in range(8):
            emit_prefetch(0, dc)

        w_cast = {}
        for nmi, (src, dst, scale) in enumerate(
                ((wq_d, wq_all, NORM), (wk_d, wk_all, 1.0),
                 (wv_d, wv_all, 1.0), (wo_d, wo_all, 1.0))):
            for j in range(4):
                i = (nmi * 4 + j) % 4
                ld = b.dma("sync", xfp[i][:, :], src[:, j * 1024:(j + 1) * 1024],
                           waits=(xfp_last[i],), lane=f"xf{i}")
                if scale == 1.0:
                    cast = b.emit("vector",
                                  lambda e, d=chunk(dst, j), s_=xfp[i]:
                                  e.tensor_copy(d, s_[:, :]),
                                  waits=(ld, m_mm_done))
                else:
                    cast = b.emit("scalar",
                                  lambda e, d=chunk(dst, j), s_=xfp[i],
                                  sc=scale:
                                  e.activation(d, s_[:, :], AF.Copy,
                                               bias=0.0, scale=sc),
                                  waits=(ld, m_mm_done))
                xfp_last[i] = cast
                w_cast[(nmi, j)] = cast
        wq_ready = lambda h: w_cast[(0, h)]
        wk_ready = lambda h: w_cast[(1, h)]
        wv_ready = lambda h: w_cast[(2, h)]
        wo_ready = w_cast[(3, 3)]

        xn_guard = [[m_mm_done] for _ in range(8)]
        lnstat_guard = [m_mm_done]
        xg_guard = [[None] for _ in range(4)]
        out_guard = [None] * 4
        ysl_guard = [None] * 12
        sq_mm = [None, None]
        last_stores = [None]
        apply_evs = [None] * 8

        def emit_stats(bi):
            cast_ev = [None] * 8
            for dc in range(8):
                cast_ev[dc] = pend_cast.pop(dc)
            sx_last = None
            invs_done = None
            prev_rsq = None
            for tt in range(4):
                sl = slice(tt * 512, (tt + 1) * 512)
                mms = [(ones[:, :], xb[dc][:, sl], ps[6][:, :],
                        cast_ev[dc] if tt == 0 else None) for dc in range(8)]
                sx_last = mm_group(6, mms, waits=(ones_ev,))
                ev = None
                for dc in range(8):
                    j = dc % 2
                    sq = b.emit("vector",
                                lambda e, j=j, dc=dc, sl=sl:
                                e.tensor_mul(xsq_t[j][:, :], xb[dc][:, sl],
                                             xb[dc][:, sl]),
                                waits=(cast_ev[dc], sq_mm[j]))
                    w = [bank_r[7], ones_ev] if dc == 0 else []
                    w.append(sq)
                    ev = b.emit("tensor",
                                lambda e, j=j, s0=(dc == 0), s1=(dc == 7):
                                e.matmul(ps[7][:, :], ones[:, :],
                                         xsq_t[j][:, :], start=s0, stop=s1),
                                waits=tuple(x for x in w if x is not None))
                    sq_mm[j] = ev
                bank_w[7] = ev
                mu_ev = reader("vector", lambda e: e.tensor_scalar_mul(
                    st_a[:, :], ps[6][:, :], 1.0 / D), 6,
                    waits=(lnstat_guard[0] if tt == 0 else None,))
                b.emit("vector", lambda e, sl=sl: e.tensor_copy(
                    mu_bf[:, sl], st_a[:, :]))
                musq = b.emit("vector", lambda e: e.tensor_mul(
                    st_b[:, :], st_a[:, :], st_a[:, :]))
                var = reader("vector", lambda e: e.scalar_tensor_tensor(
                    st_c[:, :], ps[7][:, :], 1.0 / D, st_b[:, :],
                    OP.mult, OP.subtract), 7, waits=(musq,))
                sstd = b.emit("scalar", lambda e: e.activation(
                    st_b[:, :], st_c[:, :], AF.Sqrt, bias=czc[:, 1:2],
                    scale=1.0), waits=(var, prev_rsq))
                prev_rsq = sstd
                b.emit("vector", lambda e: e.reciprocal(
                    st_a[:, :], st_b[:, :]), waits=(sstd,))
                invs_done = b.emit("vector", lambda e, sl=sl: e.tensor_copy(
                    invs_bf[:, sl], st_a[:, :]))
            return invs_done, sx_last

        def emit_apply(stats_done, sx_last):
            apply_tt = [None] * 4
            for tt in range(4):
                sl = slice(tt * 512, (tt + 1) * 512)
                ev = None
                for dc in range(8):
                    w = ((stats_done,) + tuple(xn_guard[dc])) if tt == 0 \
                        else ()
                    b.emit("vector", lambda e, dc=dc, sl=sl: e.tensor_sub(
                        xn[dc][:, sl], xb[dc][:, sl], mu_bf[:, sl]), waits=w)
                    ev = b.emit("vector", lambda e, dc=dc, sl=sl:
                                e.tensor_mul(xn[dc][:, sl], xn[dc][:, sl],
                                             invs_bf[:, sl]))
                    if tt == 3:
                        apply_evs[dc] = ev
                apply_tt[tt] = ev
            for dc in range(8):
                xb_guard[dc] = [apply_evs[dc], sx_last]
            lnstat_guard[0] = apply_evs[7]
            return apply_tt

        pcount = [0]

        def emit_proj(h, apply_tt, vpm_guard):
            L, dil = LS[h], DILS[h]
            qk_all = None
            v_all = None
            ngrp = (L + 3) // 4
            for (wt, wrdy, dst, eng) in (
                    (wq_all, wq_ready(h), q_all, "scalar"),
                    (wk_all, wk_ready(h), k_all, "scalar"),
                    (wv_all, wv_ready(h), v_pm, "vector")):
                for g in range(ngrp):
                    nb = min(4, L - g * 4)
                    bank = 4 + pcount[0] % 2
                    pcount[0] += 1
                    mms = []
                    for dc in range(8):
                        rhs = (xn[dc][:, :]
                               .rearrange("p (n bb) -> p n bb", bb=dil * 128)
                               [:, g * 4:g * 4 + nb, 0:128])
                        lh = wt[:, (h * 8 + dc) * 128:(h * 8 + dc + 1) * 128]
                        mms.append((lh, rhs, ps[bank][:, 0:nb * 128]))
                    tt_need = ((g * 4 + nb - 1) * dil * 128 + 127) // 512
                    mm_group(bank, mms, waits=(apply_tt[tt_need], wrdy))
                    src = (ps[bank][:, 0:nb * 128]
                           .rearrange("p (n s) -> p n s", s=128))
                    base = dst[:, 0:128 * L] if dst is v_pm else \
                        dst[:, HOFF[h]:HOFF[h] + 128 * L]
                    dstv = (base.rearrange("p (s n) -> p s n", n=L)
                            [:, :, g * 4:g * 4 + nb].transpose([0, 2, 1]))
                    evw = (vpm_guard,) if dst is v_pm else ()
                    ev = reader(eng,
                                (lambda e, dstv=dstv, src=src:
                                 e.copy(dstv, src)) if eng == "scalar"
                                else (lambda e, dstv=dstv, src=src:
                                      e.tensor_copy(dstv, src)), bank,
                                waits=evw)
                    if dst is v_pm:
                        v_all = ev
                    else:
                        qk_all = ev
            vt_last = None
            tr_last = None
            for sbk in range(L):
                cols = slice(sbk * 128, (sbk + 1) * 128)
                bank = 4 + pcount[0] % 2
                pcount[0] += 1
                tr_last = mm_group(bank, [("T", v_pm[:, cols], identb[:, :])],
                                   waits=(v_all, small_ready))
                vt_last = reader(
                    "vector",
                    lambda e, sbk=sbk, h=h, bank=bank:
                    e.tensor_copy(v_tm[:, HOFF[h] + sbk * 128:
                                       HOFF[h] + (sbk + 1) * 128],
                                  ps[bank][:, :].bitcast(BF16)[:, 0:128]),
                    bank)
            return qk_all, vt_last, tr_last

        def emit_chain_scores(u, qk_evs):
            sc_done = [None] * 4
            for bk, pieces in enumerate(_unit_banks(u)):
                bank = 4 + bk
                for (h, sbk, col) in pieces:
                    qc = slice(HOFF[h] + sbk * 128, HOFF[h] + (sbk + 1) * 128)
                    mms = [(mbt_sb[h][:, :], identb[:, :],
                            ps[bank][:, col:col + 128]),
                           (k_all[:, qc], q_all[:, qc],
                            ps[bank][:, col:col + 128])]
                    sc_done[bk] = mm_group(
                        bank, mms,
                        waits=(qk_evs[h], small_ready, mbt_ld[h]))
            return sc_done

        def _unit_banks(u):
            out = [[] for _ in range(4)]
            bb = 0
            for (h, s0, nb) in UNITS[u]:
                for j in range(nb):
                    out[bb // 4].append((h, s0 + j, (bb % 4) * 128))
                    bb += 1
            return out

        def emit_chain_mid(u, sc_done, apply_last):
            banks = _unit_banks(u)
            ncols = [len(p) * 128 for p in banks]
            exp_evs = []
            for bk in range(4):
                bank = 4 + bk
                off = UOFF[u] + bk * 512
                ev = reader("scalar",
                            lambda e, bank=bank, off=off, nc_=ncols[bk]:
                            e.activation(e_all[:, off:off + nc_],
                                         ps[bank][:, 0:nc_], AF.Exp,
                                         bias=0.0, scale=1.0), bank,
                            waits=(apply_last,))
                exp_evs.append(ev)
            nblk = sum(len(p) for p in banks)
            r0 = 16 * u
            red = b.emit("vector",
                         lambda e, u=u, nblk=nblk, r0=r0:
                         e.reduce_sum(
                             rs_sb[:, r0:r0 + nblk]
                             .rearrange("p (n o) -> p n o", o=1),
                             e_all[:, UOFF[u]:UOFF[u] + nblk * 128]
                             .rearrange("p (n s) -> p n s", s=128),
                             AX.X), waits=tuple(exp_evs))
            rec = None
            for c0 in range(0, nblk, 8):
                cw = min(8, nblk - c0)
                rec = b.emit("vector",
                             lambda e, a=r0 + c0, w=cw:
                             e.reciprocal(rsi_sb[:, a:a + w],
                                          rs_sb[:, a:a + w]),
                             waits=(red,) if c0 == 0 else ())
            scale_evs = [[None] * len(p) for p in banks]
            bb = 0
            for bk in range(4):
                for j in range(len(banks[bk])):
                    off = UOFF[u] + bb * 128
                    scale_evs[bk][j] = b.emit(
                        "scalar",
                        lambda e, off=off, r=r0 + bb:
                        e.activation(e_all[:, off:off + 128],
                                     e_all[:, off:off + 128], AF.Copy,
                                     bias=0.0, scale=rsi_sb[:, r:r + 1]),
                        waits=(rec,))
                    bb += 1
            return scale_evs

        def emit_chain_av(u, scale_evs, vt_done):
            att_done = None
            banks = _unit_banks(u)
            for bk in range(4):
                bank = 4 + bk
                mms = []
                for j, (h, sbk, col) in enumerate(banks[bk]):
                    vc = slice(HOFF[h] + sbk * 128, HOFF[h] + (sbk + 1) * 128)
                    eoff = UOFF[u] + bk * 512 + col
                    mms.append((v_tm[:, vc], e_all[:, eoff:eoff + 128],
                                ps[bank][:, col:col + 128],
                                scale_evs[bk][j]))
                mm_group(bank, mms, waits=(vt_done,))
                nc_ = len(banks[bk]) * 128
                off = UOFF[u] + bk * 512
                att_done = reader("scalar",
                                  lambda e, off=off, nc_=nc_, bank=bank:
                                  e.copy(at_all[:, off:off + nc_],
                                         ps[bank][:, 0:nc_]), bank)
            return att_done

        def emit_y(bi, ui, xn_ready):
            jt, oc = ui % 4, ui // 4
            bank = ui % 2
            slot = ui % 12
            mms = []
            for dc in range(8):
                for (pc, st0, stp, cnt) in JT_FFN[jt]:
                    mms.append((m_sb[dc][:, oc * 128:(oc + 1) * 128],
                                _sl(xn[dc][:, :], st0, stp, cnt),
                                ps[bank][:, pc:pc + cnt]))
            gw = ()
            if ui == 0:
                gw = (xn_ready,) + ((m_ready_v, m_ready_s) if bi == 0 else ())
            ylast = mm_group(bank, mms, waits=gw)
            yev = reader("vector",
                         lambda e, slot=slot, oc=oc, bank=bank:
                         e.tensor_scalar_add(
                             y_sb[:, slot * 512:(slot + 1) * 512],
                             ps[bank][:, :], fb_sb[:, oc:oc + 1]), bank,
                         waits=(sm_fb_bo[1] if (bi == 0 and ui == 0) else None,
                                ysl_guard[slot]))
            return ylast, yev

        def emit_silu(gi, yev_last):
            third = gi % 3
            ev = b.emit("scalar",
                        lambda e, third=third:
                        e.activation(y_sb[:, third * 2048:(third + 1) * 2048],
                                     y_sb[:, third * 2048:(third + 1) * 2048],
                                     AF.Silu, bias=0.0, scale=1.0),
                        waits=(yev_last,))
            return ev

        def emit_s(bi, si, silu_ev, att_evs):
            jt, oc = si % 4, si // 4
            bank = 2 + si % 2
            slot = si % 12
            ob = si % 4
            xgb = si % 4
            xg_ld = b.dma("gpsimd", xg_sb[xgb][:, :],
                          xg_d[bi, oc * 128:(oc + 1) * 128,
                               jt * 512:(jt + 1) * 512],
                          waits=tuple(xg_guard[xgb]), lane=f"xg{xgb}")
            mms = []
            opw = [att_evs[0] if jt < 2 else att_evs[3]]
            if bi == 0 and si == 0:
                opw += [wo_ready, sm_fb_bo[2]]
            for (h, pc, st0, stp, cnt) in JT_OP[jt]:
                mms.append((wo_all[:, h * 1024 + oc * 128:
                                   h * 1024 + (oc + 1) * 128],
                            _sl(at_all[:, HOFF[h]:HOFF[h] + 128 * LS[h]],
                                st0, stp, cnt),
                            ps[bank][:, pc:pc + cnt]))
            mm_group(bank, mms, waits=tuple(opw))
            comb = reader("vector",
                          lambda e, slot=slot, oc=oc, bank=bank, ob=ob:
                          e.scalar_tensor_tensor(
                              out_sb[ob][:, :], ps[bank][:, :],
                              bo_sb[:, oc:oc + 1],
                              y_sb[:, slot * 512:(slot + 1) * 512],
                              OP.add, OP.mult), bank,
                          waits=(silu_ev, out_guard[ob]))
            ysl_guard[slot] = comb
            add = b.emit("vector",
                         lambda e, ob=ob, xgb=xgb:
                         e.tensor_add(out_sb[ob][:, :], out_sb[ob][:, :],
                                      xg_sb[xgb][:, :]),
                         waits=(xg_ld,))
            xg_guard[xgb] = [add]
            stv = b.dma("sync", out_d[bi, oc * 128:(oc + 1) * 128,
                                      jt * 512:(jt + 1) * 512],
                        out_sb[ob][:, :], waits=(add,),
                        lane=f"st{ob}")
            out_guard[ob] = stv
            last_stores[0] = stv
            return stv

        if "msb" in dump:
            mo = dbg_out("msb", (8, 128, 1024), BF16)
            for dc in range(8):
                b.dma("gpsimd", mo[dc], m_sb[dc][:, :],
                      waits=(m_ready_v, m_ready_s), lane="gt")
        nbatch = 1 if DEBUG_BI0_ONLY else BPC
        stats_state = None
        for bi in range(nbatch):
            if stats_state is None:
                stats_state = emit_stats(bi)
            stats_done, sx_last = stats_state
            stats_state = None
            apply_tt = emit_apply(stats_done, sx_last)
            xn_ready = apply_tt[3]
            if "xn" in dump and bi == 0:
                xo = dbg_out("xn", (8, 128, T), BF16)
                for dc in range(8):
                    b.dma("gpsimd", xo[dc], xn[dc][:, :],
                          waits=(xn_ready,), lane="gt")
            qk_evs = [None] * 4
            vt_evs = [None] * 4
            vpm_guard = small_ready
            for h in range(4):
                qk_evs[h], vt_evs[h], vpm_guard = \
                    emit_proj(h, apply_tt, vpm_guard)
            if "qkv" in dump and bi == 0:
                qo = dbg_out("qk", (2, 128, 3840), BF16)
                b.dma("gpsimd", qo[0], q_all[:, :], waits=(qk_evs[3],),
                      lane="gt")
                b.dma("gpsimd", qo[1], k_all[:, :], waits=(qk_evs[3],),
                      lane="gt")
                vo = dbg_out("vtm", (128, 3840), BF16)
                b.dma("gpsimd", vo, v_tm[:, :], waits=(vt_evs[3],), lane="gt")
            att_evs = [None] * 4
            yi = si = 0
            yevs = {}
            silu_evs = {}
            have_att1 = [False]

            def run_y(n):
                nonlocal yi
                for _ in range(n):
                    if yi >= 32 or yi - si >= 12:
                        return
                    _, yev = emit_y(bi, yi, xn_ready)
                    yevs[yi] = yev
                    yi += 1
                    if yi % 4 == 0:
                        gi = yi // 4 - 1
                        silu_evs[gi] = emit_silu(gi, yevs[4 * gi + 3])

            def run_s(n):
                nonlocal si
                for _ in range(n):
                    if si >= 32 or (si // 4 + 1) * 4 > yi:
                        return
                    if si % 4 >= 2 and not have_att1[0]:
                        return
                    emit_s(bi, si, silu_evs[si // 4], att_evs)
                    si += 1

            sc = emit_chain_scores(0, qk_evs)
            run_y(4)
            scale0 = emit_chain_mid(0, sc, xn_ready)
            run_y(2)
            att_evs[0] = emit_chain_av(0, scale0, vt_evs[0])
            run_s(2)
            run_y(3)
            sc = emit_chain_scores(1, qk_evs)
            run_y(1)
            scale1 = emit_chain_mid(1, sc, xn_ready)
            run_y(1)
            att1 = emit_chain_av(1, scale1, vt_evs[3])
            att_evs[1] = att_evs[2] = att_evs[3] = att1
            have_att1[0] = True
            e_free = att1
            if bi + 1 < BPC:
                for dc in range(8):
                    emit_prefetch(bi + 1, dc)
            if "chain" in dump and bi == 0:
                eo = dbg_out("eall", (128, 3840), BF16)
                tp = b.dma("gpsimd", eo, e_all, waits=(att1,), lane="gt")
                ro = dbg_out("rs", (2, 128, 32))
                b.dma("gpsimd", ro[0], rs_sb[:, :], waits=(att1,), lane="gt")
                b.dma("gpsimd", ro[1], rsi_sb[:, :], waits=(att1,), lane="gt")
                ao = dbg_out("atall", (128, 3840), BF16)
                b.dma("gpsimd", ao, at_all[:, :], waits=(att1,), lane="gt")
                e_free = tp
            lnstat_guard[0] = e_free
            while yi < 32 or si < 32:
                prog = yi + si
                run_y(2)
                run_s(2)
                if si >= 8 and stats_state is None and bi + 1 < nbatch:
                    stats_state = emit_stats(bi + 1)
                assert yi + si > prog, (yi, si)
            if stats_state is None and bi + 1 < nbatch:
                stats_state = emit_stats(bi + 1)
            for dc in range(8):
                xn_guard[dc] = [yevs[31]]

        b.emit("gpsimd", lambda e: e.engine_nop(),
               waits=tuple(s for s in last_stores if s is not None))

        for op in b.ops:
            if op.is_dma:
                op.needed = True
            for w in op.waits:
                w.needed = True
        ctr = {}
        for op in b.ops:
            if not op.needed:
                continue
            key = ("dma_" + op.lane) if op.is_dma else op.eng
            op.sem = key
            ctr[key] = ctr.get(key, 0) + (16 if op.is_dma else 1)
            op.val = ctr[key]
        sems = {key: stack.enter_context(nc.semaphore("sem_" + key))
                for key in sorted(ctr)}
        with nc.Block() as block:
            def mk(name):
                def run(eng_obj):
                    for op in b.ops:
                        if op.eng != name:
                            continue
                        for w in op.waits:
                            if (not w.is_dma) and w.eng == name:
                                continue
                            eng_obj.wait_ge(sems[w.sem], w.val)
                        inst = op.fn(eng_obj)
                        if op.needed:
                            inst.then_inc(sems[op.sem],
                                          16 if op.is_dma else 1)
                return run
            block.sync(mk("sync"))
            block.tensor(mk("tensor"))
            block.vector(mk("vector"))
            block.scalar(mk("scalar"))
            block.gpsimd(mk("gpsimd"))
    return nc


def _stage_inputs(L0_attn_x, L0_attn_Wk, L0_attn_Wq, L0_attn_Wv,
                  L0_attn_W_out, L0_attn_b_out, L0_ffn_w_freq_real,
                  L0_ffn_w_freq_imag, L0_ffn_bias):
    perm = _gperm()
    f32 = np.float32
    idx = np.arange(D)
    ang = (2.0 * np.pi / D) * ((idx[:, None] * idx[None, :]) % D)
    Cm = (np.cos(ang) / 32.0).astype(ml_dtypes.bfloat16)
    Sm = (np.sin(ang) / 32.0).astype(ml_dtypes.bfloat16)
    Snm = (-np.sin(ang) / 32.0).astype(ml_dtypes.bfloat16)
    mbt = np.zeros((4, 128, 128), np.float32)
    for h in range(4):
        L = LS[h]
        p = np.arange(128)
        sl_p, n = p // L, p % L
        m = p % L
        allowed = (sl_p[:, None] == sl_p[None, :]) & \
                  (m[None, :] <= n[:, None])
        mb = np.where(allowed, 0.0, NEG).astype(f32)
        mbt[h] = mb.T
    mbt = mbt.astype(ml_dtypes.bfloat16)

    def pack_qkv(W, scalefold=False):
        big = np.empty((128, 4096), f32)
        for h in range(4):
            wt = np.asarray(W[h], f32).T
            for dc in range(8):
                big[:, (h * 8 + dc) * 128:(h * 8 + dc + 1) * 128] = \
                    wt[dc * 128:(dc + 1) * 128, :]
        return np.ascontiguousarray(big)

    wq = pack_qkv(L0_attn_Wq)
    wk = pack_qkv(L0_attn_Wk)
    wv = pack_qkv(L0_attn_Wv)
    wo_t = np.asarray(L0_attn_W_out, f32).T
    wo = np.empty((128, 4096), f32)
    for h in range(4):
        wo[:, h * 1024:(h + 1) * 1024] = wo_t[h * 128:(h + 1) * 128, :]
    wrt = np.ascontiguousarray(np.asarray(L0_ffn_w_freq_real, f32).T)
    wit = np.ascontiguousarray(np.asarray(L0_ffn_w_freq_imag, f32).T)
    fbT = np.ascontiguousarray(np.asarray(L0_ffn_bias, f32).reshape(8, 128).T)
    boT = np.ascontiguousarray(
        np.asarray(L0_attn_b_out, f32).reshape(8, 128).T)
    shared = dict(wq_big=wq, wk_big=wk, wv_big=wv, wo_big=np.ascontiguousarray(wo),
                  bo_t=boT, fb_t=fbT, wrt=wrt, wit=wit,
                  cdft=Cm, sdft=Sm, sndft=Snm, mbt=mbt,
                  identb=np.eye(128, dtype=ml_dtypes.bfloat16))
    xs = np.asarray(L0_attn_x, f32)
    in_maps = []
    for c in range(NCORES):
        xc = xs[BPC * c:BPC * (c + 1)]
        m = dict(shared)
        m["x"] = np.ascontiguousarray(xc.transpose(0, 2, 1))
        m["xg"] = np.ascontiguousarray(xc[:, perm, :].transpose(0, 2, 1))
        in_maps.append(m)
    return in_maps, perm


def kernel(L0_attn_x, L0_attn_Wk, L0_attn_Wq, L0_attn_Wv, L0_attn_W_out,
           L0_attn_b_out, L0_ffn_w_freq_real, L0_ffn_w_freq_imag,
           L0_ffn_bias):
    in_maps, perm = _stage_inputs(
        L0_attn_x, L0_attn_Wk, L0_attn_Wq, L0_attn_Wv, L0_attn_W_out,
        L0_attn_b_out, L0_ffn_w_freq_real, L0_ffn_w_freq_imag, L0_ffn_bias)
    nc = build(dump=tuple(DEBUG_TAPS))
    res = run_bass_kernel_spmd(nc, in_maps, core_ids=list(range(NCORES)))
    out = np.empty((B_FULL, T, D), np.float32)
    for c in range(NCORES):
        o = np.asarray(res.results[c]["out"])
        out[BPC * c:BPC * (c + 1)][:, perm, :] = o.transpose(0, 2, 1)
    if DEBUG_TAPS:
        kernel.debug = [{k: np.asarray(v) for k, v in r.items()
                         if k.startswith("dbg_")} for r in res.results]
    return out


# version 08
# speedup vs baseline: 1.0932x; 1.0009x over previous
import math
import numpy as np
import ml_dtypes
import concourse.bass as bass
import concourse.mybir as mybir
from concourse.bass_utils import run_bass_kernel_spmd

F32 = mybir.dt.float32
BF16 = mybir.dt.bfloat16
AF = mybir.ActivationFunctionType
OP = mybir.AluOpType
AX = mybir.AxisListType

B_FULL, T, D = 32, 2048, 1024
NCORES = 8
BPC = B_FULL // NCORES
DILS = [1, 2, 4, 8]
LS = [16, 8, 4, 2]
HOFF = [0, 2048, 3072, 3584]
NBLK = 30
NORM = 1.0 / math.sqrt(128.0)
NEG = -1.0e9
EPS = 1e-5

JT_FFN = [
    [(0, 1, 2, 512)],
    [(0, 1025, 2, 512)],
    [(0, 2, 4, 512)],
    [(0, 4, 8, 256), (256, 0, 8, 256)],
]
JT_OP = [
    [(0, 0, 1, 2, 512)],
    [(0, 0, 1025, 2, 512)],
    [(0, 0, 2, 4, 512), (1, 0, 1, 2, 512)],
    [(0, 0, 4, 8, 256), (0, 256, 0, 8, 256),
     (1, 0, 2, 4, 256), (1, 256, 0, 4, 256),
     (2, 0, 1, 2, 256), (2, 256, 0, 2, 256),
     (3, 256, 0, 1, 256)],
]

UNITS = [
    [(0, 0, 4), (0, 4, 4), (0, 8, 4), (0, 12, 4)],
    [(1, 0, 4), (1, 4, 4), (2, 0, 4), (3, 0, 2)],
]
UOFF = [0, 2048]


DEBUG_TAPS = []
DEBUG_BI0_ONLY = False


def _gperm():
    t = np.arange(T)
    return np.concatenate([t[t % 2 == 1], t[t % 4 == 2],
                           t[t % 8 == 4], t[t % 8 == 0]])


class _Op:
    __slots__ = ("eng", "waits", "fn", "needed", "val", "is_dma", "sem",
                 "lane")

    def __init__(self, eng, waits, fn, is_dma, lane):
        self.eng, self.waits, self.fn = eng, waits, fn
        self.needed, self.val, self.is_dma = False, None, is_dma
        self.sem = None
        self.lane = lane


class Builder:
    def __init__(self, nc):
        self.nc = nc
        self.ops = []

    def emit(self, eng, fn, waits=(), is_dma=False, lane=None):
        ws = tuple(w for w in waits if w is not None)
        for w in ws:
            assert isinstance(w, _Op)
        op = _Op(eng, ws, fn, is_dma, lane)
        self.ops.append(op)
        return op

    def dma(self, eng, out, in_, waits=(), lane="g"):
        return self.emit(eng, lambda e: e.dma_start(out=out, in_=in_),
                         waits, is_dma=True, lane=lane)


def _sl(ap, st0, stp, cnt):
    return ap[:, st0:st0 + stp * (cnt - 1) + 1:stp]


def build(dump=()):
    nc = bass.Bass("TRN2", target_bir_lowering=False, debug=False)
    b = Builder(nc)

    def din(name, shape, dt):
        return nc.dram_tensor(name, list(shape), dt, kind="ExternalInput").ap()

    x_d = din("x", (BPC, D, T), F32)
    xg_d = din("xg", (BPC, D, T), F32)
    wq_d = din("wq_big", (128, 4096), F32)
    wk_d = din("wk_big", (128, 4096), F32)
    wv_d = din("wv_big", (128, 4096), F32)
    wo_d = din("wo_big", (128, 4096), F32)
    bo_d = din("bo_t", (128, 8), F32)
    fb_d = din("fb_t", (128, 8), F32)
    wrt_d = din("wrt", (D, D), F32)
    wit_d = din("wit", (D, D), F32)
    c_d = din("cdft", (D, D), BF16)
    s_d = din("sdft", (D, D), BF16)
    sn_d = din("sndft", (D, D), BF16)
    mbt_d = din("mbt", (4, 128, 128), BF16)
    id_d = din("identb", (128, 128), BF16)
    out_d = nc.dram_tensor("out", [BPC, D, T], F32, kind="ExternalOutput").ap()

    def dbg_out(name, shape, dt=F32):
        return nc.dram_tensor("dbg_" + name, list(shape), dt,
                              kind="ExternalOutput").ap()

    import contextlib
    stack = contextlib.ExitStack()
    sb = lambda name, shape, dt: stack.enter_context(
        nc.sbuf_tensor(name, list(shape), dt))
    psum = lambda name: stack.enter_context(
        nc.psum_tensor(name, [128, 512], F32))

    with stack:
        xb = [sb(f"xb{c}", [128, T], BF16) for c in range(8)]
        xn = [sb(f"xn{c}", [128, T], BF16) for c in range(8)]
        lnstat = sb("lnstat", [128, 4096], BF16)
        mu_bf = lnstat[:, 0:2048]
        invs_bf = lnstat[:, 2048:4096]
        e_all = lnstat[:, 0:3840]
        xfp = [sb(f"xfp{i}", [128, 1024], F32) for i in range(4)]
        xsq_t = [sb(f"xsq{i}", [128, 512], BF16) for i in range(2)]
        st_a = sb("st_a", [128, 512], F32)
        st_b = sb("st_b", [128, 512], F32)
        st_c = sb("st_c", [128, 512], F32)
        ones = sb("ones", [128, 128], BF16)
        czc = sb("czc", [128, 2], F32)
        wq_all = sb("wq_all", [128, 4096], BF16)
        wk_all = sb("wk_all", [128, 4096], BF16)
        wv_all = sb("wv_all", [128, 4096], BF16)
        wo_all = sb("wo_all", [128, 4096], BF16)
        fb_sb = sb("fb_sb", [128, 8], F32)
        bo_sb = sb("bo_sb", [128, 8], F32)
        mbt_sb = [sb(f"mbt{h}", [128, 128], BF16) for h in range(4)]
        identb = sb("identb_sb", [128, 128], BF16)
        m_sb = [sb(f"m{c}", [128, 1024], BF16) for c in range(8)]
        q_all = sb("q_all", [128, 3840], BF16)
        k_all = sb("k_all", [128, 3840], BF16)
        v_tm = sb("v_tm", [128, 3840], BF16)
        at_all = sb("at_all", [128, 3840], BF16)
        v_pm = at_all[:, 0:2048]
        rs_sb = sb("rs_sb", [128, 32], F32)
        rsi_sb = sb("rsi_sb", [128, 32], F32)
        y_sb = sb("y_sb", [128, 6144], BF16)
        out_sb = [sb(f"os{i}", [128, 512], F32) for i in range(4)]
        xg_sb = [sb(f"xgs{i}", [128, 512], F32) for i in range(4)]
        ps = [psum(f"ps{i}") for i in range(8)]

        chunk = lambda t, j: t[:, j * 1024:(j + 1) * 1024]
        c_sb = [chunk(xn[e // 2], e % 2) for e in range(8)]
        sneg_sb = [chunk(xb[e // 2], e % 2) for e in range(8)]
        s_sb = [chunk(xn[4 + e // 2], e % 2) for e in range(8)]
        wrT_b = [chunk(xb[4 + e // 2], e % 2) for e in range(8)]
        wiT_b = [chunk(q_all, 0), chunk(q_all, 1),
                 chunk(k_all, 0), chunk(k_all, 1),
                 chunk(v_tm, 0), chunk(v_tm, 1),
                 chunk(at_all, 0), chunk(at_all, 1)]
        br_ch = [chunk(wq_all, j) for j in range(4)] + \
                [chunk(wk_all, j) for j in range(4)]
        bi_ch = [chunk(wv_all, j) for j in range(4)] + \
                [chunk(wo_all, j) for j in range(4)]

        bank_w = [None] * 8
        bank_r = [None] * 8

        def mm_group(bank, mms, waits=()):
            ws0 = [w for w in waits if w is not None]
            if bank_r[bank] is not None:
                ws0.append(bank_r[bank])
            ev = None
            n = len(mms)
            for i, mm in enumerate(mms):
                w = list(ws0) if i == 0 else []
                if mm[0] == "T":
                    _, src, idt = mm
                    ev = b.emit("tensor",
                                lambda e, src=src, idt=idt, bank=bank:
                                e.transpose(
                                    ps[bank][:, :].bitcast(BF16)[:, 0:128],
                                    src, idt),
                                waits=tuple(w))
                    continue
                lh, rh, o = mm[0], mm[1], mm[2]
                if len(mm) > 3 and mm[3] is not None:
                    w.append(mm[3])
                ev = b.emit("tensor",
                            lambda e, lh=lh, rh=rh, o=o, s0=(i == 0),
                            s1=(i == n - 1):
                            e.matmul(o, lh, rh, start=s0, stop=s1),
                            waits=tuple(w))
            bank_w[bank] = ev
            return ev

        def reader(eng, fn, bank, waits=()):
            ev = b.emit(eng, fn, waits=tuple(waits) + (bank_w[bank],))
            bank_r[bank] = ev
            return ev

        ones_ev = b.emit("vector", lambda e: e.memset(ones[:, :], 1.0))
        b.emit("vector", lambda e: e.memset(czc[:, 0:1], 0.0))
        cz_ev = b.emit("vector", lambda e: e.memset(czc[:, 1:2], EPS))
        nc.const_aps.register(F32, 0.0, czc[:, 0:1])
        nc.const_aps.register(F32, EPS, czc[:, 1:2])

        gl_last = [None] * 4
        c_ld, s_ld, sn_ld = [None] * 8, [None] * 8, [None] * 8
        for e in range(8):
            ln = e % 4
            c_ld[e] = b.dma("gpsimd", c_sb[e], c_d[e * 128:(e + 1) * 128, :],
                            waits=(gl_last[ln],), lane=f"gc{ln}")
            gl_last[ln] = c_ld[e]
        for e in range(8):
            ln = e % 4
            sn_ld[e] = b.dma("gpsimd", sneg_sb[e],
                             sn_d[e * 128:(e + 1) * 128, :],
                             waits=(gl_last[ln],), lane=f"gc{ln}")
            gl_last[ln] = sn_ld[e]
        for e in range(8):
            ln = e % 4
            s_ld[e] = b.dma("gpsimd", s_sb[e], s_d[e * 128:(e + 1) * 128, :],
                            waits=(gl_last[ln],), lane=f"gc{ln}")
            gl_last[ln] = s_ld[e]
        mbt_ld = [None] * 4
        for h in range(4):
            mbt_ld[h] = b.dma("gpsimd", mbt_sb[h][:, :], mbt_d[h],
                              waits=(gl_last[h % 4],), lane=f"gc{h % 4}")
            gl_last[h % 4] = mbt_ld[h]
        small_ready = b.dma("gpsimd", identb[:, :], id_d[:, :],
                            waits=(gl_last[0],), lane="gc0")
        gl_last[0] = small_ready
        b.dma("gpsimd", fb_sb[:, :], fb_d[:, :], waits=(gl_last[1],),
              lane="gc1")
        gl_last[1] = b.ops[-1]
        b.dma("gpsimd", bo_sb[:, :], bo_d[:, :], waits=(gl_last[2],),
              lane="gc2")
        gl_last[2] = b.ops[-1]
        sm_fb_bo = (small_ready, gl_last[1], gl_last[2])

        xfp_last = [None] * 4
        wrT_cast, wiT_cast = [None] * 8, [None] * 8
        for e in range(8):
            i = e % 4
            ld = b.dma("sync", xfp[i][:, :], wrt_d[e * 128:(e + 1) * 128, :],
                       waits=(xfp_last[i],), lane=f"xf{i}")
            wrT_cast[e] = b.emit("vector",
                                 lambda en, d=wrT_b[e], s_=xfp[i]:
                                 en.tensor_copy(d, s_[:, :]), waits=(ld,))
            xfp_last[i] = wrT_cast[e]
        for e in range(8):
            i = e % 4
            ld = b.dma("sync", xfp[i][:, :], wit_d[e * 128:(e + 1) * 128, :],
                       waits=(xfp_last[i],), lane=f"xf{i}")
            wiT_cast[e] = b.emit("scalar",
                                 lambda en, d=wiT_b[e], s_=xfp[i]:
                                 en.activation(d, s_[:, :], AF.Copy,
                                               bias=0.0, scale=1.0),
                                 waits=(ld, cz_ev))
            xfp_last[i] = wiT_cast[e]

        br_ev = [None] * 8
        bi_ev = [None] * 8
        ev_par = 0

        def m_wave(kind, half):
            nonlocal ev_par
            groups = [(rc, hf) for rc in range(half * 4, half * 4 + 4)
                      for hf in range(2)]
            for ec in range(8):
                for gi, (rc, hf) in enumerate(groups):
                    if kind == "A":
                        lh, rh = wrT_b[ec][:, rc * 128:(rc + 1) * 128], \
                            c_sb[ec][:, hf * 512:(hf + 1) * 512]
                        w = (wrT_cast[ec], c_ld[ec])
                    elif kind == "A2":
                        lh, rh = wiT_b[ec][:, rc * 128:(rc + 1) * 128], \
                            sneg_sb[ec][:, hf * 512:(hf + 1) * 512]
                        w = (wiT_cast[ec], sn_ld[ec])
                    elif kind == "B":
                        lh, rh = wiT_b[ec][:, rc * 128:(rc + 1) * 128], \
                            c_sb[ec][:, hf * 512:(hf + 1) * 512]
                        w = (wiT_cast[ec], c_ld[ec])
                    elif kind == "B2":
                        lh, rh = wrT_b[ec][:, rc * 128:(rc + 1) * 128], \
                            s_sb[ec][:, hf * 512:(hf + 1) * 512]
                        w = (wrT_cast[ec], s_ld[ec])
                    elif kind == "C":
                        lh, rh = c_sb[ec][:, rc * 128:(rc + 1) * 128], \
                            br_ch[ec][:, hf * 512:(hf + 1) * 512]
                        w = (c_ld[ec], br_ev[ec])
                    else:
                        lh, rh = s_sb[ec][:, rc * 128:(rc + 1) * 128], \
                            bi_ch[ec][:, hf * 512:(hf + 1) * 512]
                        w = (s_ld[ec], bi_ev[ec])
                    bank = gi
                    first = (ec == 0 and kind in ("A", "B", "C"))
                    ws = list(w) if gi == 0 else []
                    if first and bank_r[bank] is not None:
                        ws.append(bank_r[bank])
                    ev = b.emit("tensor",
                                lambda e, lh=lh, rh=rh, bank=bank,
                                s0=first,
                                s1=(ec == 7 and kind in ("A2", "B2", "C2")):
                                e.matmul(ps[bank][:, :], lh, rh,
                                         start=s0, stop=s1),
                                waits=tuple(ws))
                    bank_w[bank] = ev
            if kind in ("A", "B", "C"):
                return
            for gi, (rc, hf) in enumerate(groups):
                if kind == "A2":
                    dst = br_ch[rc][:, hf * 512:(hf + 1) * 512]
                elif kind == "B2":
                    dst = bi_ch[rc][:, hf * 512:(hf + 1) * 512]
                else:
                    dst = m_sb[rc][:, hf * 512:(hf + 1) * 512]
                eng = "vector" if ev_par % 2 == 0 else "scalar"
                ev_par += 1
                if eng == "vector":
                    ev = reader(eng, lambda e, d=dst, bank=gi:
                                e.tensor_copy(d, ps[bank][:, :]), gi)
                else:
                    ev = reader(eng, lambda e, d=dst, bank=gi:
                                e.activation(d, ps[bank][:, :], AF.Copy,
                                             bias=0.0, scale=1.0), gi)
                if kind == "A2":
                    br_ev[rc] = ev
                elif kind == "B2":
                    bi_ev[rc] = ev

        for half in range(2):
            m_wave("A", half)
            m_wave("A2", half)
        for half in range(2):
            m_wave("B", half)
            m_wave("B2", half)
        b2_done = next(op for op in reversed(b.ops) if op.eng == "tensor")
        for half in range(2):
            m_wave("C", half)
            m_wave("C2", half)
        m_mm_done = next(op for op in reversed(b.ops) if op.eng == "tensor")
        m_ready_v = next(op for op in reversed(b.ops) if op.eng == "vector")
        m_ready_s = next(op for op in reversed(b.ops) if op.eng == "scalar")

        xb_guard = [[b2_done] for _ in range(8)]
        pend_cast = {}

        def emit_prefetch(nbi, dc):
            evs = []
            for hf in range(2):
                i = (dc * 2 + hf) % 4
                ld = b.dma("sync", xfp[i][:, :],
                           x_d[nbi, dc * 128:(dc + 1) * 128,
                               hf * 1024:(hf + 1) * 1024],
                           waits=(xfp_last[i],), lane=f"xf{i}")
                eng = "vector" if dc % 2 else "scalar"
                w = (ld,) + tuple(xb_guard[dc])
                if eng == "vector":
                    ev = b.emit(eng, lambda e, dc=dc, hf=hf, s_=xfp[i]:
                                e.tensor_copy(
                                    xb[dc][:, hf * 1024:(hf + 1) * 1024],
                                    s_[:, :]), waits=w)
                else:
                    ev = b.emit(eng, lambda e, dc=dc, hf=hf, s_=xfp[i]:
                                e.activation(
                                    xb[dc][:, hf * 1024:(hf + 1) * 1024],
                                    s_[:, :], AF.Copy, bias=0.0,
                                    scale=1.0), waits=w + (cz_ev,))
                xfp_last[i] = ev
                evs.append(ev)
            pend_cast[dc] = evs[-1]

        for dc in range(8):
            emit_prefetch(0, dc)

        w_cast = {}
        for nmi, (src, dst, scale) in enumerate(
                ((wq_d, wq_all, NORM), (wk_d, wk_all, 1.0),
                 (wv_d, wv_all, 1.0), (wo_d, wo_all, 1.0))):
            for j in range(4):
                i = (nmi * 4 + j) % 4
                ld = b.dma("sync", xfp[i][:, :], src[:, j * 1024:(j + 1) * 1024],
                           waits=(xfp_last[i],), lane=f"xf{i}")
                if scale == 1.0:
                    cast = b.emit("vector",
                                  lambda e, d=chunk(dst, j), s_=xfp[i]:
                                  e.tensor_copy(d, s_[:, :]),
                                  waits=(ld, m_mm_done))
                else:
                    cast = b.emit("scalar",
                                  lambda e, d=chunk(dst, j), s_=xfp[i],
                                  sc=scale:
                                  e.activation(d, s_[:, :], AF.Copy,
                                               bias=0.0, scale=sc),
                                  waits=(ld, m_mm_done))
                xfp_last[i] = cast
                w_cast[(nmi, j)] = cast
        wq_ready = lambda h: w_cast[(0, h)]
        wk_ready = lambda h: w_cast[(1, h)]
        wv_ready = lambda h: w_cast[(2, h)]
        wo_ready = w_cast[(3, 3)]

        xn_guard = [[m_mm_done] for _ in range(8)]
        lnstat_guard = [m_mm_done]
        xg_guard = [[None] for _ in range(4)]
        out_guard = [None] * 4
        ysl_guard = [None] * 12
        sq_mm = [None, None]
        last_stores = [None]
        apply_evs = [None] * 8

        def emit_stats(bi):
            cast_ev = [None] * 8
            for dc in range(8):
                cast_ev[dc] = pend_cast.pop(dc)
            sx_last = None
            invs_done = None
            prev_rsq = None
            for tt in range(4):
                sl = slice(tt * 512, (tt + 1) * 512)
                mms = [(ones[:, :], xb[dc][:, sl], ps[6][:, :],
                        cast_ev[dc] if tt == 0 else None) for dc in range(8)]
                sx_last = mm_group(6, mms, waits=(ones_ev,))
                ev = None
                for dc in range(8):
                    j = dc % 2
                    sq = b.emit("vector",
                                lambda e, j=j, dc=dc, sl=sl:
                                e.tensor_mul(xsq_t[j][:, :], xb[dc][:, sl],
                                             xb[dc][:, sl]),
                                waits=(cast_ev[dc], sq_mm[j]))
                    w = [bank_r[7], ones_ev] if dc == 0 else []
                    w.append(sq)
                    ev = b.emit("tensor",
                                lambda e, j=j, s0=(dc == 0), s1=(dc == 7):
                                e.matmul(ps[7][:, :], ones[:, :],
                                         xsq_t[j][:, :], start=s0, stop=s1),
                                waits=tuple(x for x in w if x is not None))
                    sq_mm[j] = ev
                bank_w[7] = ev
                mu_ev = reader("vector", lambda e: e.tensor_scalar_mul(
                    st_a[:, :], ps[6][:, :], 1.0 / D), 6,
                    waits=(lnstat_guard[0] if tt == 0 else None,))
                b.emit("vector", lambda e, sl=sl: e.tensor_copy(
                    mu_bf[:, sl], st_a[:, :]))
                musq = b.emit("vector", lambda e: e.tensor_mul(
                    st_b[:, :], st_a[:, :], st_a[:, :]))
                var = reader("vector", lambda e: e.scalar_tensor_tensor(
                    st_c[:, :], ps[7][:, :], 1.0 / D, st_b[:, :],
                    OP.mult, OP.subtract), 7, waits=(musq,))
                sstd = b.emit("scalar", lambda e: e.activation(
                    st_b[:, :], st_c[:, :], AF.Sqrt, bias=czc[:, 1:2],
                    scale=1.0), waits=(var, prev_rsq))
                prev_rsq = sstd
                b.emit("vector", lambda e: e.reciprocal(
                    st_a[:, :], st_b[:, :]), waits=(sstd,))
                invs_done = b.emit("vector", lambda e, sl=sl: e.tensor_copy(
                    invs_bf[:, sl], st_a[:, :]))
            return invs_done, sx_last

        def emit_apply(stats_done, sx_last):
            apply_tt = [None] * 4
            for tt in range(4):
                sl = slice(tt * 512, (tt + 1) * 512)
                ev = None
                for dc in range(8):
                    w = ((stats_done,) + tuple(xn_guard[dc])) if tt == 0 \
                        else ()
                    b.emit("vector", lambda e, dc=dc, sl=sl: e.tensor_sub(
                        xn[dc][:, sl], xb[dc][:, sl], mu_bf[:, sl]), waits=w)
                    ev = b.emit("vector", lambda e, dc=dc, sl=sl:
                                e.tensor_mul(xn[dc][:, sl], xn[dc][:, sl],
                                             invs_bf[:, sl]))
                    if tt == 3:
                        apply_evs[dc] = ev
                apply_tt[tt] = ev
            for dc in range(8):
                xb_guard[dc] = [apply_evs[dc], sx_last]
            lnstat_guard[0] = apply_evs[7]
            return apply_tt

        pcount = [0]

        def emit_proj(h, apply_tt, vpm_guard):
            L, dil = LS[h], DILS[h]
            qk_all = None
            v_all = None
            ngrp = (L + 3) // 4
            for (wt, wrdy, dst, eng) in (
                    (wq_all, wq_ready(h), q_all, "scalar"),
                    (wk_all, wk_ready(h), k_all, "scalar"),
                    (wv_all, wv_ready(h), v_pm, "vector")):
                for g in range(ngrp):
                    nb = min(4, L - g * 4)
                    bank = 4 + pcount[0] % 2
                    pcount[0] += 1
                    mms = []
                    for dc in range(8):
                        rhs = (xn[dc][:, :]
                               .rearrange("p (n bb) -> p n bb", bb=dil * 128)
                               [:, g * 4:g * 4 + nb, 0:128])
                        lh = wt[:, (h * 8 + dc) * 128:(h * 8 + dc + 1) * 128]
                        mms.append((lh, rhs, ps[bank][:, 0:nb * 128]))
                    tt_need = ((g * 4 + nb - 1) * dil * 128 + 127) // 512
                    mm_group(bank, mms, waits=(apply_tt[tt_need], wrdy))
                    src = (ps[bank][:, 0:nb * 128]
                           .rearrange("p (n s) -> p n s", s=128))
                    base = dst[:, 0:128 * L] if dst is v_pm else \
                        dst[:, HOFF[h]:HOFF[h] + 128 * L]
                    dstv = (base.rearrange("p (s n) -> p s n", n=L)
                            [:, :, g * 4:g * 4 + nb].transpose([0, 2, 1]))
                    evw = (vpm_guard,) if dst is v_pm else ()
                    ev = reader(eng,
                                (lambda e, dstv=dstv, src=src:
                                 e.copy(dstv, src)) if eng == "scalar"
                                else (lambda e, dstv=dstv, src=src:
                                      e.tensor_copy(dstv, src)), bank,
                                waits=evw)
                    if dst is v_pm:
                        v_all = ev
                    else:
                        qk_all = ev
            vt_last = None
            tr_last = None
            for sbk in range(L):
                cols = slice(sbk * 128, (sbk + 1) * 128)
                bank = 4 + pcount[0] % 2
                pcount[0] += 1
                tr_last = mm_group(bank, [("T", v_pm[:, cols], identb[:, :])],
                                   waits=(v_all, small_ready))
                vt_last = reader(
                    "vector",
                    lambda e, sbk=sbk, h=h, bank=bank:
                    e.tensor_copy(v_tm[:, HOFF[h] + sbk * 128:
                                       HOFF[h] + (sbk + 1) * 128],
                                  ps[bank][:, :].bitcast(BF16)[:, 0:128]),
                    bank)
            return qk_all, vt_last, tr_last

        def emit_chain_scores(u, qk_evs):
            sc_done = [None] * 4
            for bk, pieces in enumerate(_unit_banks(u)):
                bank = 4 + bk
                for (h, sbk, col) in pieces:
                    qc = slice(HOFF[h] + sbk * 128, HOFF[h] + (sbk + 1) * 128)
                    mms = [(mbt_sb[h][:, :], identb[:, :],
                            ps[bank][:, col:col + 128]),
                           (k_all[:, qc], q_all[:, qc],
                            ps[bank][:, col:col + 128])]
                    sc_done[bk] = mm_group(
                        bank, mms,
                        waits=(qk_evs[h], small_ready, mbt_ld[h]))
            return sc_done

        def _unit_banks(u):
            out = [[] for _ in range(4)]
            bb = 0
            for (h, s0, nb) in UNITS[u]:
                for j in range(nb):
                    out[bb // 4].append((h, s0 + j, (bb % 4) * 128))
                    bb += 1
            return out

        def emit_chain_mid(u, sc_done, apply_last):
            banks = _unit_banks(u)
            ncols = [len(p) * 128 for p in banks]
            exp_evs = []
            for bk in range(4):
                bank = 4 + bk
                off = UOFF[u] + bk * 512
                ev = reader("scalar",
                            lambda e, bank=bank, off=off, nc_=ncols[bk]:
                            e.activation(e_all[:, off:off + nc_],
                                         ps[bank][:, 0:nc_], AF.Exp,
                                         bias=0.0, scale=1.0), bank,
                            waits=(apply_last,))
                exp_evs.append(ev)
            nblk = sum(len(p) for p in banks)
            r0 = 16 * u
            red = b.emit("vector",
                         lambda e, u=u, nblk=nblk, r0=r0:
                         e.reduce_sum(
                             rs_sb[:, r0:r0 + nblk]
                             .rearrange("p (n o) -> p n o", o=1),
                             e_all[:, UOFF[u]:UOFF[u] + nblk * 128]
                             .rearrange("p (n s) -> p n s", s=128),
                             AX.X), waits=tuple(exp_evs))
            rec = None
            for c0 in range(0, nblk, 8):
                cw = min(8, nblk - c0)
                rec = b.emit("vector",
                             lambda e, a=r0 + c0, w=cw:
                             e.reciprocal(rsi_sb[:, a:a + w],
                                          rs_sb[:, a:a + w]),
                             waits=(red,) if c0 == 0 else ())
            scale_evs = [[None] * len(p) for p in banks]
            bb = 0
            for bk in range(4):
                for j in range(len(banks[bk])):
                    off = UOFF[u] + bb * 128
                    scale_evs[bk][j] = b.emit(
                        "scalar",
                        lambda e, off=off, r=r0 + bb:
                        e.activation(e_all[:, off:off + 128],
                                     e_all[:, off:off + 128], AF.Copy,
                                     bias=0.0, scale=rsi_sb[:, r:r + 1]),
                        waits=(rec,))
                    bb += 1
            return scale_evs

        def emit_chain_av(u, scale_evs, vt_done):
            att_done = None
            banks = _unit_banks(u)
            for bk in range(4):
                bank = 4 + bk
                mms = []
                for j, (h, sbk, col) in enumerate(banks[bk]):
                    vc = slice(HOFF[h] + sbk * 128, HOFF[h] + (sbk + 1) * 128)
                    eoff = UOFF[u] + bk * 512 + col
                    mms.append((v_tm[:, vc], e_all[:, eoff:eoff + 128],
                                ps[bank][:, col:col + 128],
                                scale_evs[bk][j]))
                mm_group(bank, mms, waits=(vt_done,))
                nc_ = len(banks[bk]) * 128
                off = UOFF[u] + bk * 512
                att_done = reader("scalar",
                                  lambda e, off=off, nc_=nc_, bank=bank:
                                  e.copy(at_all[:, off:off + nc_],
                                         ps[bank][:, 0:nc_]), bank)
            return att_done

        def emit_y(bi, ui, xn_ready):
            jt, oc = ui % 4, ui // 4
            bank = ui % 2
            slot = ui % 12
            mms = []
            for dc in range(8):
                for (pc, st0, stp, cnt) in JT_FFN[jt]:
                    mms.append((m_sb[dc][:, oc * 128:(oc + 1) * 128],
                                _sl(xn[dc][:, :], st0, stp, cnt),
                                ps[bank][:, pc:pc + cnt]))
            gw = ()
            if ui == 0:
                gw = (xn_ready,) + ((m_ready_v, m_ready_s) if bi == 0 else ())
            ylast = mm_group(bank, mms, waits=gw)
            yev = reader("vector",
                         lambda e, slot=slot, oc=oc, bank=bank:
                         e.tensor_scalar_add(
                             y_sb[:, slot * 512:(slot + 1) * 512],
                             ps[bank][:, :], fb_sb[:, oc:oc + 1]), bank,
                         waits=(sm_fb_bo[1] if (bi == 0 and ui == 0) else None,
                                ysl_guard[slot]))
            return ylast, yev

        def emit_silu(gi, yev_last):
            third = gi % 3
            ev = b.emit("scalar",
                        lambda e, third=third:
                        e.activation(y_sb[:, third * 2048:(third + 1) * 2048],
                                     y_sb[:, third * 2048:(third + 1) * 2048],
                                     AF.Silu, bias=0.0, scale=1.0),
                        waits=(yev_last,))
            return ev

        def emit_s(bi, si, silu_ev, att_evs):
            jt, oc = si % 4, si // 4
            bank = 2 + si % 2 if si < 4 else (2, 3, 4, 5)[si % 4]
            slot = si % 12
            ob = si % 4
            xgb = si % 4
            xg_ld = b.dma("gpsimd", xg_sb[xgb][:, :],
                          xg_d[bi, oc * 128:(oc + 1) * 128,
                               jt * 512:(jt + 1) * 512],
                          waits=tuple(xg_guard[xgb]), lane=f"xg{xgb}")
            mms = []
            opw = [att_evs[0] if jt < 2 else att_evs[3]]
            if bi == 0 and si == 0:
                opw += [wo_ready, sm_fb_bo[2]]
            for (h, pc, st0, stp, cnt) in JT_OP[jt]:
                mms.append((wo_all[:, h * 1024 + oc * 128:
                                   h * 1024 + (oc + 1) * 128],
                            _sl(at_all[:, HOFF[h]:HOFF[h] + 128 * LS[h]],
                                st0, stp, cnt),
                            ps[bank][:, pc:pc + cnt]))
            mm_group(bank, mms, waits=tuple(opw))
            comb = reader("vector",
                          lambda e, slot=slot, oc=oc, bank=bank, ob=ob:
                          e.scalar_tensor_tensor(
                              out_sb[ob][:, :], ps[bank][:, :],
                              bo_sb[:, oc:oc + 1],
                              y_sb[:, slot * 512:(slot + 1) * 512],
                              OP.add, OP.mult), bank,
                          waits=(silu_ev, out_guard[ob]))
            ysl_guard[slot] = comb
            add = b.emit("vector",
                         lambda e, ob=ob, xgb=xgb:
                         e.tensor_add(out_sb[ob][:, :], out_sb[ob][:, :],
                                      xg_sb[xgb][:, :]),
                         waits=(xg_ld,))
            xg_guard[xgb] = [add]
            stv = b.dma("sync", out_d[bi, oc * 128:(oc + 1) * 128,
                                      jt * 512:(jt + 1) * 512],
                        out_sb[ob][:, :], waits=(add,),
                        lane=f"st{ob}")
            out_guard[ob] = stv
            last_stores[0] = stv
            return stv

        if "msb" in dump:
            mo = dbg_out("msb", (8, 128, 1024), BF16)
            for dc in range(8):
                b.dma("gpsimd", mo[dc], m_sb[dc][:, :],
                      waits=(m_ready_v, m_ready_s), lane="gt")
        nbatch = 1 if DEBUG_BI0_ONLY else BPC
        stats_state = None
        for bi in range(nbatch):
            if stats_state is None:
                stats_state = emit_stats(bi)
            stats_done, sx_last = stats_state
            stats_state = None
            apply_tt = emit_apply(stats_done, sx_last)
            xn_ready = apply_tt[3]
            if "xn" in dump and bi == 0:
                xo = dbg_out("xn", (8, 128, T), BF16)
                for dc in range(8):
                    b.dma("gpsimd", xo[dc], xn[dc][:, :],
                          waits=(xn_ready,), lane="gt")
            qk_evs = [None] * 4
            vt_evs = [None] * 4
            vpm_guard = small_ready
            for h in range(4):
                qk_evs[h], vt_evs[h], vpm_guard = \
                    emit_proj(h, apply_tt, vpm_guard)
            if "qkv" in dump and bi == 0:
                qo = dbg_out("qk", (2, 128, 3840), BF16)
                b.dma("gpsimd", qo[0], q_all[:, :], waits=(qk_evs[3],),
                      lane="gt")
                b.dma("gpsimd", qo[1], k_all[:, :], waits=(qk_evs[3],),
                      lane="gt")
                vo = dbg_out("vtm", (128, 3840), BF16)
                b.dma("gpsimd", vo, v_tm[:, :], waits=(vt_evs[3],), lane="gt")
            att_evs = [None] * 4
            yi = si = 0
            yevs = {}
            silu_evs = {}
            have_att1 = [False]

            def run_y(n):
                nonlocal yi
                for _ in range(n):
                    if yi >= 32 or yi - si >= 12:
                        return
                    _, yev = emit_y(bi, yi, xn_ready)
                    yevs[yi] = yev
                    yi += 1
                    if yi % 4 == 0:
                        gi = yi // 4 - 1
                        silu_evs[gi] = emit_silu(gi, yevs[4 * gi + 3])

            def run_s(n):
                nonlocal si
                for _ in range(n):
                    if si >= 32 or (si // 4 + 1) * 4 > yi:
                        return
                    if si % 4 >= 2 and not have_att1[0]:
                        return
                    emit_s(bi, si, silu_evs[si // 4], att_evs)
                    si += 1

            sc = emit_chain_scores(0, qk_evs)
            run_y(4)
            scale0 = emit_chain_mid(0, sc, xn_ready)
            run_y(2)
            att_evs[0] = emit_chain_av(0, scale0, vt_evs[0])
            run_s(2)
            run_y(3)
            sc = emit_chain_scores(1, qk_evs)
            run_y(1)
            scale1 = emit_chain_mid(1, sc, xn_ready)
            run_y(1)
            att1 = emit_chain_av(1, scale1, vt_evs[3])
            att_evs[1] = att_evs[2] = att_evs[3] = att1
            have_att1[0] = True
            e_free = att1
            if bi + 1 < BPC:
                for dc in range(8):
                    emit_prefetch(bi + 1, dc)
            if "chain" in dump and bi == 0:
                eo = dbg_out("eall", (128, 3840), BF16)
                tp = b.dma("gpsimd", eo, e_all, waits=(att1,), lane="gt")
                ro = dbg_out("rs", (2, 128, 32))
                b.dma("gpsimd", ro[0], rs_sb[:, :], waits=(att1,), lane="gt")
                b.dma("gpsimd", ro[1], rsi_sb[:, :], waits=(att1,), lane="gt")
                ao = dbg_out("atall", (128, 3840), BF16)
                b.dma("gpsimd", ao, at_all[:, :], waits=(att1,), lane="gt")
                e_free = tp
            lnstat_guard[0] = e_free
            while yi < 32 or si < 32:
                prog = yi + si
                run_y(2)
                run_s(2)
                if si >= 8 and stats_state is None and bi + 1 < nbatch:
                    stats_state = emit_stats(bi + 1)
                assert yi + si > prog, (yi, si)
            if stats_state is None and bi + 1 < nbatch:
                stats_state = emit_stats(bi + 1)
            for dc in range(8):
                xn_guard[dc] = [yevs[31]]

        b.emit("gpsimd", lambda e: e.engine_nop(),
               waits=tuple(s for s in last_stores if s is not None))

        for op in b.ops:
            if op.is_dma:
                op.needed = True
            for w in op.waits:
                w.needed = True
        ctr = {}
        for op in b.ops:
            if not op.needed:
                continue
            key = ("dma_" + op.lane) if op.is_dma else op.eng
            op.sem = key
            ctr[key] = ctr.get(key, 0) + (16 if op.is_dma else 1)
            op.val = ctr[key]
        sems = {key: stack.enter_context(nc.semaphore("sem_" + key))
                for key in sorted(ctr)}
        with nc.Block() as block:
            def mk(name):
                def run(eng_obj):
                    for op in b.ops:
                        if op.eng != name:
                            continue
                        for w in op.waits:
                            if (not w.is_dma) and w.eng == name:
                                continue
                            eng_obj.wait_ge(sems[w.sem], w.val)
                        inst = op.fn(eng_obj)
                        if op.needed:
                            inst.then_inc(sems[op.sem],
                                          16 if op.is_dma else 1)
                return run
            block.sync(mk("sync"))
            block.tensor(mk("tensor"))
            block.vector(mk("vector"))
            block.scalar(mk("scalar"))
            block.gpsimd(mk("gpsimd"))
    return nc


def _stage_inputs(L0_attn_x, L0_attn_Wk, L0_attn_Wq, L0_attn_Wv,
                  L0_attn_W_out, L0_attn_b_out, L0_ffn_w_freq_real,
                  L0_ffn_w_freq_imag, L0_ffn_bias):
    perm = _gperm()
    f32 = np.float32
    idx = np.arange(D)
    ang = (2.0 * np.pi / D) * ((idx[:, None] * idx[None, :]) % D)
    Cm = (np.cos(ang) / 32.0).astype(ml_dtypes.bfloat16)
    Sm = (np.sin(ang) / 32.0).astype(ml_dtypes.bfloat16)
    Snm = (-np.sin(ang) / 32.0).astype(ml_dtypes.bfloat16)
    mbt = np.zeros((4, 128, 128), np.float32)
    for h in range(4):
        L = LS[h]
        p = np.arange(128)
        sl_p, n = p // L, p % L
        m = p % L
        allowed = (sl_p[:, None] == sl_p[None, :]) & \
                  (m[None, :] <= n[:, None])
        mb = np.where(allowed, 0.0, NEG).astype(f32)
        mbt[h] = mb.T
    mbt = mbt.astype(ml_dtypes.bfloat16)

    def pack_qkv(W, scalefold=False):
        big = np.empty((128, 4096), f32)
        for h in range(4):
            wt = np.asarray(W[h], f32).T
            for dc in range(8):
                big[:, (h * 8 + dc) * 128:(h * 8 + dc + 1) * 128] = \
                    wt[dc * 128:(dc + 1) * 128, :]
        return np.ascontiguousarray(big)

    wq = pack_qkv(L0_attn_Wq)
    wk = pack_qkv(L0_attn_Wk)
    wv = pack_qkv(L0_attn_Wv)
    wo_t = np.asarray(L0_attn_W_out, f32).T
    wo = np.empty((128, 4096), f32)
    for h in range(4):
        wo[:, h * 1024:(h + 1) * 1024] = wo_t[h * 128:(h + 1) * 128, :]
    wrt = np.ascontiguousarray(np.asarray(L0_ffn_w_freq_real, f32).T)
    wit = np.ascontiguousarray(np.asarray(L0_ffn_w_freq_imag, f32).T)
    fbT = np.ascontiguousarray(np.asarray(L0_ffn_bias, f32).reshape(8, 128).T)
    boT = np.ascontiguousarray(
        np.asarray(L0_attn_b_out, f32).reshape(8, 128).T)
    shared = dict(wq_big=wq, wk_big=wk, wv_big=wv, wo_big=np.ascontiguousarray(wo),
                  bo_t=boT, fb_t=fbT, wrt=wrt, wit=wit,
                  cdft=Cm, sdft=Sm, sndft=Snm, mbt=mbt,
                  identb=np.eye(128, dtype=ml_dtypes.bfloat16))
    xs = np.asarray(L0_attn_x, f32)
    in_maps = []
    for c in range(NCORES):
        xc = xs[BPC * c:BPC * (c + 1)]
        m = dict(shared)
        m["x"] = np.ascontiguousarray(xc.transpose(0, 2, 1))
        m["xg"] = np.ascontiguousarray(xc[:, perm, :].transpose(0, 2, 1))
        in_maps.append(m)
    return in_maps, perm


def kernel(L0_attn_x, L0_attn_Wk, L0_attn_Wq, L0_attn_Wv, L0_attn_W_out,
           L0_attn_b_out, L0_ffn_w_freq_real, L0_ffn_w_freq_imag,
           L0_ffn_bias):
    in_maps, perm = _stage_inputs(
        L0_attn_x, L0_attn_Wk, L0_attn_Wq, L0_attn_Wv, L0_attn_W_out,
        L0_attn_b_out, L0_ffn_w_freq_real, L0_ffn_w_freq_imag, L0_ffn_bias)
    nc = build(dump=tuple(DEBUG_TAPS))
    res = run_bass_kernel_spmd(nc, in_maps, core_ids=list(range(NCORES)))
    out = np.empty((B_FULL, T, D), np.float32)
    for c in range(NCORES):
        o = np.asarray(res.results[c]["out"])
        out[BPC * c:BPC * (c + 1)][:, perm, :] = o.transpose(0, 2, 1)
    if DEBUG_TAPS:
        kernel.debug = [{k: np.asarray(v) for k, v in r.items()
                         if k.startswith("dbg_")} for r in res.results]
    return out


# version 09
# speedup vs baseline: 1.1027x; 1.0087x over previous
import math
import numpy as np
import ml_dtypes
import concourse.bass as bass
import concourse.mybir as mybir
from concourse.bass_utils import run_bass_kernel_spmd

F32 = mybir.dt.float32
BF16 = mybir.dt.bfloat16
AF = mybir.ActivationFunctionType
OP = mybir.AluOpType
AX = mybir.AxisListType

B_FULL, T, D = 32, 2048, 1024
NCORES = 8
BPC = B_FULL // NCORES
DILS = [1, 2, 4, 8]
LS = [16, 8, 4, 2]
HOFF = [0, 2048, 3072, 3584]
NBLK = 30
NORM = 1.0 / math.sqrt(128.0)
NEG = -1.0e9
EPS = 1e-5

JT_FFN = [
    [(0, 1, 2, 512)],
    [(0, 1025, 2, 512)],
    [(0, 2, 4, 512)],
    [(0, 4, 8, 256), (256, 0, 8, 256)],
]
JT_OP = [
    [(0, 0, 1, 2, 512)],
    [(0, 0, 1025, 2, 512)],
    [(0, 0, 2, 4, 512), (1, 0, 1, 2, 512)],
    [(0, 0, 4, 8, 256), (0, 256, 0, 8, 256),
     (1, 0, 2, 4, 256), (1, 256, 0, 4, 256),
     (2, 0, 1, 2, 256), (2, 256, 0, 2, 256),
     (3, 256, 0, 1, 256)],
]

UNITS = [
    [(0, 0, 4), (0, 4, 4), (0, 8, 4), (0, 12, 4)],
    [(1, 0, 4), (1, 4, 4), (2, 0, 4), (3, 0, 2)],
]
UOFF = [0, 2048]


DEBUG_TAPS = []
DEBUG_BI0_ONLY = False


def _gperm():
    t = np.arange(T)
    return np.concatenate([t[t % 2 == 1], t[t % 4 == 2],
                           t[t % 8 == 4], t[t % 8 == 0]])


class _Op:
    __slots__ = ("eng", "waits", "fn", "needed", "val", "is_dma", "sem",
                 "lane", "tag")

    def __init__(self, eng, waits, fn, is_dma, lane, tag=""):
        self.eng, self.waits, self.fn = eng, waits, fn
        self.needed, self.val, self.is_dma = False, None, is_dma
        self.sem = None
        self.lane = lane
        self.tag = tag


class Builder:
    def __init__(self, nc):
        self.nc = nc
        self.ops = []

    def emit(self, eng, fn, waits=(), is_dma=False, lane=None):
        ws = tuple(w for w in waits if w is not None)
        for w in ws:
            assert isinstance(w, _Op)
        import sys
        fr = sys._getframe(1)
        if fr.f_code.co_name == "dma":
            fr = fr.f_back
        op = _Op(eng, ws, fn, is_dma, lane, tag=f"L{fr.f_lineno}")
        self.ops.append(op)
        return op

    def dma(self, eng, out, in_, waits=(), lane="g"):
        return self.emit(eng, lambda e: e.dma_start(out=out, in_=in_),
                         waits, is_dma=True, lane=lane)


def _sl(ap, st0, stp, cnt):
    return ap[:, st0:st0 + stp * (cnt - 1) + 1:stp]


def build(dump=()):
    nc = bass.Bass("TRN2", target_bir_lowering=False, debug=False)
    b = Builder(nc)

    def din(name, shape, dt):
        return nc.dram_tensor(name, list(shape), dt, kind="ExternalInput").ap()

    x_d = din("x", (BPC, D, T), F32)
    xg_d = din("xg", (BPC, D, T), F32)
    wq_d = din("wq_big", (128, 4096), F32)
    wk_d = din("wk_big", (128, 4096), F32)
    wv_d = din("wv_big", (128, 4096), F32)
    wo_d = din("wo_big", (128, 4096), F32)
    bo_d = din("bo_t", (128, 8), F32)
    fb_d = din("fb_t", (128, 8), F32)
    wrt_d = din("wrt", (D, D), F32)
    wit_d = din("wit", (D, D), F32)
    c_d = din("cdft", (D, D), BF16)
    s_d = din("sdft", (D, D), BF16)
    sn_d = din("sndft", (D, D), BF16)
    mbt_d = din("mbt", (4, 128, 128), BF16)
    id_d = din("identb", (128, 128), BF16)
    out_d = nc.dram_tensor("out", [BPC, D, T], F32, kind="ExternalOutput").ap()

    def dbg_out(name, shape, dt=F32):
        return nc.dram_tensor("dbg_" + name, list(shape), dt,
                              kind="ExternalOutput").ap()

    import contextlib
    stack = contextlib.ExitStack()
    sb = lambda name, shape, dt: stack.enter_context(
        nc.sbuf_tensor(name, list(shape), dt))
    psum = lambda name: stack.enter_context(
        nc.psum_tensor(name, [128, 512], F32))

    with stack:
        xb = [sb(f"xb{c}", [128, T], BF16) for c in range(8)]
        xn = [sb(f"xn{c}", [128, T], BF16) for c in range(8)]
        lnstat = sb("lnstat", [128, 4096], BF16)
        mu_bf = lnstat[:, 0:2048]
        invs_bf = lnstat[:, 2048:4096]
        e_all = lnstat[:, 0:3840]
        xfp = [sb(f"xfp{i}", [128, 1024], F32) for i in range(4)]
        xsq_t = [sb(f"xsq{i}", [128, 512], BF16) for i in range(4)]
        st_a = sb("st_a", [128, 512], F32)
        st_b = sb("st_b", [128, 512], F32)
        st_c = sb("st_c", [128, 512], F32)
        ones = sb("ones", [128, 128], BF16)
        czc = sb("czc", [128, 2], F32)
        wq_all = sb("wq_all", [128, 4096], BF16)
        wk_all = sb("wk_all", [128, 4096], BF16)
        wv_all = sb("wv_all", [128, 4096], BF16)
        wo_all = sb("wo_all", [128, 4096], BF16)
        fb_sb = sb("fb_sb", [128, 8], F32)
        bo_sb = sb("bo_sb", [128, 8], F32)
        mbt_sb = [sb(f"mbt{h}", [128, 128], BF16) for h in range(4)]
        identb = sb("identb_sb", [128, 128], BF16)
        m_sb = [sb(f"m{c}", [128, 1024], BF16) for c in range(8)]
        q_all = sb("q_all", [128, 3840], BF16)
        k_all = sb("k_all", [128, 3840], BF16)
        v_tm = sb("v_tm", [128, 3840], BF16)
        at_all = sb("at_all", [128, 3840], BF16)
        v_pm = at_all[:, 0:2048]
        rs_sb = sb("rs_sb", [128, 32], F32)
        rsi_sb = sb("rsi_sb", [128, 32], F32)
        y_sb = sb("y_sb", [128, 6144], BF16)
        out_sb = [sb(f"os{i}", [128, 512], F32) for i in range(4)]
        xg_sb = [sb(f"xgs{i}", [128, 512], F32) for i in range(4)]
        ps = [psum(f"ps{i}") for i in range(8)]

        chunk = lambda t, j: t[:, j * 1024:(j + 1) * 1024]
        c_sb = [chunk(xn[e // 2], e % 2) for e in range(8)]
        sneg_sb = [chunk(xb[e // 2], e % 2) for e in range(8)]
        s_sb = [chunk(xn[4 + e // 2], e % 2) for e in range(8)]
        wrT_b = [chunk(xb[4 + e // 2], e % 2) for e in range(8)]
        wiT_b = [chunk(q_all, 0), chunk(q_all, 1),
                 chunk(k_all, 0), chunk(k_all, 1),
                 chunk(v_tm, 0), chunk(v_tm, 1),
                 chunk(at_all, 0), chunk(at_all, 1)]
        br_ch = [chunk(wq_all, j) for j in range(4)] + \
                [chunk(wk_all, j) for j in range(4)]
        bi_ch = [chunk(wv_all, j) for j in range(4)] + \
                [chunk(wo_all, j) for j in range(4)]

        bank_w = [None] * 8
        bank_r = [None] * 8

        def mm_group(bank, mms, waits=()):
            ws0 = [w for w in waits if w is not None]
            if bank_r[bank] is not None:
                ws0.append(bank_r[bank])
            ev = None
            n = len(mms)
            for i, mm in enumerate(mms):
                w = list(ws0) if i == 0 else []
                if mm[0] == "T":
                    _, src, idt = mm
                    ev = b.emit("tensor",
                                lambda e, src=src, idt=idt, bank=bank:
                                e.transpose(
                                    ps[bank][:, :].bitcast(BF16)[:, 0:128],
                                    src, idt),
                                waits=tuple(w))
                    continue
                lh, rh, o = mm[0], mm[1], mm[2]
                if len(mm) > 3 and mm[3] is not None:
                    w.append(mm[3])
                ev = b.emit("tensor",
                            lambda e, lh=lh, rh=rh, o=o, s0=(i == 0),
                            s1=(i == n - 1):
                            e.matmul(o, lh, rh, start=s0, stop=s1),
                            waits=tuple(w))
            bank_w[bank] = ev
            return ev

        def reader(eng, fn, bank, waits=()):
            ev = b.emit(eng, fn, waits=tuple(waits) + (bank_w[bank],))
            bank_r[bank] = ev
            return ev

        ones_ev = b.emit("vector", lambda e: e.memset(ones[:, :], 1.0))
        b.emit("vector", lambda e: e.memset(czc[:, 0:1], 0.0))
        cz_ev = b.emit("vector", lambda e: e.memset(czc[:, 1:2], EPS))
        nc.const_aps.register(F32, 0.0, czc[:, 0:1])
        nc.const_aps.register(F32, EPS, czc[:, 1:2])

        gl_last = [None] * 4
        c_ld, s_ld, sn_ld = [None] * 8, [None] * 8, [None] * 8
        for e in range(8):
            ln = e % 4
            c_ld[e] = b.dma("gpsimd", c_sb[e], c_d[e * 128:(e + 1) * 128, :],
                            waits=(gl_last[ln],), lane=f"gc{ln}")
            gl_last[ln] = c_ld[e]
        for e in range(8):
            ln = e % 4
            sn_ld[e] = b.dma("gpsimd", sneg_sb[e],
                             sn_d[e * 128:(e + 1) * 128, :],
                             waits=(gl_last[ln],), lane=f"gc{ln}")
            gl_last[ln] = sn_ld[e]
        for e in range(8):
            ln = e % 4
            s_ld[e] = b.dma("gpsimd", s_sb[e], s_d[e * 128:(e + 1) * 128, :],
                            waits=(gl_last[ln],), lane=f"gc{ln}")
            gl_last[ln] = s_ld[e]
        mbt_ld = [None] * 4
        for h in range(4):
            mbt_ld[h] = b.dma("gpsimd", mbt_sb[h][:, :], mbt_d[h],
                              waits=(gl_last[h % 4],), lane=f"gc{h % 4}")
            gl_last[h % 4] = mbt_ld[h]
        small_ready = b.dma("gpsimd", identb[:, :], id_d[:, :],
                            waits=(gl_last[0],), lane="gc0")
        gl_last[0] = small_ready
        b.dma("gpsimd", fb_sb[:, :], fb_d[:, :], waits=(gl_last[1],),
              lane="gc1")
        gl_last[1] = b.ops[-1]
        b.dma("gpsimd", bo_sb[:, :], bo_d[:, :], waits=(gl_last[2],),
              lane="gc2")
        gl_last[2] = b.ops[-1]
        sm_fb_bo = (small_ready, gl_last[1], gl_last[2])

        xfp_last = [None] * 4
        wrT_cast, wiT_cast = [None] * 8, [None] * 8
        for e in range(8):
            i = e % 4
            ld = b.dma("sync", xfp[i][:, :], wrt_d[e * 128:(e + 1) * 128, :],
                       waits=(xfp_last[i],), lane=f"xf{i}")
            wrT_cast[e] = b.emit("vector",
                                 lambda en, d=wrT_b[e], s_=xfp[i]:
                                 en.tensor_copy(d, s_[:, :]), waits=(ld,))
            xfp_last[i] = wrT_cast[e]
        for e in range(8):
            i = e % 4
            ld = b.dma("sync", xfp[i][:, :], wit_d[e * 128:(e + 1) * 128, :],
                       waits=(xfp_last[i],), lane=f"xf{i}")
            wiT_cast[e] = b.emit("scalar",
                                 lambda en, d=wiT_b[e], s_=xfp[i]:
                                 en.activation(d, s_[:, :], AF.Copy,
                                               bias=0.0, scale=1.0),
                                 waits=(ld, cz_ev))
            xfp_last[i] = wiT_cast[e]

        br_ev = [None] * 8
        bi_ev = [None] * 8
        ev_par = 0

        def m_wave(kind, half):
            nonlocal ev_par
            groups = [(rc, hf) for rc in range(half * 4, half * 4 + 4)
                      for hf in range(2)]
            for ec in range(8):
                for gi, (rc, hf) in enumerate(groups):
                    if kind == "A":
                        lh, rh = wrT_b[ec][:, rc * 128:(rc + 1) * 128], \
                            c_sb[ec][:, hf * 512:(hf + 1) * 512]
                        w = (wrT_cast[ec], c_ld[ec])
                    elif kind == "A2":
                        lh, rh = wiT_b[ec][:, rc * 128:(rc + 1) * 128], \
                            sneg_sb[ec][:, hf * 512:(hf + 1) * 512]
                        w = (wiT_cast[ec], sn_ld[ec])
                    elif kind == "B":
                        lh, rh = wiT_b[ec][:, rc * 128:(rc + 1) * 128], \
                            c_sb[ec][:, hf * 512:(hf + 1) * 512]
                        w = (wiT_cast[ec], c_ld[ec])
                    elif kind == "B2":
                        lh, rh = wrT_b[ec][:, rc * 128:(rc + 1) * 128], \
                            s_sb[ec][:, hf * 512:(hf + 1) * 512]
                        w = (wrT_cast[ec], s_ld[ec])
                    elif kind == "C":
                        lh, rh = c_sb[ec][:, rc * 128:(rc + 1) * 128], \
                            br_ch[ec][:, hf * 512:(hf + 1) * 512]
                        w = (c_ld[ec], br_ev[ec])
                    else:
                        lh, rh = s_sb[ec][:, rc * 128:(rc + 1) * 128], \
                            bi_ch[ec][:, hf * 512:(hf + 1) * 512]
                        w = (s_ld[ec], bi_ev[ec])
                    bank = gi
                    first = (ec == 0 and kind in ("A", "B", "C"))
                    ws = list(w) if gi == 0 else []
                    if first and bank_r[bank] is not None:
                        ws.append(bank_r[bank])
                    ev = b.emit("tensor",
                                lambda e, lh=lh, rh=rh, bank=bank,
                                s0=first,
                                s1=(ec == 7 and kind in ("A2", "B2", "C2")):
                                e.matmul(ps[bank][:, :], lh, rh,
                                         start=s0, stop=s1),
                                waits=tuple(ws))
                    bank_w[bank] = ev
            if kind in ("A", "B", "C"):
                return
            for gi, (rc, hf) in enumerate(groups):
                if kind == "A2":
                    dst = br_ch[rc][:, hf * 512:(hf + 1) * 512]
                elif kind == "B2":
                    dst = bi_ch[rc][:, hf * 512:(hf + 1) * 512]
                else:
                    dst = m_sb[rc][:, hf * 512:(hf + 1) * 512]
                eng = "vector" if ev_par % 2 == 0 else "scalar"
                ev_par += 1
                if eng == "vector":
                    ev = reader(eng, lambda e, d=dst, bank=gi:
                                e.tensor_copy(d, ps[bank][:, :]), gi)
                else:
                    ev = reader(eng, lambda e, d=dst, bank=gi:
                                e.activation(d, ps[bank][:, :], AF.Copy,
                                             bias=0.0, scale=1.0), gi)
                if kind == "A2":
                    br_ev[rc] = ev
                elif kind == "B2":
                    bi_ev[rc] = ev

        for half in range(2):
            m_wave("A", half)
            m_wave("A2", half)
        for half in range(2):
            m_wave("B", half)
            m_wave("B2", half)
        b2_done = next(op for op in reversed(b.ops) if op.eng == "tensor")
        for half in range(2):
            m_wave("C", half)
            m_wave("C2", half)
        m_mm_done = next(op for op in reversed(b.ops) if op.eng == "tensor")
        m_ready_v = next(op for op in reversed(b.ops) if op.eng == "vector")
        m_ready_s = next(op for op in reversed(b.ops) if op.eng == "scalar")

        pool = [xb, xn]
        pool_guard = [[[b2_done] for _ in range(8)],
                      [[m_mm_done] for _ in range(8)]]
        pend_cast = {}

        def emit_prefetch(nbi, dc):
            dst = pool[nbi % 2][dc]
            evs = []
            for hf in range(2):
                i = (dc * 2 + hf) % 4
                ld = b.dma("sync", xfp[i][:, :],
                           x_d[nbi, dc * 128:(dc + 1) * 128,
                               hf * 1024:(hf + 1) * 1024],
                           waits=(xfp_last[i],), lane=f"xf{i}")
                eng = "vector" if dc % 2 else "scalar"
                w = (ld,) + tuple(pool_guard[nbi % 2][dc])
                if eng == "vector":
                    ev = b.emit(eng, lambda e, dst=dst, hf=hf, s_=xfp[i]:
                                e.tensor_copy(
                                    dst[:, hf * 1024:(hf + 1) * 1024],
                                    s_[:, :]), waits=w)
                else:
                    ev = b.emit(eng, lambda e, dst=dst, hf=hf, s_=xfp[i]:
                                e.activation(
                                    dst[:, hf * 1024:(hf + 1) * 1024],
                                    s_[:, :], AF.Copy, bias=0.0,
                                    scale=1.0), waits=w + (cz_ev,))
                xfp_last[i] = ev
                evs.append(ev)
            pend_cast[dc] = evs[-1]

        for dc in range(8):
            emit_prefetch(0, dc)

        w_cast = {}
        for nmi, (src, dst, scale) in enumerate(
                ((wq_d, wq_all, NORM), (wk_d, wk_all, 1.0),
                 (wv_d, wv_all, 1.0), (wo_d, wo_all, 1.0))):
            for j in range(4):
                i = (nmi * 4 + j) % 4
                ld = b.dma("sync", xfp[i][:, :], src[:, j * 1024:(j + 1) * 1024],
                           waits=(xfp_last[i],), lane=f"xf{i}")
                if scale == 1.0:
                    cast = b.emit("vector",
                                  lambda e, d=chunk(dst, j), s_=xfp[i]:
                                  e.tensor_copy(d, s_[:, :]),
                                  waits=(ld, m_mm_done))
                else:
                    cast = b.emit("scalar",
                                  lambda e, d=chunk(dst, j), s_=xfp[i],
                                  sc=scale:
                                  e.activation(d, s_[:, :], AF.Copy,
                                               bias=0.0, scale=sc),
                                  waits=(ld, m_mm_done))
                xfp_last[i] = cast
                w_cast[(nmi, j)] = cast
        wq_ready = lambda h: w_cast[(0, h)]
        wk_ready = lambda h: w_cast[(1, h)]
        wv_ready = lambda h: w_cast[(2, h)]
        wo_ready = w_cast[(3, 3)]

        lnstat_guard = [m_mm_done]
        xg_guard = [[None] for _ in range(4)]
        out_guard = [None] * 4
        ysl_guard = [None] * 12
        sq_mm = [None] * 4
        last_stores = [None]

        def emit_stats(bi):
            xnb = pool[bi % 2]
            cast_ev = [None] * 8
            for dc in range(8):
                cast_ev[dc] = pend_cast.pop(dc)
            sx_last = None
            invs_done = None
            prev_rsq = None
            for tt in range(4):
                sl = slice(tt * 512, (tt + 1) * 512)
                mms = [(ones[:, :], xnb[dc][:, sl], ps[6][:, :],
                        cast_ev[dc] if tt == 0 else None) for dc in range(8)]
                sx_last = mm_group(6, mms, waits=(ones_ev,))
                ev = None
                for dc in range(8):
                    j = dc % 4
                    sq = b.emit("vector",
                                lambda e, j=j, dc=dc, sl=sl, xnb=xnb:
                                e.tensor_mul(xsq_t[j][:, :], xnb[dc][:, sl],
                                             xnb[dc][:, sl]),
                                waits=(cast_ev[dc], sq_mm[j]))
                    w = [bank_r[7], ones_ev] if dc == 0 else []
                    w.append(sq)
                    ev = b.emit("tensor",
                                lambda e, j=j, s0=(dc == 0), s1=(dc == 7):
                                e.matmul(ps[7][:, :], ones[:, :],
                                         xsq_t[j][:, :], start=s0, stop=s1),
                                waits=tuple(x for x in w if x is not None))
                    sq_mm[j] = ev
                bank_w[7] = ev
                mu_ev = reader("vector", lambda e: e.tensor_scalar_mul(
                    st_a[:, :], ps[6][:, :], 1.0 / D), 6,
                    waits=(lnstat_guard[0] if tt == 0 else None,))
                b.emit("vector", lambda e, sl=sl: e.tensor_copy(
                    mu_bf[:, sl], st_a[:, :]))
                musq = b.emit("vector", lambda e: e.tensor_mul(
                    st_b[:, :], st_a[:, :], st_a[:, :]))
                var = reader("vector", lambda e: e.scalar_tensor_tensor(
                    st_c[:, :], ps[7][:, :], 1.0 / D, st_b[:, :],
                    OP.mult, OP.subtract), 7, waits=(musq,))
                sstd = b.emit("scalar", lambda e: e.activation(
                    st_b[:, :], st_c[:, :], AF.Sqrt, bias=czc[:, 1:2],
                    scale=1.0), waits=(var, prev_rsq))
                prev_rsq = sstd
                b.emit("vector", lambda e: e.reciprocal(
                    st_a[:, :], st_b[:, :]), waits=(sstd,))
                invs_done = b.emit("vector", lambda e, sl=sl: e.tensor_copy(
                    invs_bf[:, sl], st_a[:, :]))
            return invs_done, sx_last, cast_ev

        def emit_apply(bi, stats_done, sx_last, cast_ev):
            xnb = pool[bi % 2]
            stats_pe = (sx_last,) + tuple(sq_mm)
            apply_tt = [None] * 4
            for tt in range(4):
                sl = slice(tt * 512, (tt + 1) * 512)
                ev = None
                for dc in range(8):
                    w = ((stats_done, cast_ev[dc]) + stats_pe) if tt == 0 \
                        else ()
                    b.emit("vector", lambda e, dc=dc, sl=sl, xnb=xnb:
                           e.tensor_sub(xnb[dc][:, sl], xnb[dc][:, sl],
                                        mu_bf[:, sl]), waits=w)
                    ev = b.emit("vector", lambda e, dc=dc, sl=sl, xnb=xnb:
                                e.tensor_mul(xnb[dc][:, sl], xnb[dc][:, sl],
                                             invs_bf[:, sl]))
                apply_tt[tt] = ev
            lnstat_guard[0] = apply_tt[3]
            return apply_tt

        pcount = [0]

        def emit_proj(h, xnb, apply_tt, vpm_guard):
            L, dil = LS[h], DILS[h]
            qk_all = None
            v_all = None
            ngrp = (L + 3) // 4
            for (wt, wrdy, dst, eng) in (
                    (wq_all, wq_ready(h), q_all, "scalar"),
                    (wk_all, wk_ready(h), k_all, "scalar"),
                    (wv_all, wv_ready(h), v_pm, "vector")):
                for g in range(ngrp):
                    nb = min(4, L - g * 4)
                    bank = 4 + pcount[0] % 2
                    pcount[0] += 1
                    mms = []
                    for dc in range(8):
                        rhs = (xnb[dc][:, :]
                               .rearrange("p (n bb) -> p n bb", bb=dil * 128)
                               [:, g * 4:g * 4 + nb, 0:128])
                        lh = wt[:, (h * 8 + dc) * 128:(h * 8 + dc + 1) * 128]
                        mms.append((lh, rhs, ps[bank][:, 0:nb * 128]))
                    tt_need = ((g * 4 + nb - 1) * dil * 128 + 127) // 512
                    mm_group(bank, mms, waits=(apply_tt[tt_need], wrdy))
                    src = (ps[bank][:, 0:nb * 128]
                           .rearrange("p (n s) -> p n s", s=128))
                    base = dst[:, 0:128 * L] if dst is v_pm else \
                        dst[:, HOFF[h]:HOFF[h] + 128 * L]
                    dstv = (base.rearrange("p (s n) -> p s n", n=L)
                            [:, :, g * 4:g * 4 + nb].transpose([0, 2, 1]))
                    evw = (vpm_guard,) if dst is v_pm else ()
                    ev = reader(eng,
                                (lambda e, dstv=dstv, src=src:
                                 e.copy(dstv, src)) if eng == "scalar"
                                else (lambda e, dstv=dstv, src=src:
                                      e.tensor_copy(dstv, src)), bank,
                                waits=evw)
                    if dst is v_pm:
                        v_all = ev
                    else:
                        qk_all = ev
            vt_last = None
            tr_last = None
            for sbk in range(L):
                cols = slice(sbk * 128, (sbk + 1) * 128)
                bank = 4 + pcount[0] % 2
                pcount[0] += 1
                tr_last = mm_group(bank, [("T", v_pm[:, cols], identb[:, :])],
                                   waits=(v_all, small_ready))
                vt_last = reader(
                    "vector",
                    lambda e, sbk=sbk, h=h, bank=bank:
                    e.tensor_copy(v_tm[:, HOFF[h] + sbk * 128:
                                       HOFF[h] + (sbk + 1) * 128],
                                  ps[bank][:, :].bitcast(BF16)[:, 0:128]),
                    bank)
            return qk_all, vt_last, tr_last

        def emit_chain_scores(u, qk_evs):
            sc_done = [None] * 4
            for bk, pieces in enumerate(_unit_banks(u)):
                bank = 4 + bk
                for (h, sbk, col) in pieces:
                    qc = slice(HOFF[h] + sbk * 128, HOFF[h] + (sbk + 1) * 128)
                    mms = [(mbt_sb[h][:, :], identb[:, :],
                            ps[bank][:, col:col + 128]),
                           (k_all[:, qc], q_all[:, qc],
                            ps[bank][:, col:col + 128])]
                    sc_done[bk] = mm_group(
                        bank, mms,
                        waits=(qk_evs[h], small_ready, mbt_ld[h]))
            return sc_done

        def _unit_banks(u):
            out = [[] for _ in range(4)]
            bb = 0
            for (h, s0, nb) in UNITS[u]:
                for j in range(nb):
                    out[bb // 4].append((h, s0 + j, (bb % 4) * 128))
                    bb += 1
            return out

        def emit_chain_mid(u, sc_done, apply_last):
            banks = _unit_banks(u)
            ncols = [len(p) * 128 for p in banks]
            exp_evs = []
            for bk in range(4):
                bank = 4 + bk
                off = UOFF[u] + bk * 512
                ev = reader("scalar",
                            lambda e, bank=bank, off=off, nc_=ncols[bk]:
                            e.activation(e_all[:, off:off + nc_],
                                         ps[bank][:, 0:nc_], AF.Exp,
                                         bias=0.0, scale=1.0), bank,
                            waits=(apply_last,))
                exp_evs.append(ev)
            nblk = sum(len(p) for p in banks)
            r0 = 16 * u
            red = b.emit("vector",
                         lambda e, u=u, nblk=nblk, r0=r0:
                         e.reduce_sum(
                             rs_sb[:, r0:r0 + nblk]
                             .rearrange("p (n o) -> p n o", o=1),
                             e_all[:, UOFF[u]:UOFF[u] + nblk * 128]
                             .rearrange("p (n s) -> p n s", s=128),
                             AX.X), waits=tuple(exp_evs))
            rec = None
            for c0 in range(0, nblk, 8):
                cw = min(8, nblk - c0)
                rec = b.emit("vector",
                             lambda e, a=r0 + c0, w=cw:
                             e.reciprocal(rsi_sb[:, a:a + w],
                                          rs_sb[:, a:a + w]),
                             waits=(red,) if c0 == 0 else ())
            scale_evs = [[None] * len(p) for p in banks]
            bb = 0
            for bk in range(4):
                for j in range(len(banks[bk])):
                    off = UOFF[u] + bb * 128
                    scale_evs[bk][j] = b.emit(
                        "scalar",
                        lambda e, off=off, r=r0 + bb:
                        e.activation(e_all[:, off:off + 128],
                                     e_all[:, off:off + 128], AF.Copy,
                                     bias=0.0, scale=rsi_sb[:, r:r + 1]),
                        waits=(rec,))
                    bb += 1
            return scale_evs

        def emit_chain_av(u, scale_evs, vt_done):
            att_done = None
            banks = _unit_banks(u)
            for bk in range(4):
                bank = 4 + bk
                mms = []
                for j, (h, sbk, col) in enumerate(banks[bk]):
                    vc = slice(HOFF[h] + sbk * 128, HOFF[h] + (sbk + 1) * 128)
                    eoff = UOFF[u] + bk * 512 + col
                    mms.append((v_tm[:, vc], e_all[:, eoff:eoff + 128],
                                ps[bank][:, col:col + 128],
                                scale_evs[bk][j]))
                mm_group(bank, mms, waits=(vt_done,))
                nc_ = len(banks[bk]) * 128
                off = UOFF[u] + bk * 512
                att_done = reader("scalar",
                                  lambda e, off=off, nc_=nc_, bank=bank:
                                  e.copy(at_all[:, off:off + nc_],
                                         ps[bank][:, 0:nc_]), bank)
            return att_done

        def emit_y(bi, ui, xn_ready):
            xnb = pool[bi % 2]
            jt, oc = ui % 4, ui // 4
            bank = ui % 2
            slot = ui % 12
            mms = []
            for dc in range(8):
                for (pc, st0, stp, cnt) in JT_FFN[jt]:
                    mms.append((m_sb[dc][:, oc * 128:(oc + 1) * 128],
                                _sl(xnb[dc][:, :], st0, stp, cnt),
                                ps[bank][:, pc:pc + cnt]))
            gw = ()
            if ui == 0:
                gw = (xn_ready,) + ((m_ready_v, m_ready_s) if bi == 0 else ())
            ylast = mm_group(bank, mms, waits=gw)
            yev = reader("vector",
                         lambda e, slot=slot, oc=oc, bank=bank:
                         e.tensor_scalar_add(
                             y_sb[:, slot * 512:(slot + 1) * 512],
                             ps[bank][:, :], fb_sb[:, oc:oc + 1]), bank,
                         waits=(sm_fb_bo[1] if (bi == 0 and ui == 0) else None,
                                ysl_guard[slot]))
            return ylast, yev

        def emit_silu(gi, yev_last):
            third = gi % 3
            ev = b.emit("scalar",
                        lambda e, third=third:
                        e.activation(y_sb[:, third * 2048:(third + 1) * 2048],
                                     y_sb[:, third * 2048:(third + 1) * 2048],
                                     AF.Silu, bias=0.0, scale=1.0),
                        waits=(yev_last,))
            return ev

        def emit_s(bi, si, silu_ev, att_evs):
            jt, oc = si % 4, si // 4
            bank = 2 + si % 2 if si < 4 else (2, 3, 4, 5)[si % 4]
            slot = si % 12
            ob = si % 4
            xgb = si % 4
            xg_ld = b.dma("gpsimd", xg_sb[xgb][:, :],
                          xg_d[bi, oc * 128:(oc + 1) * 128,
                               jt * 512:(jt + 1) * 512],
                          waits=tuple(xg_guard[xgb]), lane=f"xg{xgb}")
            mms = []
            opw = [att_evs[0] if jt < 2 else att_evs[3]]
            if bi == 0 and si == 0:
                opw += [wo_ready, sm_fb_bo[2]]
            for (h, pc, st0, stp, cnt) in JT_OP[jt]:
                mms.append((wo_all[:, h * 1024 + oc * 128:
                                   h * 1024 + (oc + 1) * 128],
                            _sl(at_all[:, HOFF[h]:HOFF[h] + 128 * LS[h]],
                                st0, stp, cnt),
                            ps[bank][:, pc:pc + cnt]))
            mm_group(bank, mms, waits=tuple(opw))
            comb = reader("vector",
                          lambda e, slot=slot, oc=oc, bank=bank, ob=ob:
                          e.scalar_tensor_tensor(
                              out_sb[ob][:, :], ps[bank][:, :],
                              bo_sb[:, oc:oc + 1],
                              y_sb[:, slot * 512:(slot + 1) * 512],
                              OP.add, OP.mult), bank,
                          waits=(silu_ev, out_guard[ob]))
            ysl_guard[slot] = comb
            add = b.emit("vector",
                         lambda e, ob=ob, xgb=xgb:
                         e.tensor_add(out_sb[ob][:, :], out_sb[ob][:, :],
                                      xg_sb[xgb][:, :]),
                         waits=(xg_ld,))
            xg_guard[xgb] = [add]
            stv = b.dma("sync", out_d[bi, oc * 128:(oc + 1) * 128,
                                      jt * 512:(jt + 1) * 512],
                        out_sb[ob][:, :], waits=(add,),
                        lane=f"st{ob}")
            out_guard[ob] = stv
            last_stores[0] = stv
            return stv

        if "msb" in dump:
            mo = dbg_out("msb", (8, 128, 1024), BF16)
            for dc in range(8):
                b.dma("gpsimd", mo[dc], m_sb[dc][:, :],
                      waits=(m_ready_v, m_ready_s), lane="gt")
        nbatch = 1 if DEBUG_BI0_ONLY else BPC
        apply_state = None
        for bi in range(nbatch):
            if apply_state is None:
                sd, sx, ce = emit_stats(bi)
                apply_state = emit_apply(bi, sd, sx, ce)
            apply_tt = apply_state
            apply_state = None
            xn_ready = apply_tt[3]
            xnb = pool[bi % 2]
            if "xn" in dump and bi == 0:
                xo = dbg_out("xn", (8, 128, T), BF16)
                for dc in range(8):
                    b.dma("gpsimd", xo[dc], xnb[dc][:, :],
                          waits=(xn_ready,), lane="gt")
            qk_evs = [None] * 4
            vt_evs = [None] * 4
            vpm_guard = small_ready
            for h in range(4):
                qk_evs[h], vt_evs[h], vpm_guard = \
                    emit_proj(h, xnb, apply_tt, vpm_guard)
            if "qkv" in dump and bi == 0:
                qo = dbg_out("qk", (2, 128, 3840), BF16)
                b.dma("gpsimd", qo[0], q_all[:, :], waits=(qk_evs[3],),
                      lane="gt")
                b.dma("gpsimd", qo[1], k_all[:, :], waits=(qk_evs[3],),
                      lane="gt")
                vo = dbg_out("vtm", (128, 3840), BF16)
                b.dma("gpsimd", vo, v_tm[:, :], waits=(vt_evs[3],), lane="gt")
            att_evs = [None] * 4
            yi = si = 0
            yevs = {}
            silu_evs = {}
            have_att1 = [False]

            def run_y(n):
                nonlocal yi
                for _ in range(n):
                    if yi >= 32 or yi - si >= 12:
                        return
                    _, yev = emit_y(bi, yi, xn_ready)
                    yevs[yi] = yev
                    yi += 1
                    if yi % 4 == 0:
                        gi = yi // 4 - 1
                        silu_evs[gi] = emit_silu(gi, yevs[4 * gi + 3])

            def run_s(n):
                nonlocal si
                for _ in range(n):
                    if si >= 32 or (si // 4 + 1) * 4 > yi:
                        return
                    if si % 4 >= 2 and not have_att1[0]:
                        return
                    emit_s(bi, si, silu_evs[si // 4], att_evs)
                    si += 1

            sc = emit_chain_scores(0, qk_evs)
            run_y(4)
            scale0 = emit_chain_mid(0, sc, xn_ready)
            run_y(2)
            att_evs[0] = emit_chain_av(0, scale0, vt_evs[0])
            run_s(2)
            run_y(3)
            sc = emit_chain_scores(1, qk_evs)
            run_y(1)
            scale1 = emit_chain_mid(1, sc, xn_ready)
            run_y(1)
            att1 = emit_chain_av(1, scale1, vt_evs[3])
            att_evs[1] = att_evs[2] = att_evs[3] = att1
            have_att1[0] = True
            e_free = att1
            if bi + 1 < BPC:
                for dc in range(8):
                    emit_prefetch(bi + 1, dc)
            if "chain" in dump and bi == 0:
                eo = dbg_out("eall", (128, 3840), BF16)
                tp = b.dma("gpsimd", eo, e_all, waits=(att1,), lane="gt")
                ro = dbg_out("rs", (2, 128, 32))
                b.dma("gpsimd", ro[0], rs_sb[:, :], waits=(att1,), lane="gt")
                b.dma("gpsimd", ro[1], rsi_sb[:, :], waits=(att1,), lane="gt")
                ao = dbg_out("atall", (128, 3840), BF16)
                b.dma("gpsimd", ao, at_all[:, :], waits=(att1,), lane="gt")
                e_free = tp
            lnstat_guard[0] = e_free
            while yi < 32 or si < 32:
                prog = yi + si
                run_y(2)
                run_s(2)
                if si >= 8 and apply_state is None and bi + 1 < nbatch:
                    sd, sx, ce = emit_stats(bi + 1)
                    apply_state = emit_apply(bi + 1, sd, sx, ce)
                assert yi + si > prog, (yi, si)
            if apply_state is None and bi + 1 < nbatch:
                sd, sx, ce = emit_stats(bi + 1)
                apply_state = emit_apply(bi + 1, sd, sx, ce)
            pool_guard[bi % 2] = [[yevs[31]] for _ in range(8)]

        b.emit("gpsimd", lambda e: e.engine_nop(),
               waits=tuple(s for s in last_stores if s is not None))

        for op in b.ops:
            if op.is_dma:
                op.needed = True
            for w in op.waits:
                w.needed = True
        ctr = {}
        for op in b.ops:
            if not op.needed:
                continue
            key = ("dma_" + op.lane) if op.is_dma else op.eng
            op.sem = key
            ctr[key] = ctr.get(key, 0) + (16 if op.is_dma else 1)
            op.val = ctr[key]
        sems = {key: stack.enter_context(nc.semaphore("sem_" + key))
                for key in sorted(ctr)}
        with nc.Block() as block:
            def mk(name):
                def run(eng_obj):
                    for op in b.ops:
                        if op.eng != name:
                            continue
                        for w in op.waits:
                            if (not w.is_dma) and w.eng == name:
                                continue
                            eng_obj.wait_ge(sems[w.sem], w.val)
                        inst = op.fn(eng_obj)
                        if op.needed:
                            inst.then_inc(sems[op.sem],
                                          16 if op.is_dma else 1)
                return run
            block.sync(mk("sync"))
            block.tensor(mk("tensor"))
            block.vector(mk("vector"))
            block.scalar(mk("scalar"))
            block.gpsimd(mk("gpsimd"))
    return nc


def _stage_inputs(L0_attn_x, L0_attn_Wk, L0_attn_Wq, L0_attn_Wv,
                  L0_attn_W_out, L0_attn_b_out, L0_ffn_w_freq_real,
                  L0_ffn_w_freq_imag, L0_ffn_bias):
    perm = _gperm()
    f32 = np.float32
    idx = np.arange(D)
    ang = (2.0 * np.pi / D) * ((idx[:, None] * idx[None, :]) % D)
    Cm = (np.cos(ang) / 32.0).astype(ml_dtypes.bfloat16)
    Sm = (np.sin(ang) / 32.0).astype(ml_dtypes.bfloat16)
    Snm = (-np.sin(ang) / 32.0).astype(ml_dtypes.bfloat16)
    mbt = np.zeros((4, 128, 128), np.float32)
    for h in range(4):
        L = LS[h]
        p = np.arange(128)
        sl_p, n = p // L, p % L
        m = p % L
        allowed = (sl_p[:, None] == sl_p[None, :]) & \
                  (m[None, :] <= n[:, None])
        mb = np.where(allowed, 0.0, NEG).astype(f32)
        mbt[h] = mb.T
    mbt = mbt.astype(ml_dtypes.bfloat16)

    def pack_qkv(W, scalefold=False):
        big = np.empty((128, 4096), f32)
        for h in range(4):
            wt = np.asarray(W[h], f32).T
            for dc in range(8):
                big[:, (h * 8 + dc) * 128:(h * 8 + dc + 1) * 128] = \
                    wt[dc * 128:(dc + 1) * 128, :]
        return np.ascontiguousarray(big)

    wq = pack_qkv(L0_attn_Wq)
    wk = pack_qkv(L0_attn_Wk)
    wv = pack_qkv(L0_attn_Wv)
    wo_t = np.asarray(L0_attn_W_out, f32).T
    wo = np.empty((128, 4096), f32)
    for h in range(4):
        wo[:, h * 1024:(h + 1) * 1024] = wo_t[h * 128:(h + 1) * 128, :]
    wrt = np.ascontiguousarray(np.asarray(L0_ffn_w_freq_real, f32).T)
    wit = np.ascontiguousarray(np.asarray(L0_ffn_w_freq_imag, f32).T)
    fbT = np.ascontiguousarray(np.asarray(L0_ffn_bias, f32).reshape(8, 128).T)
    boT = np.ascontiguousarray(
        np.asarray(L0_attn_b_out, f32).reshape(8, 128).T)
    shared = dict(wq_big=wq, wk_big=wk, wv_big=wv, wo_big=np.ascontiguousarray(wo),
                  bo_t=boT, fb_t=fbT, wrt=wrt, wit=wit,
                  cdft=Cm, sdft=Sm, sndft=Snm, mbt=mbt,
                  identb=np.eye(128, dtype=ml_dtypes.bfloat16))
    xs = np.asarray(L0_attn_x, f32)
    in_maps = []
    for c in range(NCORES):
        xc = xs[BPC * c:BPC * (c + 1)]
        m = dict(shared)
        m["x"] = np.ascontiguousarray(xc.transpose(0, 2, 1))
        m["xg"] = np.ascontiguousarray(xc[:, perm, :].transpose(0, 2, 1))
        in_maps.append(m)
    return in_maps, perm


def kernel(L0_attn_x, L0_attn_Wk, L0_attn_Wq, L0_attn_Wv, L0_attn_W_out,
           L0_attn_b_out, L0_ffn_w_freq_real, L0_ffn_w_freq_imag,
           L0_ffn_bias):
    in_maps, perm = _stage_inputs(
        L0_attn_x, L0_attn_Wk, L0_attn_Wq, L0_attn_Wv, L0_attn_W_out,
        L0_attn_b_out, L0_ffn_w_freq_real, L0_ffn_w_freq_imag, L0_ffn_bias)
    nc = build(dump=tuple(DEBUG_TAPS))
    res = run_bass_kernel_spmd(nc, in_maps, core_ids=list(range(NCORES)))
    out = np.empty((B_FULL, T, D), np.float32)
    for c in range(NCORES):
        o = np.asarray(res.results[c]["out"])
        out[BPC * c:BPC * (c + 1)][:, perm, :] = o.transpose(0, 2, 1)
    if DEBUG_TAPS:
        kernel.debug = [{k: np.asarray(v) for k, v in r.items()
                         if k.startswith("dbg_")} for r in res.results]
    return out
